# version 22
# baseline (speedup 1.0000x reference)
import math
import numpy as np
import ml_dtypes
import concourse.bass as bass
import concourse.mybir as mybir
from concourse.bass_utils import run_bass_kernel_spmd
from contextlib import ExitStack

F32 = mybir.dt.float32
BF16 = mybir.dt.bfloat16
AF = mybir.ActivationFunctionType
ALU = mybir.AluOpType

B, S, D = 2, 8192, 2048
H, DH, DR = 16, 128, 64
DCKV, DCQ = 512, 1536
E, R = 8, 64
NCORE = 8
TPB = S // NCORE
SCALE = 1.0 / math.sqrt(DH + DR)
EXPB = 12.0
EPS = 1e-5
MOE_SCALING = 4.0

PERM64 = list(range(0, 16)) + list(range(32, 48)) + list(range(16, 32)) + list(range(48, 64))
ROPE_ORDER = [h * 64 + j for h in range(H) for j in PERM64]

ENGS = ("tensor", "vector", "scalar", "sync", "gpsimd")
LN_CUT = None


class Sem:
    def __init__(self, nc, es, name):
        self.h = es.enter_context(nc.semaphore(name))
        self.v = 0

    def bump(self, n=1):
        self.v += n
        return (self.h, self.v, n)

    def cur(self):
        return (self.h, self.v)


class Arena:
    def __init__(self, t, cols):
        self.t, self.cols, self.off, self.stack = t, cols, 0, []

    def alloc(self, n, parts=128):
        assert self.off + n <= self.cols, ("arena overflow", self.off, n, self.cols)
        sl = self.t[0:parts, self.off:self.off + n]
        self.off += n
        return sl

    def mark(self):
        self.stack.append(self.off)

    def reset(self):
        self.off = self.stack.pop()


def build_nc(dbg=False, phases=("p1", "p3", "p4"), p1_subs=None, p1_passes=2, do_cc=True, drains=False):
    nc = bass.Bass("TRN2", target_bir_lowering=False, debug=False)

    def din(name, shape, dt=F32):
        return nc.dram_tensor(name, shape, dt, kind="ExternalInput").ap()

    x_sl = din("x_sl", [2 * TPB, D])
    wkdT = din("wkdT", [D, DCKV])
    wqdT = din("wqdT", [D, DCQ])
    wkuT = din("wkuT", [DCKV, H * DH])
    wvuT = din("wvuT", [DCKV, H * DH])
    wquT = din("wquT", [DCQ, H * DH])
    wkrT = din("wkrT", [D, H * DR])
    wqrT = din("wqrT", [DCQ, H * DR])
    woTr = din("woTr", [H * DH, D])
    aTh = din("aTh", [128, 16 * E * R])
    bwTh = din("bwTh", [64, E * D])
    wgTh = din("wgTh", [128, 16 * E])
    bkd2 = din("bkd2", [128, 4])
    bqd2 = din("bqd2", [128, 12])
    bku2 = din("bku2", [128, 16])
    bvu2 = din("bvu2", [128, 16])
    bqu2 = din("bqu2", [128, 16])
    bkr2 = din("bkr2", [128, 8])
    bqr2 = din("bqr2", [128, 8])
    bo2 = din("bo2", [1, D])
    cost = din("cost", [128, TPB])
    sintab = din("sint", [128, TPB])
    masktab = din("maskt", [128, 8 * 128], BF16)
    identt = din("identt", [128, 128], BF16)
    onest = din("onest", [128, 128], BF16)
    onehot = din("onehot", [8, E * 64], BF16)

    out_sl = nc.dram_tensor("out_sl", [2 * TPB, D], F32, kind="ExternalOutput").ap()

    kscr = "ExternalOutput" if dbg else "Internal"
    def dscr(name, shape):
        if kscr == "Internal":
            return nc.dram_tensor(name, shape, BF16)
        return nc.dram_tensor(name, shape, BF16, kind=kscr)

    qscr_t = dscr("qscr", [3 * 1024, 2 * TPB])
    shard_t = [nc.dram_tensor("shard%d" % b, [5 * 1024, TPB], BF16) for b in range(2)]
    dbgsh_t = None
    if dbg:
        dbgsh_t = [nc.dram_tensor("dbgsh%d" % b, [5 * 1024, TPB], BF16, kind="ExternalOutput")
                   for b in range(2)]
    gath_t = [nc.dram_tensor("gath%d" % b, [NCORE * 5 * 1024, TPB], BF16) for b in range(2)]
    onorm_t = dscr("onormscr", [H * DH, 2 * TPB])
    wscr_t = nc.dram_tensor("wscr", [128, 110592], BF16)
    wscr = wscr_t.ap()
    qscr = qscr_t.ap()
    shard = [t.ap() for t in shard_t]
    gath = [t.ap() for t in gath_t]
    onorm = onorm_t.ap()

    es = ExitStack()
    AB = Arena(es.enter_context(nc.sbuf_tensor("ab16", [128, 73728], BF16)), 73728)
    AFA = Arena(es.enter_context(nc.sbuf_tensor("af32", [128, 8704], F32)), 8704)
    PS = es.enter_context(nc.psum_tensor("ps", [128, 4096], F32))

    sb_id = es.enter_context(nc.sbuf_tensor("sb_id", [128, 128], BF16))
    sb_ones = es.enter_context(nc.sbuf_tensor("sb_ones", [128, 128], BF16))
    sb_mask = es.enter_context(nc.sbuf_tensor("sb_mask", [128, 1024], BF16))
    sb_cos = es.enter_context(nc.sbuf_tensor("sb_cos", [128, TPB], F32))
    sb_sin = es.enter_context(nc.sbuf_tensor("sb_sin", [128, TPB], F32))
    sb_bias = es.enter_context(nc.sbuf_tensor("sb_bias", [128, 80], F32))
    sb_bo = es.enter_context(nc.sbuf_tensor("sb_bo", [1, D], F32))
    sb_bo16 = es.enter_context(nc.sbuf_tensor("sb_bo16", [1, D], BF16))
    sb_oh = es.enter_context(nc.sbuf_tensor("sb_oh", [8, E * 64], BF16))
    sb_sm = es.enter_context(nc.sbuf_tensor("sb_sm", [128, 64], F32))

    BOFF = {}
    _o = 0
    for nm, wdt in [("bkd", 4), ("bqd", 12), ("bku", 16), ("bvu", 16), ("bqu", 16),
                    ("bkr", 8), ("bqr", 8)]:
        BOFF[nm] = _o
        _o += wdt

    sems = {}
    def mksem(name):
        s = Sem(nc, es, name)
        sems[name] = s
        return s
    s_cst = mksem("cst")
    s_x = [mksem("x0"), mksem("x1"), mksem("x2")]
    s_w = [mksem("w0"), mksem("w1")]
    s_kv = [mksem("kv0"), mksem("kv1")]
    s_kvg = [mksem("kvg0"), mksem("kvg1")]
    s_kvs = [mksem("kvs0"), mksem("kvs1")]
    s_wst = mksem("wst")
    s_w2 = [mksem("w20"), mksem("w21")]
    s_dbg = mksem("dbg")
    s_q = [mksem("q0"), mksem("q1")]
    s_on = [mksem("on0"), mksem("on1")]
    s_stg = mksem("stg")
    s_sh = [mksem("sh0"), mksem("sh1")]
    s_onst = mksem("onst")
    s_out = [mksem("outst0"), mksem("outst1")]
    s_cc = [mksem("cc0"), mksem("cc1")]
    s_t = mksem("t")
    s_v = mksem("v")
    s_s = mksem("s")

    ops = {e: [] for e in ENGS}

    def op(eng, meth, *args, waits=(), inc=None, nodrain=False, **kw):
        ops[eng].append((meth if not nodrain else "!" + meth, args, kw, tuple(waits), inc))

    def wt(eng, *waits):
        ops[eng].append(("_wait", (), {}, tuple(waits), None))

    def tinc(eng, bump):
        meth, args, kw, waits, inc = ops[eng][-1]
        assert inc is None and meth not in ("_wait",), (meth, inc)
        ops[eng][-1] = (meth, args, kw, waits, bump)

    for dst, src in [
        (sb_id[:, :], identt), (sb_ones[:, :], onest), (sb_mask[:, :], masktab),
        (sb_cos[:, :], cost), (sb_sin[:, :], sintab), (sb_bo[:, :], bo2),
        (sb_oh[:, :], onehot),
        (sb_bias[:, BOFF["bkd"]:BOFF["bkd"] + 4], bkd2),
        (sb_bias[:, BOFF["bqd"]:BOFF["bqd"] + 12], bqd2),
        (sb_bias[:, BOFF["bku"]:BOFF["bku"] + 16], bku2),
        (sb_bias[:, BOFF["bvu"]:BOFF["bvu"] + 16], bvu2),
        (sb_bias[:, BOFF["bqu"]:BOFF["bqu"] + 16], bqu2),
        (sb_bias[:, BOFF["bkr"]:BOFF["bkr"] + 8], bkr2),
        (sb_bias[:, BOFF["bqr"]:BOFF["bqr"] + 8], bqr2),
    ]:
        op("sync", "dma_start", out=dst, in_=src, inc=s_cst.bump(16))
    CST = s_cst.cur()
    for e in ("tensor", "scalar", "gpsimd", "vector"):
        wt(e, CST)
    op("vector", "memset", sb_sm[:, 32:33], EPS)
    op("vector", "memset", sb_sm[:, 33:34], -EXPB)
    op("vector", "memset", sb_sm[:, 34:35], 0.0)
    op("vector", "tensor_copy", out=sb_bo16[:, :], in_=sb_bo[:, :], inc=s_v.bump())

    AB.mark(); AFA.mark()
    xln = [AB.alloc(2048) for _ in range(2)]
    xlnT = AB.alloc(16 * 1024)
    kvc = AB.alloc(4 * 1024)
    qcs = AB.alloc(12 * 1024)
    stg = [AB.alloc(1024) for _ in range(3)]
    vfm = [AB.alloc(1024) for _ in range(4)]
    krr = [AB.alloc(1024) for _ in range(2)]
    krs = AB.alloc(1024)
    rop = [AB.alloc(1024) for _ in range(2)]
    wbuf = [AB.alloc(12288) for _ in range(2)]
    xf = [AFA.alloc(2048) for _ in range(3)]
    rtmp = [AFA.alloc(1024) for _ in range(2)]
    bnst = sb_sm[:, 0:24]
    mv = sb_sm[:, 24:26]
    rstd = sb_sm[:, 26:27]
    nmr = sb_sm[:, 27:28]
    negr = sb_sm[:, 35:36]

    def xlnT_dc(dc):
        return xlnT[:, dc * 1024:(dc + 1) * 1024]

    st_ = dict(
        x_snap=[None, None, None],
        xln_free=[None, None],
        tp_evac=[None, None],
        proj_evac=[None, None],
        kvc_done=None, qcs_done=None, xlnT_done=None,
        wfree=[None, None],
        wtog=0, pjpar=0, stgi=0,
        vfm_pend=[], vfm_free=[None] * 4,
        vt_par=0, vt_evac=[None, None], wscr_off=0, wscr_map={}, wscr_stc={}, ln_s_snap=None, xlnT_half0=0,
        kr_free=[None, None],
        ropi=0, rop_store_sem=[None, None],
    )

    def emit_v_transposes(bt):
        chunks = list(st_["vfm_pend"])
        st_["vfm_pend"].clear()
        fbase = chunks[0][0]
        lastsnap = chunks[-1][1]
        dcol0 = fbase * 128
        half = dcol0 // 1024
        col0 = dcol0 % 1024
        for tc in range(8):
            par = st_["vt_par"] % 2
            st_["vt_par"] += 1
            pb = PS[:, 2048 + par * 512: 2048 + par * 512 + 512].bitcast(BF16)
            tw = [(s_s.h, lastsnap)]
            if st_["vt_evac"][par] is not None:
                tw.append((s_s.h, st_["vt_evac"][par][1]))
            if st_["tp_evac"][par] is not None:
                tw.append((s_s.h, st_["tp_evac"][par]))
            for j, (fcg, _) in enumerate(chunks):
                op("tensor", "transpose", pb[:, j * 128:(j + 1) * 128],
                   vfm[fcg % 4][:, tc * 128:(tc + 1) * 128], sb_id[:, :],
                   waits=(tw if j == 0 else ()))
            tinc("tensor", s_t.bump())
            tsn = s_t.v
            if tc == 7:
                for fcg, _ in chunks:
                    st_["vfm_free"][fcg % 4] = tsn
            ssl = st_["stgi"] % 3
            st_["stgi"] += 1
            sdst = stg[ssl][:, 0:512]
            ew = [(s_t.h, tsn), s_sh[bt].cur(), s_stg.cur()]
            op("scalar", "copy", out=sdst, in_=pb[:, 0:512], waits=ew, inc=s_s.bump())
            st_["vt_evac"][par] = ("s", s_s.v)
            evs = s_s.v
            dview = shard[bt][3072 + 256 * tc: 3072 + 256 * (tc + 1), :].rearrange(
                "(m two) c -> m two c", two=2)[:, half, col0:col0 + 512]
            op("sync", "dma_start", out=dview, in_=sdst, waits=[(s_s.h, evs)],
               inc=s_sh[bt].bump(16))

    def emit_rope(bt, sname, fcg, evsnap):
        sl = fcg % 2
        rsl = st_["ropi"] % 2
        st_["ropi"] += 1
        shmask = [(i + 16) % 32 for i in range(32)]
        op("vector", "stream_shuffle", krs[:, :], krr[sl][:, :], shmask,
           waits=[(s_s.h, evsnap)])
        op("vector", "tensor_tensor", out=rtmp[0], in0=krr[sl][:, :], in1=sb_cos[:, :], op=ALU.mult,
           nodrain=True)
        op("vector", "tensor_tensor", out=rtmp[1], in0=krs[:, :], in1=sb_sin[:, :], op=ALU.mult)
        rw = []
        prev = st_["rop_store_sem"][rsl]
        if prev is not None:
            rw.append(prev.cur())
        op("vector", "tensor_tensor", out=rop[rsl][:, :], in0=rtmp[0], in1=rtmp[1], op=ALU.add,
           waits=rw, inc=s_v.bump())
        st_["kr_free"][sl] = s_v.v
        rsnap = s_v.v
        if sname in ("krA", "krB"):
            dview = shard[bt][2048 + fcg * 128: 2048 + (fcg + 1) * 128, :]
            sem = s_sh[bt]
        else:
            dview = qscr[2048 + fcg * 128: 2048 + (fcg + 1) * 128, bt * TPB:(bt + 1) * TPB]
            sem = s_stg
        st_["rop_store_sem"][rsl] = sem
        op("sync", "dma_start", out=dview, in_=rop[rsl][:, :], waits=[(s_v.h, rsnap)],
           inc=sem.bump(16))

    SUBS = [
        ("kvc", wkdT, 16, 0, 512, "xlnT", "bkd"),
        ("qcA", wqdT, 16, 0, 768, "xlnT", "bqd"),
        ("qcB", wqdT, 16, 768, 768, "xlnT", "bqd"),
        ("kcnt", wkuT, 4, 0, 2048, "kvc", "bku"),
        ("v", wvuT, 4, 0, 2048, "kvc", "bvu"),
        ("qcntA", wquT, 12, 0, 1024, "qcs", "bqu"),
        ("qcntB", wquT, 12, 1024, 1024, "qcs", "bqu"),
        ("krA", wkrT, 16, 0, 512, "xlnT", "bkr"),
        ("krB", wkrT, 16, 512, 512, "xlnT", "bkr"),
        ("qr", wqrT, 12, 0, 1024, "qcs", "bqr"),
    ]

    def p0_precast():
        off = 0
        for k, (sname, wap, nd, fcol0, FW, _src, _b) in enumerate(SUBS):
            wcols = nd * FW
            st_["wscr_map"][sname] = off
            st_["wscr_stc"][sname] = 16 * (k + 1)
            wsl = k % 2
            ww = []
            if k >= 2:
                ww.append(s_wst.cur())
            ndh = max(1, nd // 2)
            op("gpsimd", "dma_start",
               out=wbuf[wsl][:, 0:ndh * FW].rearrange("p (g f) -> p g f", g=ndh),
               in_=wap.rearrange("(g p) f -> p g f", p=128)[:, 0:ndh, fcol0:fcol0 + FW],
               waits=ww, inc=s_w[wsl].bump(16))
            if nd > ndh:
                op("gpsimd", "dma_start",
                   out=wbuf[wsl][:, ndh * FW:nd * FW].rearrange("p (g f) -> p g f", g=nd - ndh),
                   in_=wap.rearrange("(g p) f -> p g f", p=128)[:, ndh:nd, fcol0:fcol0 + FW],
                   waits=ww, inc=s_w[wsl].bump(16))
            op("gpsimd", "dma_start", out=wscr[:, off:off + wcols],
               in_=wbuf[wsl][:, 0:wcols],
               waits=[s_w[wsl].cur(), s_wst.cur()], inc=s_wst.bump(16))
            off += wcols

    def p1_pass(bt):
        if bt == 1:
            bar = s_t.cur()
            wt("scalar", bar)
            wt("vector", bar)
        for tc in range(8):
            sl = tc % 3
            sl2 = tc % 2
            w = []
            if st_["x_snap"][sl] is not None:
                w.append((s_s.h, st_["x_snap"][sl][1]))
            op("sync", "dma_start", out=xf[sl],
               in_=x_sl[bt * TPB + tc * 128: bt * TPB + (tc + 1) * 128, :],
               waits=w, inc=s_x[sl].bump(16))
            if LN_CUT == "x":
                st_["x_snap"][sl] = None
                continue
            wt("vector", s_x[sl].cur())
            if st_["ln_s_snap"] is not None:
                wt("vector", (s_s.h, st_["ln_s_snap"]))
            for k in range(4):
                op("vector", "bn_stats", bnst[:, 6 * k:6 * k + 6], xf[sl][:, 512 * k:512 * (k + 1)],
                   nodrain=(k > 0))
            op("vector", "bn_aggr", mv[:, :], bnst[:, :], inc=s_v.bump())
            if LN_CUT == "bn":
                st_["x_snap"][sl] = s_v.v
                continue
            wt("scalar", (s_v.h, s_v.v))
            op("scalar", "activation", out=rstd[:, :], in_=mv[:, 1:2], func=AF.Ln, bias=sb_sm[:, 32:33], scale=1.0)
            op("scalar", "activation", out=rstd[:, :], in_=rstd[:, :], func=AF.Exp, bias=0.0,
               scale=-0.5)
            op("scalar", "activation", out=negr[:, :], in_=rstd[:, :], func=AF.Identity,
               bias=sb_sm[:, 34:35], scale=-1.0)
            op("scalar", "activation", out=nmr[:, :], in_=mv[:, 0:1], func=AF.Identity,
               bias=sb_sm[:, 34:35], scale=negr[:, 0:1])
            xw = []
            if st_["xln_free"][sl2] is not None:
                xw.append((s_t.h, st_["xln_free"][sl2]))
            op("scalar", "activation", out=xln[sl2][:, :], in_=xf[sl], func=AF.Identity,
               bias=nmr[:, 0:1], scale=rstd[:, 0:1], waits=xw, inc=s_s.bump())
            st_["x_snap"][sl] = ("s", s_s.v)
            st_["ln_s_snap"] = s_s.v
            xlnsnap = ("s", s_s.v)
            if LN_CUT == "xln":
                st_["xln_free"][sl2] = None
                continue
            for g4 in range(4):
                par = g4 % 2
                pb = PS[:, 2048 + par * 512: 2048 + par * 512 + 512].bitcast(BF16)
                tw = [(s_s.h, xlnsnap[1])]
                if st_["tp_evac"][par] is not None:
                    tw.append((s_s.h, st_["tp_evac"][par]))
                if st_["vt_evac"][par] is not None:
                    tw.append((s_s.h, st_["vt_evac"][par][1]))
                for j in range(4):
                    dc = g4 * 4 + j
                    op("tensor", "transpose", pb[:, j * 128:(j + 1) * 128],
                       xln[sl2][:, dc * 128:(dc + 1) * 128], sb_id[:, :],
                       waits=(tw if j == 0 else ()))
                tinc("tensor", s_t.bump())
                tsn = s_t.v
                if LN_CUT == "tpv":
                    op("vector", "tensor_copy", out=xlnT[:, g4 * 512:(g4 + 1) * 512],
                       in_=pb[:, 0:512], waits=[(s_t.h, tsn)], inc=s_v.bump())
                    st_["tp_evac"][par] = None
                    stv_snap = s_v.v
                    op("scalar", "_wait_only") if False else None
                    st_["vt_evac"][par] = stv_snap
                    continue
                if LN_CUT == "tps":
                    op("scalar", "copy", out=xlnT[:, g4 * 512:(g4 + 1) * 512],
                       in_=pb[:, 0:512], waits=[(s_t.h, tsn)], inc=s_s.bump())
                    st_["tp_evac"][par] = s_s.v
                    continue
                dview = xlnT.rearrange("p (c t) -> p c t", c=16)[:, g4 * 4:g4 * 4 + 4,
                                                                tc * 128:(tc + 1) * 128]
                op("scalar", "copy", out=dview,
                   in_=pb[:, 0:512].rearrange("p (a b) -> p a b", a=4),
                   waits=[(s_t.h, tsn)], inc=s_s.bump())
                st_["tp_evac"][par] = s_s.v
            st_["xln_free"][sl2] = s_t.v
            if tc == 3:
                st_["xlnT_half0"] = s_s.v
        st_["xlnT_done"] = s_s.v

        if p1_subs is not None and len(p1_subs) == 0:
            return
        subs = list(SUBS)
        if p1_subs is not None:
            subs = [s for s in subs if s[0] in p1_subs]
        for sname, wap, nd, fcol0, FW, src, bnm in subs:
            wsl = st_["wtog"] % 2
            st_["wtog"] += 1
            ww = []
            if st_["wfree"][wsl] is not None:
                ww.append((s_t.h, st_["wfree"][wsl]))
            wcols = nd * FW
            if bt == 0:
                woff = st_["wscr_off"]
                st_["wscr_map"][sname] = woff
                st_["wscr_off"] += wcols
                if st_["wfree"][wsl] is not None:
                    ww.append(s_wst.cur())
                ndh = max(1, nd // 2)
                op("gpsimd", "dma_start",
                   out=wbuf[wsl][:, 0:ndh * FW].rearrange("p (g f) -> p g f", g=ndh),
                   in_=wap.rearrange("(g p) f -> p g f", p=128)[:, 0:ndh, fcol0:fcol0 + FW],
                   waits=ww, inc=s_w[wsl].bump(16))
                if nd > ndh:
                    op("gpsimd", "dma_start",
                       out=wbuf[wsl][:, ndh * FW:nd * FW].rearrange("p (g f) -> p g f", g=nd - ndh),
                       in_=wap.rearrange("(g p) f -> p g f", p=128)[:, ndh:nd, fcol0:fcol0 + FW],
                       waits=ww, inc=s_w[wsl].bump(16))
                wsem = s_w[wsl].cur()
                op("scalar", "dma_start", out=wscr[:, woff:woff + wcols],
                   in_=wbuf[wsl][:, 0:wcols], waits=[wsem], inc=s_wst.bump(16))
            else:
                ww.append(s_wst.cur())
                woff = st_["wscr_map"][sname]
                op("sync", "dma_start", out=wbuf[wsl][:, 0:wcols],
                   in_=wscr[:, woff:woff + wcols],
                   waits=ww, inc=s_w2[wsl].bump(16))
                wsem = s_w2[wsl].cur()
            nf = FW // 128
            for fc in range(nf):
                fcg = (fcol0 // 128) + fc
                par = st_["pjpar"] % 2
                st_["pjpar"] += 1
                pdst = PS[:, par * 1024:(par + 1) * 1024]
                mw = [wsem]
                if st_["proj_evac"][par] is not None:
                    mw.append((s_s.h, st_["proj_evac"][par]))
                mwh = list(mw)
                if src == "xlnT":
                    mw.append((s_s.h, st_["xlnT_half0"]))
                    mwh.append((s_s.h, st_["xlnT_done"]))
                elif src == "kvc":
                    mw.append((s_s.h, st_["kvc_done"]))
                else:
                    mw.append((s_s.h, st_["qcs_done"]))
                for qh in range(2):
                    for dc in range(nd):
                        if src == "xlnT":
                            rhs_full = xlnT_dc(dc)
                        elif src == "kvc":
                            rhs_full = kvc[:, dc * 1024:(dc + 1) * 1024]
                        else:
                            rhs_full = qcs[:, dc * 1024:(dc + 1) * 1024]
                        lhs = wbuf[wsl][:, dc * FW + fc * 128: dc * FW + fc * 128 + 128]
                        w_ = mw if qh == 0 else mwh
                        op("tensor", "matmul", pdst[:, qh * 512:(qh + 1) * 512], lhs,
                           rhs_full[:, qh * 512:(qh + 1) * 512],
                           start=(dc == 0), stop=(dc == nd - 1),
                           waits=(w_ if dc == 0 else ()))
                tinc("tensor", s_t.bump())
                mmsnap = s_t.v
                bap = sb_bias[:, BOFF[bnm] + fcg: BOFF[bnm] + fcg + 1]
                ew = [(s_t.h, mmsnap)]
                ssl = None
                if sname == "kvc":
                    edst = kvc[:, fcg * 1024:(fcg + 1) * 1024]
                elif sname in ("qcA", "qcB"):
                    edst = qcs[:, fcg * 1024:(fcg + 1) * 1024]
                elif sname in ("krA", "krB", "qr"):
                    edst = krr[fcg % 2]
                    if st_["kr_free"][fcg % 2] is not None:
                        ew.append((s_v.h, st_["kr_free"][fcg % 2]))
                elif sname == "v":
                    edst = vfm[fcg % 4]
                    if st_["vfm_free"][fcg % 4] is not None:
                        ew.append((s_t.h, st_["vfm_free"][fcg % 4]))
                else:
                    ssl = st_["stgi"] % 3
                    st_["stgi"] += 1
                    edst = stg[ssl]
                    ew.append(s_sh[bt].cur())
                    ew.append(s_stg.cur())
                op("scalar", "activation", out=edst, in_=pdst, func=AF.Identity,
                   bias=bap, scale=1.0, waits=ew, inc=s_s.bump())
                st_["proj_evac"][par] = s_s.v
                evsnap = s_s.v
                if sname == "kvc" and fcg == 3:
                    st_["kvc_done"] = evsnap
                if sname == "qcB" and fcg == 11:
                    st_["qcs_done"] = evsnap
                if sname == "kcnt":
                    op("sync", "dma_start", out=shard[bt][fcg * 128:(fcg + 1) * 128, :],
                       in_=edst, waits=[(s_s.h, evsnap)], inc=s_sh[bt].bump(16))
                elif sname in ("qcntA", "qcntB"):
                    op("sync", "dma_start",
                       out=qscr[fcg * 128:(fcg + 1) * 128, bt * TPB:(bt + 1) * TPB],
                       in_=edst, waits=[(s_s.h, evsnap)], inc=s_stg.bump(16))
                elif sname == "v":
                    st_["vfm_pend"].append((fcg, evsnap))
                    if len(st_["vfm_pend"]) == 4:
                        emit_v_transposes(bt)
                elif sname in ("krA", "krB", "qr"):
                    emit_rope(bt, sname, fcg, evsnap)
            st_["wfree"][wsl] = s_t.v

    for bt in range(p1_passes):
        p1_pass(bt)
        if do_cc:
            ccw = [s_sh[bt].cur()] + ([(s_t.h, s_t.v)] if bt == 1 else [])
            op("gpsimd", "collective", kind="AllGather",
               cins=[shard_t[bt].ap()], couts=[gath_t[bt].ap()],
               waits=ccw, inc=s_cc[bt].bump(1))
        if dbg:
            op("gpsimd", "dma_start", out=dbgsh_t[bt].ap(), in_=shard[bt],
               waits=[s_sh[bt].cur()], inc=s_dbg.bump(16))
    QS_TOT = s_stg.cur()
    if dbg:
        wt("gpsimd", s_dbg.cur())
    if "p3" not in phases:
        wt("sync", QS_TOT, s_sh[0].cur(), s_sh[1].cur())
        wt("vector", (s_v.h, s_v.v))
        wt("scalar", (s_s.h, s_s.v))
        wt("tensor", (s_t.h, s_t.v))
        if do_cc:
            wt("gpsimd", (s_cc[0].h, 1) if p1_passes > 0 else (s_cc[0].h, 0))
            if p1_passes > 1:
                wt("gpsimd", (s_cc[1].h, 1))

    if "p3" not in phases:
        return _finish(nc, es, ops, ENGS, NCORE, drains)
    AB.reset(); AFA.reset()
    AB.mark(); AFA.mark()
    qbuf = [AB.alloc(6 * 1024) for _ in range(2)]
    kvb = [AB.alloc(10240) for _ in range(2)]
    ptil = [AB.alloc(1024) for _ in range(6)]
    onstg = [AB.alloc(1024) for _ in range(4)]
    lsb16s = [AB.alloc(1024, parts=1) for _ in range(4)]
    osb = [AFA.alloc(1024) for _ in range(4)]
    lsb = [AFA.alloc(1024, parts=1) for _ in range(4)]

    OO, LL = 2048, 3072
    P1T, P1V, P1S = s_t.cur(), s_v.cur(), s_s.cur()
    wt("sync", P1T, P1V, P1S)
    wt("gpsimd", P1T, P1V, P1S)

    a_ = dict(scpar=0, ptc=0, pt_cons=[None] * 6, sc_evac=[None] * 4,
              ol_free=None, bc_evac=None, kv_free=[None, None], q_free=[None, None],
              oni=0, sweep=0, kvctr=0, pend_onst=[], lp_snap=[None, None], lp_free=None, ptil_guard=None)

    def kv_load(bt, g, bi, slot):
        h0 = g * 4
        w = [(s_cc[bt].h, 1)] if do_cc else []
        if a_["kv_free"][slot] is not None:
            w.append((s_t.h, a_["kv_free"][slot]))
        gb = gath[bt].rearrange("(r f) c -> r f c", f=5120)
        for hh in range(4):
            h = h0 + hh
            src = gb[:, h * 128:(h + 1) * 128, bi * 128:(bi + 1) * 128]
            dst = kvb[slot][:, hh * 1024:(hh + 1) * 1024].rearrange("p (r t) -> p r t", r=8)
            if hh < 2:
                op("sync", "dma_start", out=dst, in_=src.rearrange("r f c -> f r c"),
                   waits=(w if hh == 0 else ()), inc=s_kv[slot].bump(16))
            else:
                op("scalar", "dma_start", out=dst, in_=src.rearrange("r f c -> f r c"),
                   waits=(w if hh == 2 else ()), inc=s_kvs[slot].bump(16))
        for pp in range(2):
            pj = g * 2 + pp
            src = gb[:, 2048 + pj * 128: 2048 + (pj + 1) * 128, bi * 128:(bi + 1) * 128]
            dst = kvb[slot][:, 4096 + pp * 1024: 4096 + (pp + 1) * 1024].rearrange(
                "p (r t) -> p r t", r=8)
            op("gpsimd", "dma_start", out=dst, in_=src.rearrange("r f c -> f r c"),
               waits=(w if pp == 0 else ()), inc=s_kvg[slot].bump(16))
        dcol0 = h0 * 128
        half = dcol0 // 1024
        col0 = dcol0 % 1024
        src = gb[:, 3072 + 256 * bi: 3072 + 256 * (bi + 1), :].rearrange(
            "r (m two) c -> r m two c", two=2)[:, :, half, col0:col0 + 512]
        dst = kvb[slot][:, 6144:6144 + 4096].rearrange("p (r d) -> p r d", r=8)
        op("gpsimd", "dma_start", out=dst, in_=src.rearrange("r m d -> m r d"),
           inc=s_kvg[slot].bump(16))

    def q_load(bt, g, slot):
        w = [QS_TOT]
        if a_["q_free"][slot] is not None:
            w.append((s_t.h, a_["q_free"][slot]))
        for hh in range(4):
            h = g * 4 + hh
            op("sync", "dma_start", out=qbuf[slot][:, hh * 1024:(hh + 1) * 1024],
               in_=qscr[h * 128:(h + 1) * 128, bt * TPB:(bt + 1) * TPB],
               waits=(w if hh == 0 else ()), inc=s_q[slot].bump(16))
        for pp in range(2):
            pj = g * 2 + pp
            op("sync", "dma_start", out=qbuf[slot][:, 4096 + pp * 1024: 4096 + (pp + 1) * 1024],
               in_=qscr[2048 + pj * 128: 2048 + (pj + 1) * 128, bt * TPB:(bt + 1) * TPB],
               inc=s_q[slot].bump(16))

    def flush_onst():
        for (h, bt2, osl, onsnap) in a_["pend_onst"]:
            op("sync", "dma_start",
               out=onorm[h * 128:(h + 1) * 128, bt2 * TPB:(bt2 + 1) * TPB],
               in_=onstg[osl][:, :], waits=[(s_v.h, onsnap)], inc=s_onst.bump(16))
        a_["pend_onst"] = []

    def attention_sweep(bt, g):
        si = a_["sweep"]
        a_["sweep"] += 1
        qsl = si % 2
        q_load(bt, g, qsl)
        qsem = s_q[qsl].cur()
        for bi in range(8):
            ksl = a_["kvctr"] % 2
            a_["kvctr"] += 1
            kv_load(bt, g, bi, ksl)
            if bi == 1:
                flush_onst()
            kvsem = s_kv[ksl].cur()
            kvgsem = s_kvg[ksl].cur()
            kvssem = s_kvs[ksl].cur()
            Nq = 1024 - 128 * bi
            nqt = (Nq + 511) // 512
            qoff = 128 * bi
            for pair in range(2):
                for qh in range(nqt):
                    wq = min(512, Nq - qh * 512)
                    q0 = qoff + qh * 512
                    pend = []

                    def emit_attnv(j, lh, psl, msnap, wq=wq, q0=q0, sem=None):
                        hh = pair * 2 + lh
                        aw = [((sem or s_v).h, msnap)]
                        if j == 0 and a_["ol_free"] is not None:
                            aw.append((s_v.h, a_["ol_free"]))
                        vlhs = kvb[ksl][:, 6144 + j * 512 + hh * 128:
                                        6144 + j * 512 + hh * 128 + 128]
                        op("tensor", "matmul",
                           PS[0:1, 3072 + lh * 512: 3072 + lh * 512 + wq],
                           sb_ones[:, 0:1], ptil[psl][:, 0:wq],
                           start=(j == 0), stop=(j == 7), waits=aw)
                        op("tensor", "matmul",
                           PS[:, 2048 + lh * 512: 2048 + lh * 512 + wq],
                           vlhs, ptil[psl][:, 0:wq],
                           start=(j == 0), stop=(j == 7))
                        tinc("tensor", s_t.bump())
                        a_["pt_cons"][psl] = s_t.v

                    for j in range(8):
                        par = j % 2
                        w = [kvsem, kvgsem, kvssem, qsem]
                        for lh in range(2):
                            if a_["sc_evac"][lh * 2 + par] is not None:
                                w.append((s_s.h, a_["sc_evac"][lh * 2 + par]))
                        if a_["bc_evac"] is not None:
                            w.append((s_v.h, a_["bc_evac"]))
                        for lh in range(2):
                            hh = pair * 2 + lh
                            lhs_n = kvb[ksl][:, hh * 1024 + j * 128: hh * 1024 + j * 128 + 128]
                            rq_n = qbuf[qsl][:, hh * 1024 + q0: hh * 1024 + q0 + wq]
                            op("tensor", "matmul",
                               PS[:, (lh * 2 + par) * 512: (lh * 2 + par) * 512 + wq],
                               lhs_n, rq_n, start=True, stop=False,
                               waits=(w if lh == 0 else ()))
                        for lh in range(2):
                            hh = pair * 2 + lh
                            half = hh % 2
                            lhs_r = kvb[ksl][64 * half:64 * half + 64,
                                             4096 + pair * 1024 + j * 128:
                                             4096 + pair * 1024 + j * 128 + 128]
                            rq_r = qbuf[qsl][64 * half:64 * half + 64,
                                             4096 + pair * 1024 + q0:
                                             4096 + pair * 1024 + q0 + wq]
                            op("tensor", "matmul",
                               PS[:, (lh * 2 + par) * 512: (lh * 2 + par) * 512 + wq],
                               lhs_r, rq_r, start=False, stop=True)
                        tinc("tensor", s_t.bump())
                        scsnap = s_t.v
                        for lh in range(2):
                            psl = a_["ptc"] % 6
                            a_["ptc"] += 1
                            ew = [(s_t.h, scsnap)]
                            if a_["pt_cons"][psl] is not None:
                                ew.append((s_t.h, a_["pt_cons"][psl]))
                            op("scalar", "activation", out=ptil[psl][:, 0:wq],
                               in_=PS[:, (lh * 2 + par) * 512: (lh * 2 + par) * 512 + wq],
                               func=AF.Exp, bias=sb_sm[:, 33:34], scale=SCALE,
                               waits=ew, inc=s_s.bump())
                            a_["sc_evac"][lh * 2 + par] = s_s.v
                            expsnap = s_s.v
                            if qh == 0:
                                op("vector", "tensor_tensor", out=ptil[psl][:, 0:128],
                                   in0=ptil[psl][:, 0:128],
                                   in1=sb_mask[:, j * 128:(j + 1) * 128],
                                   op=ALU.mult, waits=[(s_s.h, expsnap)], inc=s_v.bump(),
                                   nodrain=True)
                                msnap = s_v.v
                            else:
                                msnap = None
                            pend.append((j, lh, psl, msnap if msnap is not None else expsnap))
                            if msnap is None:
                                pass
                        while len(pend) > 4:
                            jj, lh2, psl2, sn2 = pend.pop(0)
                            emit_attnv(jj, lh2, psl2, sn2,
                                       sem=(s_v if qh == 0 else s_s))
                    while pend:
                        jj, lh2, psl2, sn2 = pend.pop(0)
                        emit_attnv(jj, lh2, psl2, sn2,
                                   sem=(s_v if qh == 0 else s_s))
                    avsnap = s_t.v
                    for lh in range(2):
                        hh = pair * 2 + lh
                        if bi == 0:
                            op("vector", "tensor_copy", out=osb[hh][:, q0:q0 + wq],
                               in_=PS[:, 2048 + lh * 512: 2048 + lh * 512 + wq],
                               waits=([(s_t.h, avsnap)] if lh == 0 else ()), nodrain=(lh > 0))
                            op("vector", "tensor_copy", out=lsb[hh][0:1, q0:q0 + wq],
                               in_=PS[0:1, 3072 + lh * 512: 3072 + lh * 512 + wq],
                               inc=s_v.bump(), nodrain=True)
                        else:
                            op("vector", "tensor_tensor", out=osb[hh][:, q0:q0 + wq],
                               in0=osb[hh][:, q0:q0 + wq],
                               in1=PS[:, 2048 + lh * 512: 2048 + lh * 512 + wq],
                               op=ALU.add, waits=([(s_t.h, avsnap)] if lh == 0 else ()),
                               nodrain=(lh > 0))
                            op("vector", "tensor_tensor", out=lsb[hh][0:1, q0:q0 + wq],
                               in0=lsb[hh][0:1, q0:q0 + wq],
                               in1=PS[0:1, 3072 + lh * 512: 3072 + lh * 512 + wq],
                               op=ALU.add, inc=s_v.bump(), nodrain=True)
                    a_["ol_free"] = s_v.v
            a_["kv_free"][ksl] = s_t.v
        a_["q_free"][qsl] = s_t.v
        rsnaps = []
        for hh in range(4):
            op("vector", "reciprocal", lsb[hh][0:1, :], lsb[hh][0:1, :], nodrain=(hh > 0))
            op("vector", "tensor_copy", out=lsb16s[hh][0:1, :], in_=lsb[hh][0:1, :],
               inc=s_v.bump())
            rsnaps.append(s_v.v)
        mult_snap = [None, None]
        for hh in range(4):
            h = g * 4 + hh
            pb_off = (hh % 2) * 1024
            bw = [(s_v.h, rsnaps[hh])]
            if mult_snap[hh % 2] is not None:
                bw.append((s_v.h, mult_snap[hh % 2]))
            elif a_["bc_evac"] is not None:
                bw.append((s_v.h, a_["bc_evac"]))
            for k4 in range(4):
                if a_["sc_evac"][k4] is not None:
                    bw.append((s_s.h, a_["sc_evac"][k4]))
            for qt in range(2):
                op("tensor", "matmul", PS[:, pb_off + qt * 512: pb_off + qt * 512 + 512],
                   sb_ones[0:1, :], lsb16s[hh][0:1, qt * 512: qt * 512 + 512],
                   start=True, stop=True, waits=(bw if qt == 0 else ()))
            tinc("tensor", s_t.bump())
            bsnap = s_t.v
            osl = a_["oni"] % 4
            a_["oni"] += 1
            ow = [(s_t.h, bsnap), s_onst.cur()]
            op("vector", "tensor_tensor", out=onstg[osl][:, :], in0=osb[hh][:, :],
               in1=PS[:, pb_off: pb_off + 1024], op=ALU.mult, waits=ow, inc=s_v.bump())
            mult_snap[hh % 2] = s_v.v
            a_["bc_evac"] = s_v.v
            onsnap = s_v.v
            a_["pend_onst"].append((h, bt, osl, onsnap))

    for bt in range(2):
        for g in range(4):
            attention_sweep(bt, g)
    flush_onst()
    ON_TOT = s_onst.cur()
    if "p4" not in phases:
        wt("sync", ON_TOT)

    if "p4" not in phases:
        return _finish(nc, es, ops, ENGS, NCORE, drains)
    AB.reset(); AFA.reset()
    AB.mark(); AFA.mark()
    woT = AB.alloc(16 * 2048)
    onb = [AB.alloc(2048) for _ in range(2)]
    x2l = [AB.alloc(2048) for _ in range(2)]
    x2T = AB.alloc(16 * 256)
    aTb = AB.alloc(16 * E * R)
    bwTb = AB.alloc(E * D, parts=64)
    wgTb = AB.alloc(16 * E)
    ghb = AB.alloc(E * 256, parts=64)
    gbc = AB.alloc(256, parts=64)
    gtb = AB.alloc(256, parts=8)
    scr8 = AB.alloc(8)
    xf4 = [AFA.alloc(2048) for _ in range(2)]
    xnw = [AFA.alloc(2048) for _ in range(2)]

    P3T, P3V, P3S = s_t.cur(), s_v.cur(), s_s.cur()
    wt("sync", P3T, P3V, P3S)
    wt("gpsimd", P3T, P3V, P3S)

    op("gpsimd", "dma_start", out=woT[:, 0:8 * 2048].rearrange("p (c d) -> p c d", c=8),
       in_=woTr.rearrange("(c p) d -> p c d", p=128)[:, 0:8, :], inc=s_w[0].bump(16))
    op("gpsimd", "dma_start", out=woT[:, 8 * 2048:].rearrange("p (c d) -> p c d", c=8),
       in_=woTr.rearrange("(c p) d -> p c d", p=128)[:, 8:16, :], inc=s_w[0].bump(16))
    op("gpsimd", "dma_start", out=aTb, in_=aTh, inc=s_w[0].bump(16))
    op("gpsimd", "dma_start", out=bwTb, in_=bwTh, inc=s_w[0].bump(16))
    op("gpsimd", "dma_start", out=wgTb, in_=wgTh, inc=s_w[0].bump(16))
    P4W = s_w[0].cur()

    f_ = dict(on_free=[None, None], x_free=[None, None], tp_evac=[None, None],
              x2l_free=[None, None], x2T_cons=None, logit_free=None,
              gt_free=None, gbc_free=None, h_free=[None, None],
              py_free=[None, None], xn_store=[None, None], op_free=None,
              y67_free=None, xnw_snap=[None, None])

    def p4_loads(bt, gg):
        for ii, tt in enumerate((gg * 2, gg * 2 + 1)):
            r0 = bt * TPB + tt * 128
            osl = tt % 2
            xsl = tt % 2
            w = [ON_TOT]
            if f_["on_free"][osl] is not None:
                w.append((s_t.h, f_["on_free"][osl]))
            src2 = onorm.rearrange("(c p) t -> p c t", p=128)[:, :, r0: r0 + 128]
            onv = onb[osl].rearrange("p (c t) -> p c t", c=16)
            op("sync", "dma_start", out=onv[:, 0:8, :], in_=src2[:, 0:8, :],
               waits=w, inc=s_on[osl].bump(16))
            op("sync", "dma_start", out=onv[:, 8:16, :], in_=src2[:, 8:16, :],
               inc=s_on[osl].bump(16))
            xw = []
            if f_["x_free"][xsl] is not None:
                xw.append((s_v.h, f_["x_free"][xsl]))
            op("sync", "dma_start", out=xf4[xsl], in_=x_sl[r0: r0 + 128, :], waits=xw,
               inc=s_x[xsl].bump(16))

    def p4_opmm(bt, gg, ii):
        tt = gg * 2 + ii
        osl = tt % 2
        mw = [s_on[osl].cur(), P4W]
        if f_["op_free"] is not None:
            mw.append((s_v.h, f_["op_free"]))
        for hc in range(16):
            for dt in range(4):
                op("tensor", "matmul", PS[:, dt * 512: dt * 512 + 512],
                   onb[osl][:, hc * 128:(hc + 1) * 128],
                   woT[:, hc * 2048 + dt * 512: hc * 2048 + dt * 512 + 512],
                   start=(hc == 0), stop=False,
                   waits=(mw if (hc == 0 and dt == 0) else ()))
        for dt in range(4):
            op("tensor", "matmul", PS[:, dt * 512: dt * 512 + 512],
               sb_ones[0:1, 0:128], sb_bo16[0:1, dt * 512:(dt + 1) * 512],
               start=False, stop=True)
        tinc("tensor", s_t.bump())
        f_["on_free"][osl] = s_t.v
        f_["opmm_snap_%d" % ii] = s_t.v

    def p4_post_add(bt, gg, ii):
        tt = gg * 2 + ii
        xsl = tt % 2
        mmsnap = f_["opmm_snap_%d" % ii]
        if f_["xn_store"][ii] is not None:
            wt("vector", (s_out[ii].h, f_["xn_store"][ii]))
        vw = [(s_t.h, mmsnap), s_x[xsl].cur()]
        for dt in range(4):
            last = dt == 3
            op("vector", "tensor_tensor", out=xnw[ii][:, dt * 512:(dt + 1) * 512],
               in0=xf4[xsl][:, dt * 512:(dt + 1) * 512], in1=PS[:, dt * 512: dt * 512 + 512],
               op=ALU.add, waits=(vw if dt == 0 else ()),
               inc=(s_v.bump() if last else None), nodrain=(dt > 0))
        f_["x_free"][xsl] = s_v.v
        f_["xnw_snap"][ii] = s_v.v
        f_["op_free"] = s_v.v

    def p4_post_ln(bt, gg, ii):
        if f_.get("p4_ln_s") is not None:
            wt("vector", (s_s.h, f_["p4_ln_s"]))
        for k in range(4):
            op("vector", "bn_stats", bnst[:, 6 * k:6 * k + 6],
               xnw[ii][:, 512 * k:512 * (k + 1)], nodrain=(k > 0))
        op("vector", "bn_aggr", mv[:, :], bnst[:, :], inc=s_v.bump())
        wt("scalar", (s_v.h, s_v.v))
        op("scalar", "activation", out=rstd[:, :], in_=mv[:, 1:2], func=AF.Ln,
           bias=sb_sm[:, 32:33], scale=1.0)
        op("scalar", "activation", out=rstd[:, :], in_=rstd[:, :], func=AF.Exp, bias=0.0,
           scale=-0.5)
        op("scalar", "activation", out=negr[:, :], in_=rstd[:, :], func=AF.Identity,
           bias=sb_sm[:, 34:35], scale=-1.0)
        op("scalar", "activation", out=nmr[:, :], in_=mv[:, 0:1], func=AF.Identity,
           bias=sb_sm[:, 34:35], scale=negr[:, 0:1])
        x2w = []
        if f_["x2l_free"][ii] is not None:
            x2w.append((s_t.h, f_["x2l_free"][ii]))
        op("scalar", "activation", out=x2l[ii][:, :], in_=xnw[ii], func=AF.Identity,
           bias=nmr[:, 0:1], scale=rstd[:, 0:1], waits=x2w, inc=s_s.bump())
        x2snap = s_s.v
        f_["p4_ln_s"] = s_s.v
        for g4 in range(4):
            par = g4 % 2
            pb = PS[:, 2048 + par * 512: 2048 + par * 512 + 512].bitcast(BF16)
            tw = [(s_s.h, x2snap)]
            if f_["tp_evac"][par] is not None:
                tw.append((s_s.h, f_["tp_evac"][par]))
            if f_["h_free"][par] is not None:
                tw.append((s_v.h, f_["h_free"][par]))
            for j in range(4):
                dc = g4 * 4 + j
                op("tensor", "transpose", pb[:, j * 128:(j + 1) * 128],
                   x2l[ii][:, dc * 128:(dc + 1) * 128], sb_id[:, :],
                   waits=(tw if j == 0 else ()))
            tinc("tensor", s_t.bump())
            tsn = s_t.v
            dview = x2T.rearrange("p (c t) -> p c t", c=16)[:, g4 * 4: g4 * 4 + 4,
                                                            ii * 128:(ii + 1) * 128]
            ew = [(s_t.h, tsn)]
            if ii == 0 and g4 == 0 and f_["x2T_cons"] is not None:
                ew.append((s_t.h, f_["x2T_cons"]))
            op("scalar", "copy", out=dview,
               in_=pb[:, 0:512].rearrange("p (a b) -> p a b", a=4),
               waits=ew, inc=s_s.bump())
            f_["tp_evac"][par] = s_s.v
        f_["x2l_free"][ii] = s_t.v

    def p4_ffn(bt, gg, next_cb=None):
        tts = (gg * 2, gg * 2 + 1)
        x2T_snap = s_s.cur()
        for ii in range(2):
            lw = [x2T_snap]
            if f_["logit_free"] is not None:
                lw.append((s_v.h, f_["logit_free"]))
            if f_["y67_free"] is not None:
                lw.append((s_v.h, f_["y67_free"]))
            for dc in range(16):
                op("tensor", "matmul", PS[:, 3072:3080],
                   x2T.rearrange("p (c t) -> p c t", c=16)[:, dc, ii * 128:(ii + 1) * 128],
                   wgTb.rearrange("p (c e) -> p c e", c=16)[:, dc, :],
                   start=(dc == 0), stop=(dc == 15), waits=(lw if dc == 0 else ()))
            tinc("tensor", s_t.bump())
            lsnap = s_t.v
            op("vector", "tensor_reduce", sb_sm[:, 28:29], PS[:, 3072:3080],
               mybir.AxisListType.X, ALU.max, waits=[(s_t.h, lsnap)])
            op("vector", "tensor_scalar", out=sb_sm[:, 29:30], in0=sb_sm[:, 28:29],
               scalar1=-1.0, scalar2=None, op0=ALU.mult, inc=s_v.bump())
            wt("scalar", (s_v.h, s_v.v))
            op("scalar", "activation", out=scr8[:, :], in_=PS[:, 3072:3080], func=AF.Exp,
               bias=sb_sm[:, 29:30], scale=1.0, accum_out=sb_sm[:, 30:31], inc=s_s.bump())
            wt("vector", (s_s.h, s_s.v))
            op("vector", "reciprocal", sb_sm[:, 31:32], sb_sm[:, 30:31])
            op("vector", "tensor_scalar", out=scr8[:, :], in0=PS[:, 3072:3080],
               scalar1=sb_sm[:, 29:30], scalar2=0.0, op0=ALU.add, op1=ALU.is_equal)
            op("vector", "tensor_scalar", out=scr8[:, :], in0=scr8[:, :],
               scalar1=sb_sm[:, 31:32], scalar2=MOE_SCALING, op0=ALU.mult, op1=ALU.mult,
               inc=s_v.bump())
            f_["logit_free"] = s_v.v
            gsnap = s_v.v
            pgt = PS[:, 3584:3648].bitcast(BF16)
            gw = [(s_v.h, gsnap)]
            if f_["gt_free"] is not None:
                gw.append((s_v.h, f_["gt_free"]))
            if f_["y67_free"] is not None:
                gw.append((s_v.h, f_["y67_free"]))
            op("tensor", "transpose", pgt[0:8, 0:128], scr8[:, :], sb_id[:, :], waits=gw)
            tinc("tensor", s_t.bump())
            gtsn = s_t.v
            op("vector", "tensor_copy", out=gtb[0:8, ii * 128:(ii + 1) * 128],
               in_=pgt[0:8, 0:128], waits=[(s_t.h, gtsn)], inc=s_v.bump())
            f_["gt_free"] = s_v.v
        gT_snap = s_v.cur()

        if next_cb is not None:
            next_cb()

        for e in range(E):
            par = e % 2
            bcw = [gT_snap]
            if f_["gbc_free"] is not None:
                bcw.append((s_v.h, f_["gbc_free"]))
            if f_["y67_free"] is not None:
                bcw.append((s_v.h, f_["y67_free"]))
            op("tensor", "matmul", PS[0:64, 3584:3840], sb_oh[:, e * 64:(e + 1) * 64],
               gtb[0:8, 0:256], start=True, stop=True, waits=bcw)
            tinc("tensor", s_t.bump())
            bcsnap = s_t.v
            op("vector", "tensor_copy", out=gbc[0:64, :], in_=PS[0:64, 3584:3840],
               waits=[(s_t.h, bcsnap)], inc=s_v.bump())
            gbcsnap = s_v.v
            f_["gbc_free"] = gbcsnap
            hw = [(s_v.h, gbcsnap)]
            if f_["h_free"][par] is not None:
                hw.append((s_v.h, f_["h_free"][par]))
            for dc in range(16):
                op("tensor", "matmul", PS[0:64, 2048 + par * 512: 2048 + par * 512 + 256],
                   aTb.rearrange("p (c e r) -> p c e r", c=16, e=E)[:, dc, e, :],
                   x2T.rearrange("p (c t) -> p c t", c=16)[:, dc, :],
                   start=(dc == 0), stop=(dc == 15), waits=(hw if dc == 0 else ()))
            tinc("tensor", s_t.bump())
            hsnap = s_t.v
            op("vector", "tensor_tensor", out=ghb[0:64, e * 256:(e + 1) * 256],
               in0=PS[0:64, 2048 + par * 512: 2048 + par * 512 + 256], in1=gbc[0:64, :],
               op=ALU.mult, waits=[(s_t.h, hsnap)], inc=s_v.bump())
            f_["h_free"][par] = s_v.v
        gh_snap = s_v.cur()
        f_["x2T_cons"] = s_t.v

        ycnt = 0
        for ii in range(2):
            tt = tts[ii]
            r0 = bt * TPB + tt * 128
            for dt in range(4):
                yb = 3072 + (ycnt % 2) * 512
                ycnt += 1
                yw = [gh_snap]
                if f_["py_free"][(ycnt - 1) % 2] is not None:
                    yw.append((s_v.h, f_["py_free"][(ycnt - 1) % 2]))
                for e in range(E):
                    op("tensor", "matmul", PS[:, yb: yb + 512],
                       ghb[0:64, e * 256 + ii * 128: e * 256 + ii * 128 + 128],
                       bwTb[0:64, e * 2048 + dt * 512: e * 2048 + dt * 512 + 512],
                       start=(e == 0), stop=(e == E - 1), waits=(yw if e == 0 else ()))
                tinc("tensor", s_t.bump())
                ysnap = s_t.v
                op("vector", "tensor_tensor", out=xnw[ii][:, dt * 512:(dt + 1) * 512],
                   in0=xnw[ii][:, dt * 512:(dt + 1) * 512],
                   in1=PS[:, yb: yb + 512],
                   op=ALU.add, waits=[(s_t.h, ysnap)], inc=s_v.bump(), nodrain=(dt > 0))
                f_["py_free"][(ycnt - 1) % 2] = s_v.v
                f_["y67_free"] = s_v.v
            donesnap = s_v.v
            op("sync", "dma_start", out=out_sl[r0: r0 + 128, :], in_=xnw[ii][:, :],
               waits=[(s_v.h, donesnap)], inc=s_out[ii].bump(16))
            f_["xn_store"][ii] = s_out[ii].v

    groups = [(bt, gg) for bt in range(2) for gg in range(4)]
    p4_loads(*groups[0])
    p4_opmm(*groups[0], 0)
    p4_post_add(*groups[0], 0)
    p4_opmm(*groups[0], 1)
    p4_post_ln(*groups[0], 0)
    p4_post_add(*groups[0], 1)
    p4_post_ln(*groups[0], 1)
    for gi, (bt, gg) in enumerate(groups):
        nxt = groups[gi + 1] if gi + 1 < len(groups) else None
        if nxt is not None:
            p4_loads(*nxt)

            def cb(nxt=nxt):
                p4_opmm(*nxt, 0)
            p4_ffn(bt, gg, next_cb=cb)
            p4_post_add(*nxt, 0)
            p4_opmm(*nxt, 1)
            p4_post_ln(*nxt, 0)
            p4_post_add(*nxt, 1)
            p4_post_ln(*nxt, 1)
        else:
            p4_ffn(bt, gg)

    wt("sync", s_out[0].cur(), s_out[1].cur())
    return _finish(nc, es, ops, ENGS, NCORE, drains)


def _finish(nc, es, ops, engs, ncore, drains=False):
    block = es.enter_context(nc.Block())

    def runner(lst, auto_drain=False):
        def go(e):
            for meth, args, kw, waits, inc in lst:
                for (h, v) in waits:
                    e.wait_ge(h, v)
                if meth == "_wait":
                    continue
                if meth == "_drain":
                    e.drain()
                    continue
                if meth.startswith("!"):
                    meth = meth[1:]
                elif auto_drain and meth not in ("_nop", "dma_start"):
                    e.drain()
                if meth == "_nop":
                    ins = e.nop()
                elif meth == "collective":
                    ins = e.collective_compute(
                        kw["kind"], ALU.bypass,
                        replica_groups=[list(range(ncore))],
                        ins=[a.opt() for a in kw["cins"]],
                        outs=[a.opt() for a in kw["couts"]],
                    )
                    if inc is not None:
                        ins.then_inc(inc[0])
                    continue
                else:
                    ins = getattr(e, meth)(*args, **kw)
                if inc is not None:
                    ins.then_inc(inc[0], inc[2])
        return go

    @block.tensor
    def _(e):
        runner(ops["tensor"])(e)

    @block.vector
    def _(e):
        runner(ops["vector"], auto_drain=drains)(e)

    @block.scalar
    def _(e):
        runner(ops["scalar"], auto_drain=True)(e)

    @block.sync
    def _(e):
        runner(ops["sync"])(e)

    @block.gpsimd
    def _(e):
        runner(ops["gpsimd"])(e)

    es.close()
    return nc



def _prep_inputs(kw):
    f32 = np.float32
    bf = ml_dtypes.bfloat16
    x = np.asarray(kw["L0_attn_x"], f32)

    def T(a):
        return np.ascontiguousarray(np.asarray(a, f32).T)

    wkdT = T(kw["L0_attn_Wkd"])
    wqdT = T(kw["L0_attn_Wqd"])
    wkuT = T(kw["L0_attn_Wku"])
    wvuT = T(kw["L0_attn_Wvu"])
    wquT = T(kw["L0_attn_Wqu"])
    wkrT = np.ascontiguousarray(np.asarray(kw["L0_attn_Wkr"], f32)[ROPE_ORDER, :].T)
    wqrT = np.ascontiguousarray(np.asarray(kw["L0_attn_Wqr"], f32)[ROPE_ORDER, :].T)
    woTr = T(kw["L0_attn_Wo"])
    A = np.asarray(kw["L0_ffn_A"], f32)
    Bw = np.asarray(kw["L0_ffn_Bw"], f32)
    Wg = np.asarray(kw["L0_ffn_Wg"], f32)

    aTh = np.ascontiguousarray(
        A.transpose(2, 0, 1).reshape(16, 128, E, R).transpose(1, 0, 2, 3).reshape(128, 16 * E * R))
    bwTh = np.ascontiguousarray(Bw.transpose(2, 0, 1).reshape(64, E * D))
    wgTh = np.ascontiguousarray(
        Wg.T.reshape(16, 128, E).transpose(1, 0, 2).reshape(128, 16 * E))

    def b2(v, nf):
        return np.ascontiguousarray(np.asarray(v, f32).reshape(nf, 128).T)

    bkd2 = b2(kw["L0_attn_bkd"], 4)
    bqd2 = b2(kw["L0_attn_bqd"], 12)
    bku2 = b2(kw["L0_attn_bku"], 16)
    bvu2 = b2(kw["L0_attn_bvu"], 16)
    bqu2 = b2(kw["L0_attn_bqu"], 16)
    bkr2 = b2(np.asarray(kw["L0_attn_bkr"], f32)[ROPE_ORDER], 8)
    bqr2 = b2(np.asarray(kw["L0_attn_bqr"], f32)[ROPE_ORDER], 8)
    bo2 = np.ascontiguousarray(np.asarray(kw["L0_attn_bo"], f32).reshape(1, D))

    ident = np.eye(128, dtype=bf)
    ones = np.ones((128, 128), dtype=bf)
    onehot = np.zeros((8, E * 64), dtype=bf)
    for e in range(E):
        onehot[e, e * 64:(e + 1) * 64] = 1.0

    inv_freq = (1.0 / (10000.0 ** (np.arange(0, DR, 2, dtype=np.float64) / DR))).astype(f32)

    shared = dict(wkdT=wkdT, wqdT=wqdT, wkuT=wkuT, wvuT=wvuT, wquT=wquT,
                  wkrT=wkrT, wqrT=wqrT, woTr=woTr, aTh=aTh, bwTh=bwTh, wgTh=wgTh,
                  bkd2=bkd2, bqd2=bqd2, bku2=bku2, bvu2=bvu2, bqu2=bqu2,
                  bkr2=bkr2, bqr2=bqr2, bo2=bo2, identt=ident, onest=ones,
                  onehot=onehot)
    in_maps = []
    for c in range(NCORE):
        pos = (8 * np.arange(TPB, dtype=f32) + c).astype(f32)
        ang = pos[None, :] * inv_freq[:, None]
        cos64 = np.cos(ang).astype(f32)
        sin64 = np.sin(ang).astype(f32)
        cosp = np.empty((64, TPB), f32)
        sinp = np.empty((64, TPB), f32)
        for j, d in enumerate(PERM64):
            cosp[j] = cos64[d % 32]
            sinp[j] = sin64[d % 32] * (-1.0 if d < 32 else 1.0)
        costab = np.ascontiguousarray(np.concatenate([cosp, cosp], 0))
        sintb = np.ascontiguousarray(np.concatenate([sinp, sinp], 0))
        m = np.arange(128)
        jj = np.arange(8)
        mq = np.arange(128)
        mk = (8 * m[:, None, None] + jj[None, :, None] <= 8 * mq[None, None, :] + c)
        masktab = np.ascontiguousarray(mk.astype(bf).reshape(128, 8 * 128))
        x_c = np.ascontiguousarray(x[:, c::8, :].reshape(2 * TPB, D))
        im = dict(shared)
        im.update(x_sl=x_c, cost=costab, sint=sintb, maskt=masktab)
        in_maps.append(im)
    return in_maps


_NC_CACHE = {}


def kernel(**kw):
    in_maps = _prep_inputs(kw)
    if "nc" not in _NC_CACHE:
        _NC_CACHE["nc"] = build_nc(drains=True)
    nc = _NC_CACHE["nc"]
    res = run_bass_kernel_spmd(nc, in_maps, core_ids=list(range(NCORE)))
    out = np.empty((B, S, D), np.float32)
    for c in range(NCORE):
        o = np.asarray(res.results[c]["out_sl"], np.float32).reshape(B, TPB, D)
        out[:, c::8, :] = o
    return out


# version 23
# speedup vs baseline: 1.0469x; 1.0469x over previous
import math
import numpy as np
import ml_dtypes
import concourse.bass as bass
import concourse.mybir as mybir
from concourse.bass_utils import run_bass_kernel_spmd
from contextlib import ExitStack

F32 = mybir.dt.float32
BF16 = mybir.dt.bfloat16
AF = mybir.ActivationFunctionType
ALU = mybir.AluOpType

B, S, D = 2, 8192, 2048
H, DH, DR = 16, 128, 64
DCKV, DCQ = 512, 1536
E, R = 8, 64
NCORE = 8
TPB = S // NCORE
SCALE = 1.0 / math.sqrt(DH + DR)
EXPB = 12.0
EPS = 1e-5
MOE_SCALING = 4.0

PERM64 = list(range(0, 16)) + list(range(32, 48)) + list(range(16, 32)) + list(range(48, 64))
ROPE_ORDER = [h * 64 + j for h in range(H) for j in PERM64]

ENGS = ("tensor", "vector", "scalar", "sync", "gpsimd")
LN_CUT = None


class Sem:
    def __init__(self, nc, es, name):
        self.h = es.enter_context(nc.semaphore(name))
        self.v = 0

    def bump(self, n=1):
        self.v += n
        return (self.h, self.v, n)

    def cur(self):
        return (self.h, self.v)


class Arena:
    def __init__(self, t, cols):
        self.t, self.cols, self.off, self.stack = t, cols, 0, []

    def alloc(self, n, parts=128):
        assert self.off + n <= self.cols, ("arena overflow", self.off, n, self.cols)
        sl = self.t[0:parts, self.off:self.off + n]
        self.off += n
        return sl

    def mark(self):
        self.stack.append(self.off)

    def reset(self):
        self.off = self.stack.pop()


def build_nc(dbg=False, phases=("p1", "p3", "p4"), p1_subs=None, p1_passes=2, do_cc=True, drains=False):
    nc = bass.Bass("TRN2", target_bir_lowering=False, debug=False)

    def din(name, shape, dt=F32):
        return nc.dram_tensor(name, shape, dt, kind="ExternalInput").ap()

    x_sl = din("x_sl", [2 * TPB, D])
    wkdT = din("wkdT", [D, DCKV])
    wqdT = din("wqdT", [D, DCQ])
    wkuT = din("wkuT", [DCKV, H * DH])
    wvuT = din("wvuT", [DCKV, H * DH])
    wquT = din("wquT", [DCQ, H * DH])
    wkrT = din("wkrT", [D, H * DR])
    wqrT = din("wqrT", [DCQ, H * DR])
    woTr = din("woTr", [H * DH, D])
    aTh = din("aTh", [128, 16 * E * R])
    bwTh = din("bwTh", [64, E * D])
    wgTh = din("wgTh", [128, 16 * E])
    bkd2 = din("bkd2", [128, 4])
    bqd2 = din("bqd2", [128, 12])
    bku2 = din("bku2", [128, 16])
    bvu2 = din("bvu2", [128, 16])
    bqu2 = din("bqu2", [128, 16])
    bkr2 = din("bkr2", [128, 8])
    bqr2 = din("bqr2", [128, 8])
    bo2 = din("bo2", [1, D])
    cost = din("cost", [128, TPB])
    sintab = din("sint", [128, TPB])
    masktab = din("maskt", [128, 8 * 128], BF16)
    identt = din("identt", [128, 128], BF16)
    onest = din("onest", [128, 128], BF16)
    onehot = din("onehot", [8, E * 64], BF16)

    out_sl = nc.dram_tensor("out_sl", [2 * TPB, D], F32, kind="ExternalOutput").ap()

    kscr = "ExternalOutput" if dbg else "Internal"
    def dscr(name, shape):
        if kscr == "Internal":
            return nc.dram_tensor(name, shape, BF16)
        return nc.dram_tensor(name, shape, BF16, kind=kscr)

    qscr_t = dscr("qscr", [3 * 1024, 2 * TPB])
    shard_t = [nc.dram_tensor("shard%d" % b, [5 * 1024, TPB], BF16) for b in range(2)]
    dbgsh_t = None
    if dbg:
        dbgsh_t = [nc.dram_tensor("dbgsh%d" % b, [5 * 1024, TPB], BF16, kind="ExternalOutput")
                   for b in range(2)]
    gath_t = [nc.dram_tensor("gath%d" % b, [NCORE * 5 * 1024, TPB], BF16) for b in range(2)]
    onorm_t = dscr("onormscr", [H * DH, 2 * TPB])
    wscr_t = nc.dram_tensor("wscr", [128, 110592], BF16)
    wscr = wscr_t.ap()
    qscr = qscr_t.ap()
    shard = [t.ap() for t in shard_t]
    gath = [t.ap() for t in gath_t]
    onorm = onorm_t.ap()

    es = ExitStack()
    AB = Arena(es.enter_context(nc.sbuf_tensor("ab16", [128, 73728], BF16)), 73728)
    AFA = Arena(es.enter_context(nc.sbuf_tensor("af32", [128, 8704], F32)), 8704)
    PS = es.enter_context(nc.psum_tensor("ps", [128, 4096], F32))

    sb_id = es.enter_context(nc.sbuf_tensor("sb_id", [128, 128], BF16))
    sb_ones = es.enter_context(nc.sbuf_tensor("sb_ones", [128, 128], BF16))
    sb_mask = es.enter_context(nc.sbuf_tensor("sb_mask", [128, 1024], BF16))
    sb_cos = es.enter_context(nc.sbuf_tensor("sb_cos", [128, TPB], F32))
    sb_sin = es.enter_context(nc.sbuf_tensor("sb_sin", [128, TPB], F32))
    sb_bias = es.enter_context(nc.sbuf_tensor("sb_bias", [128, 80], F32))
    sb_bo = es.enter_context(nc.sbuf_tensor("sb_bo", [1, D], F32))
    sb_bo16 = es.enter_context(nc.sbuf_tensor("sb_bo16", [1, D], BF16))
    sb_oh = es.enter_context(nc.sbuf_tensor("sb_oh", [8, E * 64], BF16))
    sb_sm = es.enter_context(nc.sbuf_tensor("sb_sm", [128, 64], F32))

    BOFF = {}
    _o = 0
    for nm, wdt in [("bkd", 4), ("bqd", 12), ("bku", 16), ("bvu", 16), ("bqu", 16),
                    ("bkr", 8), ("bqr", 8)]:
        BOFF[nm] = _o
        _o += wdt

    sems = {}
    def mksem(name):
        s = Sem(nc, es, name)
        sems[name] = s
        return s
    s_cst = mksem("cst")
    s_x = [mksem("x0"), mksem("x1"), mksem("x2")]
    s_w = [mksem("w0"), mksem("w1")]
    s_kv = [mksem("kv0"), mksem("kv1")]
    s_kvg = [mksem("kvg0"), mksem("kvg1")]
    s_kvs = [mksem("kvs0"), mksem("kvs1")]
    s_wst = mksem("wst")
    s_w2 = [mksem("w20"), mksem("w21")]
    s_dbg = mksem("dbg")
    s_q = [mksem("q0"), mksem("q1")]
    s_on = [mksem("on0"), mksem("on1")]
    s_stg = mksem("stg")
    s_sh = [mksem("sh0"), mksem("sh1")]
    s_onst = mksem("onst")
    s_out = [mksem("outst0"), mksem("outst1")]
    s_cc = [mksem("cc0"), mksem("cc1")]
    s_t = mksem("t")
    s_v = mksem("v")
    s_s = mksem("s")

    ops = {e: [] for e in ENGS}

    def op(eng, meth, *args, waits=(), inc=None, nodrain=False, **kw):
        ops[eng].append((meth if not nodrain else "!" + meth, args, kw, tuple(waits), inc))

    def wt(eng, *waits):
        ops[eng].append(("_wait", (), {}, tuple(waits), None))

    def tinc(eng, bump):
        meth, args, kw, waits, inc = ops[eng][-1]
        assert inc is None and meth not in ("_wait",), (meth, inc)
        ops[eng][-1] = (meth, args, kw, waits, bump)

    for dst, src in [
        (sb_id[:, :], identt), (sb_ones[:, :], onest), (sb_mask[:, :], masktab),
        (sb_cos[:, :], cost), (sb_sin[:, :], sintab), (sb_bo[:, :], bo2),
        (sb_oh[:, :], onehot),
        (sb_bias[:, BOFF["bkd"]:BOFF["bkd"] + 4], bkd2),
        (sb_bias[:, BOFF["bqd"]:BOFF["bqd"] + 12], bqd2),
        (sb_bias[:, BOFF["bku"]:BOFF["bku"] + 16], bku2),
        (sb_bias[:, BOFF["bvu"]:BOFF["bvu"] + 16], bvu2),
        (sb_bias[:, BOFF["bqu"]:BOFF["bqu"] + 16], bqu2),
        (sb_bias[:, BOFF["bkr"]:BOFF["bkr"] + 8], bkr2),
        (sb_bias[:, BOFF["bqr"]:BOFF["bqr"] + 8], bqr2),
    ]:
        op("sync", "dma_start", out=dst, in_=src, inc=s_cst.bump(16))
    CST = s_cst.cur()
    for e in ("tensor", "scalar", "gpsimd", "vector"):
        wt(e, CST)
    op("vector", "memset", sb_sm[:, 32:33], EPS)
    op("vector", "memset", sb_sm[:, 33:34], -EXPB)
    op("vector", "memset", sb_sm[:, 34:35], 0.0)
    op("vector", "tensor_copy", out=sb_bo16[:, :], in_=sb_bo[:, :], inc=s_v.bump())

    AB.mark(); AFA.mark()
    xln = [AB.alloc(2048) for _ in range(2)]
    xlnT = AB.alloc(16 * 1024)
    kvc = AB.alloc(4 * 1024)
    qcs = AB.alloc(12 * 1024)
    stg = [AB.alloc(1024) for _ in range(3)]
    vfm = [AB.alloc(1024) for _ in range(4)]
    krr = [AB.alloc(1024) for _ in range(2)]
    krs = AB.alloc(1024)
    rop = [AB.alloc(1024) for _ in range(2)]
    wbuf = [AB.alloc(12288) for _ in range(2)]
    xf = [AFA.alloc(2048) for _ in range(3)]
    rtmp = [AFA.alloc(1024) for _ in range(2)]
    bnst = sb_sm[:, 0:24]
    mv = sb_sm[:, 24:26]
    rstd = sb_sm[:, 26:27]
    nmr = sb_sm[:, 27:28]
    negr = sb_sm[:, 35:36]

    def xlnT_dc(dc):
        return xlnT[:, dc * 1024:(dc + 1) * 1024]

    st_ = dict(
        x_snap=[None, None, None],
        xln_free=[None, None],
        tp_evac=[None, None],
        proj_evac=[None, None],
        kvc_done=None, qcs_done=None, xlnT_done=None,
        wfree=[None, None],
        wtog=0, pjpar=0, stgi=0,
        vfm_pend=[], vfm_free=[None] * 4,
        vt_par=0, vt_evac=[None, None], wscr_off=0, wscr_map={}, wscr_stc={},
        wst_slot=[None, None], ln_s_snap=None, xlnT_half0=0,
        kr_free=[None, None],
        ropi=0, rop_store_sem=[None, None],
    )

    def emit_v_transposes(bt):
        chunks = list(st_["vfm_pend"])
        st_["vfm_pend"].clear()
        fbase = chunks[0][0]
        lastsnap = chunks[-1][1]
        dcol0 = fbase * 128
        half = dcol0 // 1024
        col0 = dcol0 % 1024
        for tc in range(8):
            par = st_["vt_par"] % 2
            st_["vt_par"] += 1
            pb = PS[:, 2048 + par * 512: 2048 + par * 512 + 512].bitcast(BF16)
            tw = [(s_s.h, lastsnap)]
            if st_["vt_evac"][par] is not None:
                tw.append((s_s.h, st_["vt_evac"][par][1]))
            if st_["tp_evac"][par] is not None:
                tw.append((s_s.h, st_["tp_evac"][par]))
            for j, (fcg, _) in enumerate(chunks):
                op("tensor", "transpose", pb[:, j * 128:(j + 1) * 128],
                   vfm[fcg % 4][:, tc * 128:(tc + 1) * 128], sb_id[:, :],
                   waits=(tw if j == 0 else ()))
            tinc("tensor", s_t.bump())
            tsn = s_t.v
            if tc == 7:
                for fcg, _ in chunks:
                    st_["vfm_free"][fcg % 4] = tsn
            ssl = st_["stgi"] % 3
            st_["stgi"] += 1
            sdst = stg[ssl][:, 0:512]
            ew = [(s_t.h, tsn), s_sh[bt].cur(), s_stg.cur()]
            op("scalar", "copy", out=sdst, in_=pb[:, 0:512], waits=ew, inc=s_s.bump())
            st_["vt_evac"][par] = ("s", s_s.v)
            evs = s_s.v
            dview = shard[bt][3072 + 256 * tc: 3072 + 256 * (tc + 1), :].rearrange(
                "(m two) c -> m two c", two=2)[:, half, col0:col0 + 512]
            op("sync", "dma_start", out=dview, in_=sdst, waits=[(s_s.h, evs)],
               inc=s_sh[bt].bump(16))

    def emit_rope(bt, sname, fcg, evsnap):
        sl = fcg % 2
        rsl = st_["ropi"] % 2
        st_["ropi"] += 1
        shmask = [(i + 16) % 32 for i in range(32)]
        op("vector", "stream_shuffle", krs[:, :], krr[sl][:, :], shmask,
           waits=[(s_s.h, evsnap)])
        op("vector", "tensor_tensor", out=rtmp[0], in0=krr[sl][:, :], in1=sb_cos[:, :], op=ALU.mult,
           nodrain=True)
        op("vector", "tensor_tensor", out=rtmp[1], in0=krs[:, :], in1=sb_sin[:, :], op=ALU.mult)
        rw = []
        prev = st_["rop_store_sem"][rsl]
        if prev is not None:
            rw.append(prev.cur())
        op("vector", "tensor_tensor", out=rop[rsl][:, :], in0=rtmp[0], in1=rtmp[1], op=ALU.add,
           waits=rw, inc=s_v.bump())
        st_["kr_free"][sl] = s_v.v
        rsnap = s_v.v
        if sname in ("krA", "krB"):
            dview = shard[bt][2048 + fcg * 128: 2048 + (fcg + 1) * 128, :]
            sem = s_sh[bt]
        else:
            dview = qscr[2048 + fcg * 128: 2048 + (fcg + 1) * 128, bt * TPB:(bt + 1) * TPB]
            sem = s_stg
        st_["rop_store_sem"][rsl] = sem
        op("sync", "dma_start", out=dview, in_=rop[rsl][:, :], waits=[(s_v.h, rsnap)],
           inc=sem.bump(16))

    SUBS = [
        ("kvc", wkdT, 16, 0, 512, "xlnT", "bkd"),
        ("qcA", wqdT, 16, 0, 768, "xlnT", "bqd"),
        ("qcB", wqdT, 16, 768, 768, "xlnT", "bqd"),
        ("kcnt", wkuT, 4, 0, 2048, "kvc", "bku"),
        ("v", wvuT, 4, 0, 2048, "kvc", "bvu"),
        ("qcntA", wquT, 12, 0, 1024, "qcs", "bqu"),
        ("qcntB", wquT, 12, 1024, 1024, "qcs", "bqu"),
        ("krA", wkrT, 16, 0, 512, "xlnT", "bkr"),
        ("krB", wkrT, 16, 512, 512, "xlnT", "bkr"),
        ("qr", wqrT, 12, 0, 1024, "qcs", "bqr"),
    ]

    def p0_precast():
        off = 0
        for k, (sname, wap, nd, fcol0, FW, _src, _b) in enumerate(SUBS):
            wcols = nd * FW
            st_["wscr_map"][sname] = off
            st_["wscr_stc"][sname] = 16 * (k + 1)
            wsl = k % 2
            ww = []
            if k >= 2:
                ww.append(s_wst.cur())
            ndh = max(1, nd // 2)
            op("gpsimd", "dma_start",
               out=wbuf[wsl][:, 0:ndh * FW].rearrange("p (g f) -> p g f", g=ndh),
               in_=wap.rearrange("(g p) f -> p g f", p=128)[:, 0:ndh, fcol0:fcol0 + FW],
               waits=ww, inc=s_w[wsl].bump(16))
            if nd > ndh:
                op("gpsimd", "dma_start",
                   out=wbuf[wsl][:, ndh * FW:nd * FW].rearrange("p (g f) -> p g f", g=nd - ndh),
                   in_=wap.rearrange("(g p) f -> p g f", p=128)[:, ndh:nd, fcol0:fcol0 + FW],
                   waits=ww, inc=s_w[wsl].bump(16))
            op("gpsimd", "dma_start", out=wscr[:, off:off + wcols],
               in_=wbuf[wsl][:, 0:wcols],
               waits=[s_w[wsl].cur(), s_wst.cur()], inc=s_wst.bump(16))
            off += wcols

    def p1_pass(bt):
        for tc in range(8):
            sl = tc % 3
            sl2 = tc % 2
            w = []
            if st_["x_snap"][sl] is not None:
                w.append((s_s.h, st_["x_snap"][sl][1]))
            op("sync", "dma_start", out=xf[sl],
               in_=x_sl[bt * TPB + tc * 128: bt * TPB + (tc + 1) * 128, :],
               waits=w, inc=s_x[sl].bump(16))
            if LN_CUT == "x":
                st_["x_snap"][sl] = None
                continue
            wt("vector", s_x[sl].cur())
            if st_["ln_s_snap"] is not None:
                wt("vector", (s_s.h, st_["ln_s_snap"]))
            for k in range(4):
                op("vector", "bn_stats", bnst[:, 6 * k:6 * k + 6], xf[sl][:, 512 * k:512 * (k + 1)],
                   nodrain=(k > 0))
            op("vector", "bn_aggr", mv[:, :], bnst[:, :], inc=s_v.bump())
            if LN_CUT == "bn":
                st_["x_snap"][sl] = s_v.v
                continue
            wt("scalar", (s_v.h, s_v.v))
            op("scalar", "activation", out=rstd[:, :], in_=mv[:, 1:2], func=AF.Ln, bias=sb_sm[:, 32:33], scale=1.0)
            op("scalar", "activation", out=rstd[:, :], in_=rstd[:, :], func=AF.Exp, bias=0.0,
               scale=-0.5)
            op("scalar", "activation", out=negr[:, :], in_=rstd[:, :], func=AF.Identity,
               bias=sb_sm[:, 34:35], scale=-1.0)
            op("scalar", "activation", out=nmr[:, :], in_=mv[:, 0:1], func=AF.Identity,
               bias=sb_sm[:, 34:35], scale=negr[:, 0:1])
            xw = []
            if st_["xln_free"][sl2] is not None:
                xw.append((s_t.h, st_["xln_free"][sl2]))
            op("scalar", "activation", out=xln[sl2][:, :], in_=xf[sl], func=AF.Identity,
               bias=nmr[:, 0:1], scale=rstd[:, 0:1], waits=xw, inc=s_s.bump())
            st_["x_snap"][sl] = ("s", s_s.v)
            st_["ln_s_snap"] = s_s.v
            xlnsnap = ("s", s_s.v)
            if LN_CUT == "xln":
                st_["xln_free"][sl2] = None
                continue
            for g4 in range(4):
                par = g4 % 2
                pb = PS[:, 2048 + par * 512: 2048 + par * 512 + 512].bitcast(BF16)
                tw = [(s_s.h, xlnsnap[1])]
                if st_["tp_evac"][par] is not None:
                    tw.append((s_s.h, st_["tp_evac"][par]))
                if st_["vt_evac"][par] is not None:
                    tw.append((s_s.h, st_["vt_evac"][par][1]))
                for j in range(4):
                    dc = g4 * 4 + j
                    op("tensor", "transpose", pb[:, j * 128:(j + 1) * 128],
                       xln[sl2][:, dc * 128:(dc + 1) * 128], sb_id[:, :],
                       waits=(tw if j == 0 else ()))
                tinc("tensor", s_t.bump())
                tsn = s_t.v
                if LN_CUT == "tpv":
                    op("vector", "tensor_copy", out=xlnT[:, g4 * 512:(g4 + 1) * 512],
                       in_=pb[:, 0:512], waits=[(s_t.h, tsn)], inc=s_v.bump())
                    st_["tp_evac"][par] = None
                    stv_snap = s_v.v
                    op("scalar", "_wait_only") if False else None
                    st_["vt_evac"][par] = stv_snap
                    continue
                if LN_CUT == "tps":
                    op("scalar", "copy", out=xlnT[:, g4 * 512:(g4 + 1) * 512],
                       in_=pb[:, 0:512], waits=[(s_t.h, tsn)], inc=s_s.bump())
                    st_["tp_evac"][par] = s_s.v
                    continue
                dview = xlnT.rearrange("p (c t) -> p c t", c=16)[:, g4 * 4:g4 * 4 + 4,
                                                                tc * 128:(tc + 1) * 128]
                op("scalar", "copy", out=dview,
                   in_=pb[:, 0:512].rearrange("p (a b) -> p a b", a=4),
                   waits=[(s_t.h, tsn)], inc=s_s.bump())
                st_["tp_evac"][par] = s_s.v
            st_["xln_free"][sl2] = s_t.v
            if tc == 3:
                st_["xlnT_half0"] = s_s.v
        st_["xlnT_done"] = s_s.v

        if p1_subs is not None and len(p1_subs) == 0:
            return
        subs = list(SUBS)
        if p1_subs is not None:
            subs = [s for s in subs if s[0] in p1_subs]
        for sname, wap, nd, fcol0, FW, src, bnm in subs:
            wsl = st_["wtog"] % 2
            st_["wtog"] += 1
            ww = []
            if st_["wfree"][wsl] is not None:
                ww.append((s_t.h, st_["wfree"][wsl]))
            wcols = nd * FW
            if bt == 0:
                woff = st_["wscr_off"]
                st_["wscr_map"][sname] = woff
                st_["wscr_off"] += wcols
                if st_["wfree"][wsl] is not None and st_["wst_slot"][wsl] is not None:
                    ww.append((s_wst.h, st_["wst_slot"][wsl]))
                ndh = max(1, nd // 2)
                op("gpsimd", "dma_start",
                   out=wbuf[wsl][:, 0:ndh * FW].rearrange("p (g f) -> p g f", g=ndh),
                   in_=wap.rearrange("(g p) f -> p g f", p=128)[:, 0:ndh, fcol0:fcol0 + FW],
                   waits=ww, inc=s_w[wsl].bump(16))
                if nd > ndh:
                    op("gpsimd", "dma_start",
                       out=wbuf[wsl][:, ndh * FW:nd * FW].rearrange("p (g f) -> p g f", g=nd - ndh),
                       in_=wap.rearrange("(g p) f -> p g f", p=128)[:, ndh:nd, fcol0:fcol0 + FW],
                       waits=ww, inc=s_w[wsl].bump(16))
                wsem = s_w[wsl].cur()
                op("scalar", "dma_start", out=wscr[:, woff:woff + wcols],
                   in_=wbuf[wsl][:, 0:wcols], waits=[wsem], inc=s_wst.bump(16))
                st_["wst_slot"][wsl] = s_wst.v
                st_["wscr_stc"][sname] = s_wst.v
            else:
                ww.append((s_wst.h, st_["wscr_stc"][sname]))
                woff = st_["wscr_map"][sname]
                op("sync", "dma_start", out=wbuf[wsl][:, 0:wcols],
                   in_=wscr[:, woff:woff + wcols],
                   waits=ww, inc=s_w2[wsl].bump(16))
                wsem = s_w2[wsl].cur()
            nf = FW // 128
            for fc in range(nf):
                fcg = (fcol0 // 128) + fc
                par = st_["pjpar"] % 2
                st_["pjpar"] += 1
                pdst = PS[:, par * 1024:(par + 1) * 1024]
                mw = [wsem]
                if st_["proj_evac"][par] is not None:
                    mw.append((s_s.h, st_["proj_evac"][par]))
                mwh = list(mw)
                if src == "xlnT":
                    mw.append((s_s.h, st_["xlnT_half0"]))
                    mwh.append((s_s.h, st_["xlnT_done"]))
                elif src == "kvc":
                    mw.append((s_s.h, st_["kvc_done"]))
                else:
                    mw.append((s_s.h, st_["qcs_done"]))
                for qh in range(2):
                    for dc in range(nd):
                        if src == "xlnT":
                            rhs_full = xlnT_dc(dc)
                        elif src == "kvc":
                            rhs_full = kvc[:, dc * 1024:(dc + 1) * 1024]
                        else:
                            rhs_full = qcs[:, dc * 1024:(dc + 1) * 1024]
                        lhs = wbuf[wsl][:, dc * FW + fc * 128: dc * FW + fc * 128 + 128]
                        w_ = mw if qh == 0 else mwh
                        op("tensor", "matmul", pdst[:, qh * 512:(qh + 1) * 512], lhs,
                           rhs_full[:, qh * 512:(qh + 1) * 512],
                           start=(dc == 0), stop=(dc == nd - 1),
                           waits=(w_ if dc == 0 else ()))
                tinc("tensor", s_t.bump())
                mmsnap = s_t.v
                bap = sb_bias[:, BOFF[bnm] + fcg: BOFF[bnm] + fcg + 1]
                ew = [(s_t.h, mmsnap)]
                ssl = None
                if sname == "kvc":
                    edst = kvc[:, fcg * 1024:(fcg + 1) * 1024]
                elif sname in ("qcA", "qcB"):
                    edst = qcs[:, fcg * 1024:(fcg + 1) * 1024]
                elif sname in ("krA", "krB", "qr"):
                    edst = krr[fcg % 2]
                    if st_["kr_free"][fcg % 2] is not None:
                        ew.append((s_v.h, st_["kr_free"][fcg % 2]))
                elif sname == "v":
                    edst = vfm[fcg % 4]
                    if st_["vfm_free"][fcg % 4] is not None:
                        ew.append((s_t.h, st_["vfm_free"][fcg % 4]))
                else:
                    ssl = st_["stgi"] % 3
                    st_["stgi"] += 1
                    edst = stg[ssl]
                    ew.append(s_sh[bt].cur())
                    ew.append(s_stg.cur())
                op("scalar", "activation", out=edst, in_=pdst, func=AF.Identity,
                   bias=bap, scale=1.0, waits=ew, inc=s_s.bump())
                st_["proj_evac"][par] = s_s.v
                evsnap = s_s.v
                if sname == "kvc" and fcg == 3:
                    st_["kvc_done"] = evsnap
                if sname == "qcB" and fcg == 11:
                    st_["qcs_done"] = evsnap
                if sname == "kcnt":
                    op("sync", "dma_start", out=shard[bt][fcg * 128:(fcg + 1) * 128, :],
                       in_=edst, waits=[(s_s.h, evsnap)], inc=s_sh[bt].bump(16))
                elif sname in ("qcntA", "qcntB"):
                    op("sync", "dma_start",
                       out=qscr[fcg * 128:(fcg + 1) * 128, bt * TPB:(bt + 1) * TPB],
                       in_=edst, waits=[(s_s.h, evsnap)], inc=s_stg.bump(16))
                elif sname == "v":
                    st_["vfm_pend"].append((fcg, evsnap))
                    if len(st_["vfm_pend"]) == 4:
                        emit_v_transposes(bt)
                elif sname in ("krA", "krB", "qr"):
                    emit_rope(bt, sname, fcg, evsnap)
            st_["wfree"][wsl] = s_t.v

    defer_cc1 = do_cc and ("p3" in phases) and p1_passes > 1
    cc1_w = [None]
    for bt in range(p1_passes):
        p1_pass(bt)
        if do_cc:
            ccw = [s_sh[bt].cur()]
            if bt == 1 and defer_cc1:
                cc1_w[0] = ccw
                continue
            op("gpsimd", "collective", kind="AllGather",
               cins=[shard_t[bt].ap()], couts=[gath_t[bt].ap()],
               waits=ccw, inc=s_cc[bt].bump(1))
        if dbg:
            op("gpsimd", "dma_start", out=dbgsh_t[bt].ap(), in_=shard[bt],
               waits=[s_sh[bt].cur()], inc=s_dbg.bump(16))
    QS_TOT = s_stg.cur()
    if dbg:
        wt("gpsimd", s_dbg.cur())
    if "p3" not in phases:
        wt("sync", QS_TOT, s_sh[0].cur(), s_sh[1].cur())
        wt("vector", (s_v.h, s_v.v))
        wt("scalar", (s_s.h, s_s.v))
        wt("tensor", (s_t.h, s_t.v))
        if do_cc:
            wt("gpsimd", (s_cc[0].h, 1) if p1_passes > 0 else (s_cc[0].h, 0))
            if p1_passes > 1:
                wt("gpsimd", (s_cc[1].h, 1))

    if "p3" not in phases:
        return _finish(nc, es, ops, ENGS, NCORE, drains)
    AB.reset(); AFA.reset()
    AB.mark(); AFA.mark()
    qbuf = [AB.alloc(6 * 1024) for _ in range(2)]
    kvb = [AB.alloc(10240) for _ in range(2)]
    ptil = [AB.alloc(1024) for _ in range(6)]
    onstg = [AB.alloc(1024) for _ in range(4)]
    lsb16s = [AB.alloc(1024, parts=1) for _ in range(4)]
    osbb = AFA.alloc(4096)
    lsbb = AFA.alloc(4096, parts=1)

    OO, LL = 2048, 3072
    P1T, P1V, P1S = s_t.cur(), s_v.cur(), s_s.cur()
    wt("sync", P1T, P1V, P1S)
    wt("gpsimd", P1T, P1V, P1S)

    a_ = dict(scpar=0, ptc=0, pt_cons=[None] * 6, sc_evac=[None] * 4,
              ol_free=None, bc_evac=None, kv_free=[None, None], q_free=[None, None],
              oni=0, sweep=0, kvctr=0, pend_onst=[], lp_snap=[None, None], lp_free=None, ptil_guard=None)

    def kv_load(bt, g, bi, slot):
        h0 = g * 4
        w = [(s_cc[bt].h, 1)] if do_cc else []
        if a_["kv_free"][slot] is not None:
            w.append((s_t.h, a_["kv_free"][slot]))
        gb = gath[bt].rearrange("(r f) c -> r f c", f=5120)
        for hh in range(4):
            h = h0 + hh
            src = gb[:, h * 128:(h + 1) * 128, bi * 128:(bi + 1) * 128]
            dst = kvb[slot][:, hh * 1024:(hh + 1) * 1024].rearrange("p (r t) -> p r t", r=8)
            if hh < 2:
                op("sync", "dma_start", out=dst, in_=src.rearrange("r f c -> f r c"),
                   waits=(w if hh == 0 else ()), inc=s_kv[slot].bump(16))
            else:
                op("scalar", "dma_start", out=dst, in_=src.rearrange("r f c -> f r c"),
                   waits=(w if hh == 2 else ()), inc=s_kvs[slot].bump(16))
        for pp in range(2):
            pj = g * 2 + pp
            src = gb[:, 2048 + pj * 128: 2048 + (pj + 1) * 128, bi * 128:(bi + 1) * 128]
            dst = kvb[slot][:, 4096 + pp * 1024: 4096 + (pp + 1) * 1024].rearrange(
                "p (r t) -> p r t", r=8)
            op("gpsimd", "dma_start", out=dst, in_=src.rearrange("r f c -> f r c"),
               waits=(w if pp == 0 else ()), inc=s_kvg[slot].bump(16))
        dcol0 = h0 * 128
        half = dcol0 // 1024
        col0 = dcol0 % 1024
        src = gb[:, 3072 + 256 * bi: 3072 + 256 * (bi + 1), :].rearrange(
            "r (m two) c -> r m two c", two=2)[:, :, half, col0:col0 + 512]
        dst = kvb[slot][:, 6144:6144 + 4096].rearrange("p (r d) -> p r d", r=8)
        op("gpsimd", "dma_start", out=dst, in_=src.rearrange("r m d -> m r d"),
           inc=s_kvg[slot].bump(16))

    def q_load(bt, g, slot):
        w = [QS_TOT]
        if a_["q_free"][slot] is not None:
            w.append((s_t.h, a_["q_free"][slot]))
        for hh in range(4):
            h = g * 4 + hh
            op("sync", "dma_start", out=qbuf[slot][:, hh * 1024:(hh + 1) * 1024],
               in_=qscr[h * 128:(h + 1) * 128, bt * TPB:(bt + 1) * TPB],
               waits=(w if hh == 0 else ()), inc=s_q[slot].bump(16))
        for pp in range(2):
            pj = g * 2 + pp
            op("sync", "dma_start", out=qbuf[slot][:, 4096 + pp * 1024: 4096 + (pp + 1) * 1024],
               in_=qscr[2048 + pj * 128: 2048 + (pj + 1) * 128, bt * TPB:(bt + 1) * TPB],
               inc=s_q[slot].bump(16))

    def flush_onst():
        for (h, bt2, osl, onsnap) in a_["pend_onst"]:
            op("sync", "dma_start",
               out=onorm[h * 128:(h + 1) * 128, bt2 * TPB:(bt2 + 1) * TPB],
               in_=onstg[osl][:, :], waits=[(s_v.h, onsnap)], inc=s_onst.bump(16))
        a_["pend_onst"] = []

    def attention_sweep(bt, g):
        si = a_["sweep"]
        a_["sweep"] += 1
        qsl = si % 2
        q_load(bt, g, qsl)
        qsem = s_q[qsl].cur()
        for bi in range(8):
            if si == 0 and bi == 4 and cc1_w[0] is not None:
                op("gpsimd", "collective", kind="AllGather",
                   cins=[shard_t[1].ap()], couts=[gath_t[1].ap()],
                   waits=cc1_w[0], inc=s_cc[1].bump(1))
                cc1_w[0] = None
            ksl = a_["kvctr"] % 2
            a_["kvctr"] += 1
            kv_load(bt, g, bi, ksl)
            if bi == 1:
                flush_onst()
            kvsem = s_kv[ksl].cur()
            kvgsem = s_kvg[ksl].cur()
            kvssem = s_kvs[ksl].cur()
            Nq = 1024 - 128 * bi
            nqt = (Nq + 511) // 512
            qoff = 128 * bi
            for pair in range(2):
                for qh in range(nqt):
                    wq = min(512, Nq - qh * 512)
                    q0 = qoff + qh * 512
                    pend = []

                    def emit_attnv(j, lh, psl, msnap, wq=wq, q0=q0, sem=None):
                        hh = pair * 2 + lh
                        aw = [((sem or s_v).h, msnap)]
                        if j == 0 and a_["ol_free"] is not None:
                            aw.append((s_v.h, a_["ol_free"]))
                        vlhs = kvb[ksl][:, 6144 + j * 512 + hh * 128:
                                        6144 + j * 512 + hh * 128 + 128]
                        op("tensor", "matmul",
                           PS[0:1, 3072 + lh * 512: 3072 + lh * 512 + wq],
                           sb_ones[:, 0:1], ptil[psl][:, 0:wq],
                           start=(j == 0), stop=(j == 7), waits=aw)
                        op("tensor", "matmul",
                           PS[:, 2048 + lh * 512: 2048 + lh * 512 + wq],
                           vlhs, ptil[psl][:, 0:wq],
                           start=(j == 0), stop=(j == 7))
                        tinc("tensor", s_t.bump())
                        a_["pt_cons"][psl] = s_t.v

                    for j in range(8):
                        par = j % 2
                        w = [kvsem, kvgsem, kvssem, qsem]
                        for lh in range(2):
                            if a_["sc_evac"][lh * 2 + par] is not None:
                                w.append((s_s.h, a_["sc_evac"][lh * 2 + par]))
                        if a_["bc_evac"] is not None:
                            w.append((s_v.h, a_["bc_evac"]))
                        for lh in range(2):
                            hh = pair * 2 + lh
                            lhs_n = kvb[ksl][:, hh * 1024 + j * 128: hh * 1024 + j * 128 + 128]
                            rq_n = qbuf[qsl][:, hh * 1024 + q0: hh * 1024 + q0 + wq]
                            op("tensor", "matmul",
                               PS[:, (lh * 2 + par) * 512: (lh * 2 + par) * 512 + wq],
                               lhs_n, rq_n, start=True, stop=False,
                               waits=(w if lh == 0 else ()))
                        for lh in range(2):
                            hh = pair * 2 + lh
                            half = hh % 2
                            lhs_r = kvb[ksl][64 * half:64 * half + 64,
                                             4096 + pair * 1024 + j * 128:
                                             4096 + pair * 1024 + j * 128 + 128]
                            rq_r = qbuf[qsl][64 * half:64 * half + 64,
                                             4096 + pair * 1024 + q0:
                                             4096 + pair * 1024 + q0 + wq]
                            op("tensor", "matmul",
                               PS[:, (lh * 2 + par) * 512: (lh * 2 + par) * 512 + wq],
                               lhs_r, rq_r, start=False, stop=True)
                        tinc("tensor", s_t.bump())
                        scsnap = s_t.v
                        for lh in range(2):
                            psl = a_["ptc"] % 6
                            a_["ptc"] += 1
                            ew = [(s_t.h, scsnap)]
                            if a_["pt_cons"][psl] is not None:
                                ew.append((s_t.h, a_["pt_cons"][psl]))
                            op("scalar", "activation", out=ptil[psl][:, 0:wq],
                               in_=PS[:, (lh * 2 + par) * 512: (lh * 2 + par) * 512 + wq],
                               func=AF.Exp, bias=sb_sm[:, 33:34], scale=SCALE,
                               waits=ew, inc=s_s.bump())
                            a_["sc_evac"][lh * 2 + par] = s_s.v
                            expsnap = s_s.v
                            if qh == 0:
                                op("vector", "tensor_tensor", out=ptil[psl][:, 0:128],
                                   in0=ptil[psl][:, 0:128],
                                   in1=sb_mask[:, j * 128:(j + 1) * 128],
                                   op=ALU.mult, waits=[(s_s.h, expsnap)], inc=s_v.bump(),
                                   nodrain=True)
                                msnap = s_v.v
                            else:
                                msnap = None
                            pend.append((j, lh, psl, msnap if msnap is not None else expsnap))
                            if msnap is None:
                                pass
                        while len(pend) > 4:
                            jj, lh2, psl2, sn2 = pend.pop(0)
                            emit_attnv(jj, lh2, psl2, sn2,
                                       sem=(s_v if qh == 0 else s_s))
                    while pend:
                        jj, lh2, psl2, sn2 = pend.pop(0)
                        emit_attnv(jj, lh2, psl2, sn2,
                                   sem=(s_v if qh == 0 else s_s))
                    avsnap = s_t.v
                    o_out = osbb[:, pair * 2048: pair * 2048 + 2048].rearrange(
                        "p (g c) -> p g c", g=2)[:, :, q0:q0 + wq]
                    o_in = PS[:, 2048:3072].rearrange("p (g c) -> p g c", g=2)[:, :, 0:wq]
                    l_out = lsbb[0:1, pair * 2048: pair * 2048 + 2048].rearrange(
                        "p (g c) -> p g c", g=2)[:, :, q0:q0 + wq]
                    l_in = PS[0:1, 3072:4096].rearrange("p (g c) -> p g c", g=2)[:, :, 0:wq]
                    if bi == 0:
                        op("vector", "tensor_copy", out=o_out, in_=o_in,
                           waits=[(s_t.h, avsnap)])
                        op("vector", "tensor_copy", out=l_out, in_=l_in,
                           inc=s_v.bump(), nodrain=True)
                    else:
                        op("vector", "tensor_tensor", out=o_out, in0=o_out, in1=o_in,
                           op=ALU.add, waits=[(s_t.h, avsnap)])
                        op("vector", "tensor_tensor", out=l_out, in0=l_out, in1=l_in,
                           op=ALU.add, inc=s_v.bump(), nodrain=True)
                    a_["ol_free"] = s_v.v
            a_["kv_free"][ksl] = s_t.v
        a_["q_free"][qsl] = s_t.v
        rsnaps = []
        for hh in range(4):
            lview = lsbb[0:1, hh * 1024:(hh + 1) * 1024]
            op("vector", "reciprocal", lview, lview, nodrain=(hh > 0))
            op("vector", "tensor_copy", out=lsb16s[hh][0:1, :], in_=lview,
               inc=s_v.bump())
            rsnaps.append(s_v.v)
        mult_snap = [None, None]
        for hh in range(4):
            h = g * 4 + hh
            pb_off = (hh % 2) * 1024
            bw = [(s_v.h, rsnaps[hh])]
            if mult_snap[hh % 2] is not None:
                bw.append((s_v.h, mult_snap[hh % 2]))
            elif a_["bc_evac"] is not None:
                bw.append((s_v.h, a_["bc_evac"]))
            for k4 in range(4):
                if a_["sc_evac"][k4] is not None:
                    bw.append((s_s.h, a_["sc_evac"][k4]))
            for qt in range(2):
                op("tensor", "matmul", PS[:, pb_off + qt * 512: pb_off + qt * 512 + 512],
                   sb_ones[0:1, :], lsb16s[hh][0:1, qt * 512: qt * 512 + 512],
                   start=True, stop=True, waits=(bw if qt == 0 else ()))
            tinc("tensor", s_t.bump())
            bsnap = s_t.v
            osl = a_["oni"] % 4
            a_["oni"] += 1
            ow = [(s_t.h, bsnap), s_onst.cur()]
            op("vector", "tensor_tensor", out=onstg[osl][:, :],
               in0=osbb[:, hh * 1024:(hh + 1) * 1024],
               in1=PS[:, pb_off: pb_off + 1024], op=ALU.mult, waits=ow, inc=s_v.bump())
            mult_snap[hh % 2] = s_v.v
            a_["bc_evac"] = s_v.v
            onsnap = s_v.v
            a_["pend_onst"].append((h, bt, osl, onsnap))

    for bt in range(2):
        for g in range(4):
            attention_sweep(bt, g)
    flush_onst()
    ON_TOT = s_onst.cur()
    if "p4" not in phases:
        wt("sync", ON_TOT)

    if "p4" not in phases:
        return _finish(nc, es, ops, ENGS, NCORE, drains)
    AB.reset(); AFA.reset()
    AB.mark(); AFA.mark()
    woT = AB.alloc(16 * 2048)
    onb = [AB.alloc(2048) for _ in range(2)]
    x2l = [AB.alloc(2048) for _ in range(2)]
    x2T = AB.alloc(16 * 256)
    aTb = AB.alloc(16 * E * R)
    bwTb = AB.alloc(E * D, parts=64)
    wgTb = AB.alloc(16 * E)
    ghb = AB.alloc(E * 256, parts=64)
    gbc = AB.alloc(256, parts=64)
    gtb = AB.alloc(256, parts=8)
    scr8 = AB.alloc(8)
    xf4 = [AFA.alloc(2048) for _ in range(2)]
    xnw = [AFA.alloc(2048) for _ in range(2)]

    P3T, P3V, P3S = s_t.cur(), s_v.cur(), s_s.cur()
    wt("sync", P3T, P3V, P3S)
    wt("gpsimd", P3T, P3V, P3S)

    op("gpsimd", "dma_start", out=woT[:, 0:8 * 2048].rearrange("p (c d) -> p c d", c=8),
       in_=woTr.rearrange("(c p) d -> p c d", p=128)[:, 0:8, :], inc=s_w[0].bump(16))
    P4WA = s_w[0].cur()
    op("gpsimd", "dma_start", out=woT[:, 8 * 2048:].rearrange("p (c d) -> p c d", c=8),
       in_=woTr.rearrange("(c p) d -> p c d", p=128)[:, 8:16, :], inc=s_w[0].bump(16))
    P4WB = s_w[0].cur()
    op("gpsimd", "dma_start", out=aTb, in_=aTh, inc=s_w[0].bump(16))
    op("gpsimd", "dma_start", out=bwTb, in_=bwTh, inc=s_w[0].bump(16))
    op("gpsimd", "dma_start", out=wgTb, in_=wgTh, inc=s_w[0].bump(16))
    P4W = s_w[0].cur()

    f_ = dict(on_free=[None, None], x_free=[None, None], tp_evac=[None, None],
              x2l_free=[None, None], x2T_cons=None, logit_free=None,
              gt_free=None, gbc_free=None, h_free=[None, None],
              py_free=[None, None], xn_store=[None, None], op_free=None,
              y67_free=None, xnw_snap=[None, None])

    def p4_loads(bt, gg):
        for ii, tt in enumerate((gg * 2, gg * 2 + 1)):
            r0 = bt * TPB + tt * 128
            osl = tt % 2
            xsl = tt % 2
            w = [ON_TOT]
            if f_["on_free"][osl] is not None:
                w.append((s_t.h, f_["on_free"][osl]))
            src2 = onorm.rearrange("(c p) t -> p c t", p=128)[:, :, r0: r0 + 128]
            onv = onb[osl].rearrange("p (c t) -> p c t", c=16)
            op("sync", "dma_start", out=onv[:, 0:8, :], in_=src2[:, 0:8, :],
               waits=w, inc=s_on[osl].bump(16))
            op("sync", "dma_start", out=onv[:, 8:16, :], in_=src2[:, 8:16, :],
               inc=s_on[osl].bump(16))
            xw = []
            if f_["x_free"][xsl] is not None:
                xw.append((s_v.h, f_["x_free"][xsl]))
            op("sync", "dma_start", out=xf4[xsl], in_=x_sl[r0: r0 + 128, :], waits=xw,
               inc=s_x[xsl].bump(16))

    def p4_opmm(bt, gg, ii):
        tt = gg * 2 + ii
        osl = tt % 2
        mw = [s_on[osl].cur(), P4WA]
        if f_["op_free"] is not None:
            mw.append((s_v.h, f_["op_free"]))
        for hc in range(16):
            for dt in range(4):
                w_ = ()
                if hc == 0 and dt == 0:
                    w_ = mw
                elif hc == 8 and dt == 0:
                    w_ = [P4WB]
                op("tensor", "matmul", PS[:, dt * 512: dt * 512 + 512],
                   onb[osl][:, hc * 128:(hc + 1) * 128],
                   woT[:, hc * 2048 + dt * 512: hc * 2048 + dt * 512 + 512],
                   start=(hc == 0), stop=False,
                   waits=w_)
        for dt in range(4):
            op("tensor", "matmul", PS[:, dt * 512: dt * 512 + 512],
               sb_ones[0:1, 0:128], sb_bo16[0:1, dt * 512:(dt + 1) * 512],
               start=False, stop=True)
        tinc("tensor", s_t.bump())
        f_["on_free"][osl] = s_t.v
        f_["opmm_snap_%d" % ii] = s_t.v

    def p4_post_add(bt, gg, ii):
        tt = gg * 2 + ii
        xsl = tt % 2
        mmsnap = f_["opmm_snap_%d" % ii]
        if f_["xn_store"][ii] is not None:
            wt("vector", (s_out[ii].h, f_["xn_store"][ii]))
        vw = [(s_t.h, mmsnap), s_x[xsl].cur()]
        for dt in range(4):
            last = dt == 3
            op("vector", "tensor_tensor", out=xnw[ii][:, dt * 512:(dt + 1) * 512],
               in0=xf4[xsl][:, dt * 512:(dt + 1) * 512], in1=PS[:, dt * 512: dt * 512 + 512],
               op=ALU.add, waits=(vw if dt == 0 else ()),
               inc=(s_v.bump() if last else None), nodrain=(dt > 0))
        f_["x_free"][xsl] = s_v.v
        f_["xnw_snap"][ii] = s_v.v
        f_["op_free"] = s_v.v

    def p4_post_ln(bt, gg, ii):
        if f_.get("p4_ln_s") is not None:
            wt("vector", (s_s.h, f_["p4_ln_s"]))
        for k in range(4):
            op("vector", "bn_stats", bnst[:, 6 * k:6 * k + 6],
               xnw[ii][:, 512 * k:512 * (k + 1)], nodrain=(k > 0))
        op("vector", "bn_aggr", mv[:, :], bnst[:, :], inc=s_v.bump())
        wt("scalar", (s_v.h, s_v.v))
        op("scalar", "activation", out=rstd[:, :], in_=mv[:, 1:2], func=AF.Ln,
           bias=sb_sm[:, 32:33], scale=1.0)
        op("scalar", "activation", out=rstd[:, :], in_=rstd[:, :], func=AF.Exp, bias=0.0,
           scale=-0.5)
        op("scalar", "activation", out=negr[:, :], in_=rstd[:, :], func=AF.Identity,
           bias=sb_sm[:, 34:35], scale=-1.0)
        op("scalar", "activation", out=nmr[:, :], in_=mv[:, 0:1], func=AF.Identity,
           bias=sb_sm[:, 34:35], scale=negr[:, 0:1])
        x2w = []
        if f_["x2l_free"][ii] is not None:
            x2w.append((s_t.h, f_["x2l_free"][ii]))
        op("scalar", "activation", out=x2l[ii][:, :], in_=xnw[ii], func=AF.Identity,
           bias=nmr[:, 0:1], scale=rstd[:, 0:1], waits=x2w, inc=s_s.bump())
        x2snap = s_s.v
        f_["p4_ln_s"] = s_s.v
        for g4 in range(4):
            par = g4 % 2
            pb = PS[:, 2048 + par * 512: 2048 + par * 512 + 512].bitcast(BF16)
            tw = [(s_s.h, x2snap)]
            if f_["tp_evac"][par] is not None:
                tw.append((s_s.h, f_["tp_evac"][par]))
            if f_["h_free"][par] is not None:
                tw.append((s_v.h, f_["h_free"][par]))
            for j in range(4):
                dc = g4 * 4 + j
                op("tensor", "transpose", pb[:, j * 128:(j + 1) * 128],
                   x2l[ii][:, dc * 128:(dc + 1) * 128], sb_id[:, :],
                   waits=(tw if j == 0 else ()))
            tinc("tensor", s_t.bump())
            tsn = s_t.v
            dview = x2T.rearrange("p (c t) -> p c t", c=16)[:, g4 * 4: g4 * 4 + 4,
                                                            ii * 128:(ii + 1) * 128]
            ew = [(s_t.h, tsn)]
            if ii == 0 and g4 == 0 and f_["x2T_cons"] is not None:
                ew.append((s_t.h, f_["x2T_cons"]))
            op("scalar", "copy", out=dview,
               in_=pb[:, 0:512].rearrange("p (a b) -> p a b", a=4),
               waits=ew, inc=s_s.bump())
            f_["tp_evac"][par] = s_s.v
        f_["x2l_free"][ii] = s_t.v

    def p4_ffn(bt, gg, next_cb=None):
        tts = (gg * 2, gg * 2 + 1)
        x2T_snap = s_s.cur()
        for ii in range(2):
            lw = [x2T_snap, P4W]
            if f_["logit_free"] is not None:
                lw.append((s_v.h, f_["logit_free"]))
            if f_["y67_free"] is not None:
                lw.append((s_v.h, f_["y67_free"]))
            for dc in range(16):
                op("tensor", "matmul", PS[:, 3072:3080],
                   x2T.rearrange("p (c t) -> p c t", c=16)[:, dc, ii * 128:(ii + 1) * 128],
                   wgTb.rearrange("p (c e) -> p c e", c=16)[:, dc, :],
                   start=(dc == 0), stop=(dc == 15), waits=(lw if dc == 0 else ()))
            tinc("tensor", s_t.bump())
            lsnap = s_t.v
            op("vector", "tensor_reduce", sb_sm[:, 28:29], PS[:, 3072:3080],
               mybir.AxisListType.X, ALU.max, waits=[(s_t.h, lsnap)])
            op("vector", "tensor_scalar", out=sb_sm[:, 29:30], in0=sb_sm[:, 28:29],
               scalar1=-1.0, scalar2=None, op0=ALU.mult, inc=s_v.bump())
            wt("scalar", (s_v.h, s_v.v))
            op("scalar", "activation", out=scr8[:, :], in_=PS[:, 3072:3080], func=AF.Exp,
               bias=sb_sm[:, 29:30], scale=1.0, accum_out=sb_sm[:, 30:31], inc=s_s.bump())
            wt("vector", (s_s.h, s_s.v))
            op("vector", "reciprocal", sb_sm[:, 31:32], sb_sm[:, 30:31])
            op("vector", "tensor_scalar", out=scr8[:, :], in0=PS[:, 3072:3080],
               scalar1=sb_sm[:, 29:30], scalar2=0.0, op0=ALU.add, op1=ALU.is_equal)
            op("vector", "tensor_scalar", out=scr8[:, :], in0=scr8[:, :],
               scalar1=sb_sm[:, 31:32], scalar2=MOE_SCALING, op0=ALU.mult, op1=ALU.mult,
               inc=s_v.bump())
            f_["logit_free"] = s_v.v
            gsnap = s_v.v
            pgt = PS[:, 3584:3648].bitcast(BF16)
            gw = [(s_v.h, gsnap)]
            if f_["gt_free"] is not None:
                gw.append((s_v.h, f_["gt_free"]))
            if f_["y67_free"] is not None:
                gw.append((s_v.h, f_["y67_free"]))
            op("tensor", "transpose", pgt[0:8, 0:128], scr8[:, :], sb_id[:, :], waits=gw)
            tinc("tensor", s_t.bump())
            gtsn = s_t.v
            op("vector", "tensor_copy", out=gtb[0:8, ii * 128:(ii + 1) * 128],
               in_=pgt[0:8, 0:128], waits=[(s_t.h, gtsn)], inc=s_v.bump())
            f_["gt_free"] = s_v.v
        gT_snap = s_v.cur()

        if next_cb is not None:
            next_cb()

        for e in range(E):
            par = e % 2
            bcw = [gT_snap]
            if f_["gbc_free"] is not None:
                bcw.append((s_v.h, f_["gbc_free"]))
            if f_["y67_free"] is not None:
                bcw.append((s_v.h, f_["y67_free"]))
            op("tensor", "matmul", PS[0:64, 3584:3840], sb_oh[:, e * 64:(e + 1) * 64],
               gtb[0:8, 0:256], start=True, stop=True, waits=bcw)
            tinc("tensor", s_t.bump())
            bcsnap = s_t.v
            op("vector", "tensor_copy", out=gbc[0:64, :], in_=PS[0:64, 3584:3840],
               waits=[(s_t.h, bcsnap)], inc=s_v.bump())
            gbcsnap = s_v.v
            f_["gbc_free"] = gbcsnap
            hw = [(s_v.h, gbcsnap), P4W]
            if f_["h_free"][par] is not None:
                hw.append((s_v.h, f_["h_free"][par]))
            for dc in range(16):
                op("tensor", "matmul", PS[0:64, 2048 + par * 512: 2048 + par * 512 + 256],
                   aTb.rearrange("p (c e r) -> p c e r", c=16, e=E)[:, dc, e, :],
                   x2T.rearrange("p (c t) -> p c t", c=16)[:, dc, :],
                   start=(dc == 0), stop=(dc == 15), waits=(hw if dc == 0 else ()))
            tinc("tensor", s_t.bump())
            hsnap = s_t.v
            op("vector", "tensor_tensor", out=ghb[0:64, e * 256:(e + 1) * 256],
               in0=PS[0:64, 2048 + par * 512: 2048 + par * 512 + 256], in1=gbc[0:64, :],
               op=ALU.mult, waits=[(s_t.h, hsnap)], inc=s_v.bump())
            f_["h_free"][par] = s_v.v
        gh_snap = s_v.cur()
        f_["x2T_cons"] = s_t.v

        ycnt = 0
        for ii in range(2):
            tt = tts[ii]
            r0 = bt * TPB + tt * 128
            for dt in range(4):
                yb = 3072 + (ycnt % 2) * 512
                ycnt += 1
                yw = [gh_snap, P4W]
                if f_["py_free"][(ycnt - 1) % 2] is not None:
                    yw.append((s_v.h, f_["py_free"][(ycnt - 1) % 2]))
                for e in range(E):
                    op("tensor", "matmul", PS[:, yb: yb + 512],
                       ghb[0:64, e * 256 + ii * 128: e * 256 + ii * 128 + 128],
                       bwTb[0:64, e * 2048 + dt * 512: e * 2048 + dt * 512 + 512],
                       start=(e == 0), stop=(e == E - 1), waits=(yw if e == 0 else ()))
                tinc("tensor", s_t.bump())
                ysnap = s_t.v
                op("vector", "tensor_tensor", out=xnw[ii][:, dt * 512:(dt + 1) * 512],
                   in0=xnw[ii][:, dt * 512:(dt + 1) * 512],
                   in1=PS[:, yb: yb + 512],
                   op=ALU.add, waits=[(s_t.h, ysnap)], inc=s_v.bump(), nodrain=(dt > 0))
                f_["py_free"][(ycnt - 1) % 2] = s_v.v
                f_["y67_free"] = s_v.v
            donesnap = s_v.v
            op("sync", "dma_start", out=out_sl[r0: r0 + 128, :], in_=xnw[ii][:, :],
               waits=[(s_v.h, donesnap)], inc=s_out[ii].bump(16))
            f_["xn_store"][ii] = s_out[ii].v

    groups = [(bt, gg) for bt in range(2) for gg in range(4)]
    p4_loads(*groups[0])
    p4_opmm(*groups[0], 0)
    p4_post_add(*groups[0], 0)
    p4_opmm(*groups[0], 1)
    p4_post_ln(*groups[0], 0)
    p4_post_add(*groups[0], 1)
    p4_post_ln(*groups[0], 1)
    for gi, (bt, gg) in enumerate(groups):
        nxt = groups[gi + 1] if gi + 1 < len(groups) else None
        if nxt is not None:
            p4_loads(*nxt)

            def cb(nxt=nxt):
                p4_opmm(*nxt, 0)
            p4_ffn(bt, gg, next_cb=cb)
            p4_post_add(*nxt, 0)
            p4_opmm(*nxt, 1)
            p4_post_ln(*nxt, 0)
            p4_post_add(*nxt, 1)
            p4_post_ln(*nxt, 1)
        else:
            p4_ffn(bt, gg)

    wt("sync", s_out[0].cur(), s_out[1].cur())
    return _finish(nc, es, ops, ENGS, NCORE, drains)


def _finish(nc, es, ops, engs, ncore, drains=False):
    block = es.enter_context(nc.Block())

    def runner(lst, auto_drain=False):
        def go(e):
            waited = {}
            for meth, args, kw, waits, inc in lst:
                for (h, v) in waits:
                    k = id(h)
                    if waited.get(k, -1) >= v:
                        continue
                    e.wait_ge(h, v)
                    waited[k] = v
                if meth == "_wait":
                    continue
                if meth == "_drain":
                    e.drain()
                    continue
                if meth.startswith("!"):
                    meth = meth[1:]
                elif auto_drain and meth not in ("_nop", "dma_start"):
                    e.drain()
                if meth == "_nop":
                    ins = e.nop()
                elif meth == "collective":
                    ins = e.collective_compute(
                        kw["kind"], ALU.bypass,
                        replica_groups=[list(range(ncore))],
                        ins=[a.opt() for a in kw["cins"]],
                        outs=[a.opt() for a in kw["couts"]],
                    )
                    if inc is not None:
                        ins.then_inc(inc[0])
                    continue
                else:
                    ins = getattr(e, meth)(*args, **kw)
                if inc is not None:
                    ins.then_inc(inc[0], inc[2])
        return go

    @block.tensor
    def _(e):
        runner(ops["tensor"])(e)

    @block.vector
    def _(e):
        runner(ops["vector"], auto_drain=drains)(e)

    @block.scalar
    def _(e):
        runner(ops["scalar"], auto_drain=True)(e)

    @block.sync
    def _(e):
        runner(ops["sync"])(e)

    @block.gpsimd
    def _(e):
        runner(ops["gpsimd"])(e)

    es.close()
    return nc



def _prep_inputs(kw):
    f32 = np.float32
    bf = ml_dtypes.bfloat16
    x = np.asarray(kw["L0_attn_x"], f32)

    def T(a):
        return np.ascontiguousarray(np.asarray(a, f32).T)

    wkdT = T(kw["L0_attn_Wkd"])
    wqdT = T(kw["L0_attn_Wqd"])
    wkuT = T(kw["L0_attn_Wku"])
    wvuT = T(kw["L0_attn_Wvu"])
    wquT = T(kw["L0_attn_Wqu"])
    wkrT = np.ascontiguousarray(np.asarray(kw["L0_attn_Wkr"], f32)[ROPE_ORDER, :].T)
    wqrT = np.ascontiguousarray(np.asarray(kw["L0_attn_Wqr"], f32)[ROPE_ORDER, :].T)
    woTr = T(kw["L0_attn_Wo"])
    A = np.asarray(kw["L0_ffn_A"], f32)
    Bw = np.asarray(kw["L0_ffn_Bw"], f32)
    Wg = np.asarray(kw["L0_ffn_Wg"], f32)

    aTh = np.ascontiguousarray(
        A.transpose(2, 0, 1).reshape(16, 128, E, R).transpose(1, 0, 2, 3).reshape(128, 16 * E * R))
    bwTh = np.ascontiguousarray(Bw.transpose(2, 0, 1).reshape(64, E * D))
    wgTh = np.ascontiguousarray(
        Wg.T.reshape(16, 128, E).transpose(1, 0, 2).reshape(128, 16 * E))

    def b2(v, nf):
        return np.ascontiguousarray(np.asarray(v, f32).reshape(nf, 128).T)

    bkd2 = b2(kw["L0_attn_bkd"], 4)
    bqd2 = b2(kw["L0_attn_bqd"], 12)
    bku2 = b2(kw["L0_attn_bku"], 16)
    bvu2 = b2(kw["L0_attn_bvu"], 16)
    bqu2 = b2(kw["L0_attn_bqu"], 16)
    bkr2 = b2(np.asarray(kw["L0_attn_bkr"], f32)[ROPE_ORDER], 8)
    bqr2 = b2(np.asarray(kw["L0_attn_bqr"], f32)[ROPE_ORDER], 8)
    bo2 = np.ascontiguousarray(np.asarray(kw["L0_attn_bo"], f32).reshape(1, D))

    ident = np.eye(128, dtype=bf)
    ones = np.ones((128, 128), dtype=bf)
    onehot = np.zeros((8, E * 64), dtype=bf)
    for e in range(E):
        onehot[e, e * 64:(e + 1) * 64] = 1.0

    inv_freq = (1.0 / (10000.0 ** (np.arange(0, DR, 2, dtype=np.float64) / DR))).astype(f32)

    shared = dict(wkdT=wkdT, wqdT=wqdT, wkuT=wkuT, wvuT=wvuT, wquT=wquT,
                  wkrT=wkrT, wqrT=wqrT, woTr=woTr, aTh=aTh, bwTh=bwTh, wgTh=wgTh,
                  bkd2=bkd2, bqd2=bqd2, bku2=bku2, bvu2=bvu2, bqu2=bqu2,
                  bkr2=bkr2, bqr2=bqr2, bo2=bo2, identt=ident, onest=ones,
                  onehot=onehot)
    in_maps = []
    for c in range(NCORE):
        pos = (8 * np.arange(TPB, dtype=f32) + c).astype(f32)
        ang = pos[None, :] * inv_freq[:, None]
        cos64 = np.cos(ang).astype(f32)
        sin64 = np.sin(ang).astype(f32)
        cosp = np.empty((64, TPB), f32)
        sinp = np.empty((64, TPB), f32)
        for j, d in enumerate(PERM64):
            cosp[j] = cos64[d % 32]
            sinp[j] = sin64[d % 32] * (-1.0 if d < 32 else 1.0)
        costab = np.ascontiguousarray(np.concatenate([cosp, cosp], 0))
        sintb = np.ascontiguousarray(np.concatenate([sinp, sinp], 0))
        m = np.arange(128)
        jj = np.arange(8)
        mq = np.arange(128)
        mk = (8 * m[:, None, None] + jj[None, :, None] <= 8 * mq[None, None, :] + c)
        masktab = np.ascontiguousarray(mk.astype(bf).reshape(128, 8 * 128))
        x_c = np.ascontiguousarray(x[:, c::8, :].reshape(2 * TPB, D))
        im = dict(shared)
        im.update(x_sl=x_c, cost=costab, sint=sintb, maskt=masktab)
        in_maps.append(im)
    return in_maps


_NC_CACHE = {}


def kernel(**kw):
    in_maps = _prep_inputs(kw)
    if "nc" not in _NC_CACHE:
        _NC_CACHE["nc"] = build_nc(drains=True)
    nc = _NC_CACHE["nc"]
    res = run_bass_kernel_spmd(nc, in_maps, core_ids=list(range(NCORE)))
    out = np.empty((B, S, D), np.float32)
    for c in range(NCORE):
        o = np.asarray(res.results[c]["out_sl"], np.float32).reshape(B, TPB, D)
        out[:, c::8, :] = o
    return out


# version 24
# speedup vs baseline: 1.0612x; 1.0137x over previous
import math
import numpy as np
import ml_dtypes
import concourse.bass as bass
import concourse.mybir as mybir
from concourse.bass_utils import run_bass_kernel_spmd
from contextlib import ExitStack

F32 = mybir.dt.float32
BF16 = mybir.dt.bfloat16
AF = mybir.ActivationFunctionType
ALU = mybir.AluOpType

B, S, D = 2, 8192, 2048
H, DH, DR = 16, 128, 64
DCKV, DCQ = 512, 1536
E, R = 8, 64
NCORE = 8
TPB = S // NCORE
SCALE = 1.0 / math.sqrt(DH + DR)
EXPB = 12.0
EPS = 1e-5
MOE_SCALING = 4.0

PERM64 = list(range(0, 16)) + list(range(32, 48)) + list(range(16, 32)) + list(range(48, 64))
ROPE_ORDER = [h * 64 + j for h in range(H) for j in PERM64]

ENGS = ("tensor", "vector", "scalar", "sync", "gpsimd")
LN_CUT = None


class Sem:
    def __init__(self, nc, es, name):
        self.h = es.enter_context(nc.semaphore(name))
        self.v = 0

    def bump(self, n=1):
        self.v += n
        return (self.h, self.v, n)

    def cur(self):
        return (self.h, self.v)


class Arena:
    def __init__(self, t, cols):
        self.t, self.cols, self.off, self.stack = t, cols, 0, []

    def alloc(self, n, parts=128):
        assert self.off + n <= self.cols, ("arena overflow", self.off, n, self.cols)
        sl = self.t[0:parts, self.off:self.off + n]
        self.off += n
        return sl

    def mark(self):
        self.stack.append(self.off)

    def reset(self):
        self.off = self.stack.pop()


def build_nc(dbg=False, phases=("p1", "p3", "p4"), p1_subs=None, p1_passes=2, do_cc=True, drains=False):
    nc = bass.Bass("TRN2", target_bir_lowering=False, debug=False)

    def din(name, shape, dt=F32):
        return nc.dram_tensor(name, shape, dt, kind="ExternalInput").ap()

    x_sl = din("x_sl", [2 * TPB, D])
    wkdT = din("wkdT", [D, DCKV])
    wqdT = din("wqdT", [D, DCQ])
    wkuT = din("wkuT", [DCKV, H * DH])
    wvuT = din("wvuT", [DCKV, H * DH])
    wquT = din("wquT", [DCQ, H * DH])
    wkrT = din("wkrT", [D, H * DR])
    wqrT = din("wqrT", [DCQ, H * DR])
    woTr = din("woTr", [H * DH, D])
    aTh = din("aTh", [128, 16 * E * R])
    bwTh = din("bwTh", [64, E * D])
    wgTh = din("wgTh", [128, 16 * E])
    bkd2 = din("bkd2", [128, 4])
    bqd2 = din("bqd2", [128, 12])
    bku2 = din("bku2", [128, 16])
    bvu2 = din("bvu2", [128, 16])
    bqu2 = din("bqu2", [128, 16])
    bkr2 = din("bkr2", [128, 8])
    bqr2 = din("bqr2", [128, 8])
    bo2 = din("bo2", [1, D])
    cost = din("cost", [128, TPB])
    sintab = din("sint", [128, TPB])
    masktab = din("maskt", [128, 8 * 128], BF16)
    identt = din("identt", [128, 128], BF16)
    onest = din("onest", [128, 128], BF16)
    onehot = din("onehot", [8, E * 64], BF16)

    out_sl = nc.dram_tensor("out_sl", [2 * TPB, D], F32, kind="ExternalOutput").ap()

    kscr = "ExternalOutput" if dbg else "Internal"
    def dscr(name, shape):
        if kscr == "Internal":
            return nc.dram_tensor(name, shape, BF16)
        return nc.dram_tensor(name, shape, BF16, kind=kscr)

    qscr_t = dscr("qscr", [3 * 1024, 2 * TPB])
    shard_t = [nc.dram_tensor("shard%d" % b, [5 * 1024, TPB], BF16) for b in range(2)]
    dbgsh_t = None
    if dbg:
        dbgsh_t = [nc.dram_tensor("dbgsh%d" % b, [5 * 1024, TPB], BF16, kind="ExternalOutput")
                   for b in range(2)]
    gath_t = [nc.dram_tensor("gath%d" % b, [NCORE * 5 * 1024, TPB], BF16) for b in range(2)]
    onorm_t = dscr("onormscr", [H * DH, 2 * TPB])
    wscr_t = nc.dram_tensor("wscr", [128, 110592], BF16)
    wscr = wscr_t.ap()
    qscr = qscr_t.ap()
    shard = [t.ap() for t in shard_t]
    gath = [t.ap() for t in gath_t]
    onorm = onorm_t.ap()

    es = ExitStack()
    AB = Arena(es.enter_context(nc.sbuf_tensor("ab16", [128, 73728], BF16)), 73728)
    AFA = Arena(es.enter_context(nc.sbuf_tensor("af32", [128, 8704], F32)), 8704)
    PS = es.enter_context(nc.psum_tensor("ps", [128, 4096], F32))

    sb_id = es.enter_context(nc.sbuf_tensor("sb_id", [128, 128], BF16))
    sb_ones = es.enter_context(nc.sbuf_tensor("sb_ones", [128, 128], BF16))
    sb_mask = es.enter_context(nc.sbuf_tensor("sb_mask", [128, 1024], BF16))
    sb_cos = es.enter_context(nc.sbuf_tensor("sb_cos", [128, TPB], F32))
    sb_sin = es.enter_context(nc.sbuf_tensor("sb_sin", [128, TPB], F32))
    sb_bias = es.enter_context(nc.sbuf_tensor("sb_bias", [128, 80], F32))
    sb_bo = es.enter_context(nc.sbuf_tensor("sb_bo", [1, D], F32))
    sb_bo16 = es.enter_context(nc.sbuf_tensor("sb_bo16", [1, D], BF16))
    sb_oh = es.enter_context(nc.sbuf_tensor("sb_oh", [8, E * 64], BF16))
    sb_sm = es.enter_context(nc.sbuf_tensor("sb_sm", [128, 64], F32))

    BOFF = {}
    _o = 0
    for nm, wdt in [("bkd", 4), ("bqd", 12), ("bku", 16), ("bvu", 16), ("bqu", 16),
                    ("bkr", 8), ("bqr", 8)]:
        BOFF[nm] = _o
        _o += wdt

    sems = {}
    def mksem(name):
        s = Sem(nc, es, name)
        sems[name] = s
        return s
    s_cst = mksem("cst")
    s_x = [mksem("x0"), mksem("x1"), mksem("x2")]
    s_w = [mksem("w0"), mksem("w1")]
    s_kv = [mksem("kv0"), mksem("kv1")]
    s_kvg = [mksem("kvg0"), mksem("kvg1")]
    s_kvs = [mksem("kvs0"), mksem("kvs1")]
    s_wst = mksem("wst")
    s_w2 = [mksem("w20"), mksem("w21")]
    s_dbg = mksem("dbg")
    s_q = [mksem("q0"), mksem("q1")]
    s_on = [mksem("on0"), mksem("on1")]
    s_stg = mksem("stg")
    s_sh = [mksem("sh0"), mksem("sh1")]
    s_onst = mksem("onst")
    s_out = [mksem("outst0"), mksem("outst1")]
    s_cc = [mksem("cc0"), mksem("cc1")]
    s_t = mksem("t")
    s_v = mksem("v")
    s_s = mksem("s")

    ops = {e: [] for e in ENGS}

    def op(eng, meth, *args, waits=(), inc=None, nodrain=False, **kw):
        ops[eng].append((meth if not nodrain else "!" + meth, args, kw, tuple(waits), inc))

    def wt(eng, *waits):
        ops[eng].append(("_wait", (), {}, tuple(waits), None))

    def tinc(eng, bump):
        meth, args, kw, waits, inc = ops[eng][-1]
        assert inc is None and meth not in ("_wait",), (meth, inc)
        ops[eng][-1] = (meth, args, kw, waits, bump)

    for dst, src in [
        (sb_id[:, :], identt), (sb_ones[:, :], onest), (sb_mask[:, :], masktab),
        (sb_cos[:, :], cost), (sb_sin[:, :], sintab), (sb_bo[:, :], bo2),
        (sb_oh[:, :], onehot),
        (sb_bias[:, BOFF["bkd"]:BOFF["bkd"] + 4], bkd2),
        (sb_bias[:, BOFF["bqd"]:BOFF["bqd"] + 12], bqd2),
        (sb_bias[:, BOFF["bku"]:BOFF["bku"] + 16], bku2),
        (sb_bias[:, BOFF["bvu"]:BOFF["bvu"] + 16], bvu2),
        (sb_bias[:, BOFF["bqu"]:BOFF["bqu"] + 16], bqu2),
        (sb_bias[:, BOFF["bkr"]:BOFF["bkr"] + 8], bkr2),
        (sb_bias[:, BOFF["bqr"]:BOFF["bqr"] + 8], bqr2),
    ]:
        op("sync", "dma_start", out=dst, in_=src, inc=s_cst.bump(16))
    CST = s_cst.cur()
    for e in ("tensor", "scalar", "gpsimd", "vector"):
        wt(e, CST)
    op("vector", "memset", sb_sm[:, 32:33], EPS)
    op("vector", "memset", sb_sm[:, 33:34], -EXPB)
    op("vector", "memset", sb_sm[:, 34:35], 0.0)
    op("vector", "tensor_copy", out=sb_bo16[:, :], in_=sb_bo[:, :], inc=s_v.bump())

    AB.mark(); AFA.mark()
    xln = [AB.alloc(2048) for _ in range(2)]
    xlnT = AB.alloc(16 * 1024)
    kvc = AB.alloc(4 * 1024)
    qcs = AB.alloc(12 * 1024)
    stg = [AB.alloc(1024) for _ in range(3)]
    vfm = [AB.alloc(1024) for _ in range(4)]
    krr = [AB.alloc(1024) for _ in range(2)]
    krs = AB.alloc(1024)
    rop = [AB.alloc(1024) for _ in range(2)]
    wbuf = [AB.alloc(12288) for _ in range(2)]
    xf = [AFA.alloc(2048) for _ in range(3)]
    rtmp = [AFA.alloc(1024) for _ in range(2)]
    bnst = sb_sm[:, 0:24]
    mv = sb_sm[:, 24:26]
    rstd = sb_sm[:, 26:27]
    nmr = sb_sm[:, 27:28]
    negr = sb_sm[:, 35:36]

    def xlnT_dc(dc):
        return xlnT[:, dc * 1024:(dc + 1) * 1024]

    st_ = dict(
        x_snap=[None, None, None],
        xln_free=[None, None],
        tp_evac=[None, None],
        proj_evac=[None, None],
        kvc_done=None, qcs_done=None, xlnT_done=None,
        wfree=[None, None],
        wtog=0, pjpar=0, stgi=0,
        vfm_pend=[], vfm_free=[None] * 4,
        vt_par=0, vt_evac=[None, None], wscr_off=0, wscr_map={}, wscr_stc={},
        wst_slot=[None, None], ln_s_snap=None, xlnT_half0=0,
        kr_free=[None, None],
        ropi=0, rop_store_sem=[None, None],
    )

    def emit_v_transposes(bt):
        chunks = list(st_["vfm_pend"])
        st_["vfm_pend"].clear()
        fbase = chunks[0][0]
        lastsnap = chunks[-1][1]
        dcol0 = fbase * 128
        half = dcol0 // 1024
        col0 = dcol0 % 1024
        for tc in range(8):
            par = st_["vt_par"] % 2
            st_["vt_par"] += 1
            pb = PS[:, 2048 + par * 512: 2048 + par * 512 + 512].bitcast(BF16)
            tw = [(s_s.h, lastsnap)]
            if st_["vt_evac"][par] is not None:
                tw.append((s_s.h, st_["vt_evac"][par][1]))
            if st_["tp_evac"][par] is not None:
                tw.append((s_s.h, st_["tp_evac"][par]))
            for j, (fcg, _) in enumerate(chunks):
                op("tensor", "transpose", pb[:, j * 128:(j + 1) * 128],
                   vfm[fcg % 4][:, tc * 128:(tc + 1) * 128], sb_id[:, :],
                   waits=(tw if j == 0 else ()))
            tinc("tensor", s_t.bump())
            tsn = s_t.v
            if tc == 7:
                for fcg, _ in chunks:
                    st_["vfm_free"][fcg % 4] = tsn
            ssl = st_["stgi"] % 3
            st_["stgi"] += 1
            sdst = stg[ssl][:, 0:512]
            ew = [(s_t.h, tsn), s_sh[bt].cur(), s_stg.cur()]
            op("scalar", "copy", out=sdst, in_=pb[:, 0:512], waits=ew, inc=s_s.bump(),
               nodrain=True)
            st_["vt_evac"][par] = ("s", s_s.v)
            evs = s_s.v
            dview = shard[bt][3072 + 256 * tc: 3072 + 256 * (tc + 1), :].rearrange(
                "(m two) c -> m two c", two=2)[:, half, col0:col0 + 512]
            op("sync", "dma_start", out=dview, in_=sdst, waits=[(s_s.h, evs)],
               inc=s_sh[bt].bump(16))

    def emit_rope(bt, sname, fcg, evsnap):
        sl = fcg % 2
        rsl = st_["ropi"] % 2
        st_["ropi"] += 1
        shmask = [(i + 16) % 32 for i in range(32)]
        op("vector", "stream_shuffle", krs[:, :], krr[sl][:, :], shmask,
           waits=[(s_s.h, evsnap)])
        op("vector", "tensor_tensor", out=rtmp[0], in0=krr[sl][:, :], in1=sb_cos[:, :], op=ALU.mult,
           nodrain=True)
        op("vector", "tensor_tensor", out=rtmp[1], in0=krs[:, :], in1=sb_sin[:, :], op=ALU.mult)
        rw = []
        prev = st_["rop_store_sem"][rsl]
        if prev is not None:
            rw.append(prev.cur())
        op("vector", "tensor_tensor", out=rop[rsl][:, :], in0=rtmp[0], in1=rtmp[1], op=ALU.add,
           waits=rw, inc=s_v.bump())
        st_["kr_free"][sl] = s_v.v
        rsnap = s_v.v
        if sname in ("krA", "krB"):
            dview = shard[bt][2048 + fcg * 128: 2048 + (fcg + 1) * 128, :]
            sem = s_sh[bt]
        else:
            dview = qscr[2048 + fcg * 128: 2048 + (fcg + 1) * 128, bt * TPB:(bt + 1) * TPB]
            sem = s_stg
        st_["rop_store_sem"][rsl] = sem
        op("sync", "dma_start", out=dview, in_=rop[rsl][:, :], waits=[(s_v.h, rsnap)],
           inc=sem.bump(16))

    SUBS = [
        ("kvc", wkdT, 16, 0, 512, "xlnT", "bkd"),
        ("qcA", wqdT, 16, 0, 768, "xlnT", "bqd"),
        ("qcB", wqdT, 16, 768, 768, "xlnT", "bqd"),
        ("kcnt", wkuT, 4, 0, 2048, "kvc", "bku"),
        ("v", wvuT, 4, 0, 2048, "kvc", "bvu"),
        ("qcntA", wquT, 12, 0, 1024, "qcs", "bqu"),
        ("qcntB", wquT, 12, 1024, 1024, "qcs", "bqu"),
        ("krA", wkrT, 16, 0, 512, "xlnT", "bkr"),
        ("krB", wkrT, 16, 512, 512, "xlnT", "bkr"),
        ("qr", wqrT, 12, 0, 1024, "qcs", "bqr"),
    ]

    def p0_precast():
        off = 0
        for k, (sname, wap, nd, fcol0, FW, _src, _b) in enumerate(SUBS):
            wcols = nd * FW
            st_["wscr_map"][sname] = off
            st_["wscr_stc"][sname] = 16 * (k + 1)
            wsl = k % 2
            ww = []
            if k >= 2:
                ww.append(s_wst.cur())
            ndh = max(1, nd // 2)
            op("gpsimd", "dma_start",
               out=wbuf[wsl][:, 0:ndh * FW].rearrange("p (g f) -> p g f", g=ndh),
               in_=wap.rearrange("(g p) f -> p g f", p=128)[:, 0:ndh, fcol0:fcol0 + FW],
               waits=ww, inc=s_w[wsl].bump(16))
            if nd > ndh:
                op("gpsimd", "dma_start",
                   out=wbuf[wsl][:, ndh * FW:nd * FW].rearrange("p (g f) -> p g f", g=nd - ndh),
                   in_=wap.rearrange("(g p) f -> p g f", p=128)[:, ndh:nd, fcol0:fcol0 + FW],
                   waits=ww, inc=s_w[wsl].bump(16))
            op("gpsimd", "dma_start", out=wscr[:, off:off + wcols],
               in_=wbuf[wsl][:, 0:wcols],
               waits=[s_w[wsl].cur(), s_wst.cur()], inc=s_wst.bump(16))
            off += wcols

    def p1_pass(bt):
        for tc in range(8):
            sl = tc % 3
            sl2 = tc % 2
            w = []
            if st_["x_snap"][sl] is not None:
                w.append((s_s.h, st_["x_snap"][sl][1]))
            op("sync", "dma_start", out=xf[sl],
               in_=x_sl[bt * TPB + tc * 128: bt * TPB + (tc + 1) * 128, :],
               waits=w, inc=s_x[sl].bump(16))
            if LN_CUT == "x":
                st_["x_snap"][sl] = None
                continue
            wt("vector", s_x[sl].cur())
            if st_["ln_s_snap"] is not None:
                wt("vector", (s_s.h, st_["ln_s_snap"]))
            for k in range(4):
                op("vector", "bn_stats", bnst[:, 6 * k:6 * k + 6], xf[sl][:, 512 * k:512 * (k + 1)],
                   nodrain=(k > 0))
            op("vector", "bn_aggr", mv[:, :], bnst[:, :], inc=s_v.bump())
            if LN_CUT == "bn":
                st_["x_snap"][sl] = s_v.v
                continue
            wt("scalar", (s_v.h, s_v.v))
            op("scalar", "activation", out=rstd[:, :], in_=mv[:, 1:2], func=AF.Ln, bias=sb_sm[:, 32:33], scale=1.0)
            op("scalar", "activation", out=rstd[:, :], in_=rstd[:, :], func=AF.Exp, bias=0.0,
               scale=-0.5)
            op("scalar", "activation", out=negr[:, :], in_=rstd[:, :], func=AF.Identity,
               bias=sb_sm[:, 34:35], scale=-1.0)
            op("scalar", "activation", out=nmr[:, :], in_=mv[:, 0:1], func=AF.Identity,
               bias=sb_sm[:, 34:35], scale=negr[:, 0:1])
            xw = []
            if st_["xln_free"][sl2] is not None:
                xw.append((s_t.h, st_["xln_free"][sl2]))
            op("scalar", "activation", out=xln[sl2][:, :], in_=xf[sl], func=AF.Identity,
               bias=nmr[:, 0:1], scale=rstd[:, 0:1], waits=xw, inc=s_s.bump())
            st_["x_snap"][sl] = ("s", s_s.v)
            st_["ln_s_snap"] = s_s.v
            xlnsnap = ("s", s_s.v)
            if LN_CUT == "xln":
                st_["xln_free"][sl2] = None
                continue
            for g4 in range(4):
                par = g4 % 2
                pb = PS[:, 2048 + par * 512: 2048 + par * 512 + 512].bitcast(BF16)
                tw = [(s_s.h, xlnsnap[1])]
                if st_["tp_evac"][par] is not None:
                    tw.append((s_s.h, st_["tp_evac"][par]))
                if st_["vt_evac"][par] is not None:
                    tw.append((s_s.h, st_["vt_evac"][par][1]))
                for j in range(4):
                    dc = g4 * 4 + j
                    op("tensor", "transpose", pb[:, j * 128:(j + 1) * 128],
                       xln[sl2][:, dc * 128:(dc + 1) * 128], sb_id[:, :],
                       waits=(tw if j == 0 else ()))
                tinc("tensor", s_t.bump())
                tsn = s_t.v
                if LN_CUT == "tpv":
                    op("vector", "tensor_copy", out=xlnT[:, g4 * 512:(g4 + 1) * 512],
                       in_=pb[:, 0:512], waits=[(s_t.h, tsn)], inc=s_v.bump())
                    st_["tp_evac"][par] = None
                    stv_snap = s_v.v
                    op("scalar", "_wait_only") if False else None
                    st_["vt_evac"][par] = stv_snap
                    continue
                if LN_CUT == "tps":
                    op("scalar", "copy", out=xlnT[:, g4 * 512:(g4 + 1) * 512],
                       in_=pb[:, 0:512], waits=[(s_t.h, tsn)], inc=s_s.bump())
                    st_["tp_evac"][par] = s_s.v
                    continue
                dview = xlnT.rearrange("p (c t) -> p c t", c=16)[:, g4 * 4:g4 * 4 + 4,
                                                                tc * 128:(tc + 1) * 128]
                op("scalar", "copy", out=dview,
                   in_=pb[:, 0:512].rearrange("p (a b) -> p a b", a=4),
                   waits=[(s_t.h, tsn)], inc=s_s.bump(), nodrain=True)
                st_["tp_evac"][par] = s_s.v
            st_["xln_free"][sl2] = s_t.v
            if tc == 3:
                st_["xlnT_half0"] = s_s.v
        st_["xlnT_done"] = s_s.v

        if p1_subs is not None and len(p1_subs) == 0:
            return
        subs = list(SUBS)
        if p1_subs is not None:
            subs = [s for s in subs if s[0] in p1_subs]
        for sname, wap, nd, fcol0, FW, src, bnm in subs:
            wsl = st_["wtog"] % 2
            st_["wtog"] += 1
            ww = []
            if st_["wfree"][wsl] is not None:
                ww.append((s_t.h, st_["wfree"][wsl]))
            wcols = nd * FW
            if bt == 0:
                woff = st_["wscr_off"]
                st_["wscr_map"][sname] = woff
                st_["wscr_off"] += wcols
                if st_["wfree"][wsl] is not None and st_["wst_slot"][wsl] is not None:
                    ww.append((s_wst.h, st_["wst_slot"][wsl]))
                ndh = max(1, nd // 2)
                op("gpsimd", "dma_start",
                   out=wbuf[wsl][:, 0:ndh * FW].rearrange("p (g f) -> p g f", g=ndh),
                   in_=wap.rearrange("(g p) f -> p g f", p=128)[:, 0:ndh, fcol0:fcol0 + FW],
                   waits=ww, inc=s_w[wsl].bump(16))
                if nd > ndh:
                    op("gpsimd", "dma_start",
                       out=wbuf[wsl][:, ndh * FW:nd * FW].rearrange("p (g f) -> p g f", g=nd - ndh),
                       in_=wap.rearrange("(g p) f -> p g f", p=128)[:, ndh:nd, fcol0:fcol0 + FW],
                       waits=ww, inc=s_w[wsl].bump(16))
                wsem = s_w[wsl].cur()
                op("scalar", "dma_start", out=wscr[:, woff:woff + wcols],
                   in_=wbuf[wsl][:, 0:wcols], waits=[wsem], inc=s_wst.bump(16))
                st_["wst_slot"][wsl] = s_wst.v
                st_["wscr_stc"][sname] = s_wst.v
            else:
                ww.append((s_wst.h, st_["wscr_stc"][sname]))
                woff = st_["wscr_map"][sname]
                op("sync", "dma_start", out=wbuf[wsl][:, 0:wcols],
                   in_=wscr[:, woff:woff + wcols],
                   waits=ww, inc=s_w2[wsl].bump(16))
                wsem = s_w2[wsl].cur()
            nf = FW // 128
            for fc in range(nf):
                fcg = (fcol0 // 128) + fc
                par = st_["pjpar"] % 2
                st_["pjpar"] += 1
                pdst = PS[:, par * 1024:(par + 1) * 1024]
                mw = [wsem]
                if st_["proj_evac"][par] is not None:
                    mw.append((s_s.h, st_["proj_evac"][par]))
                mwh = list(mw)
                if src == "xlnT":
                    mw.append((s_s.h, st_["xlnT_half0"]))
                    mwh.append((s_s.h, st_["xlnT_done"]))
                elif src == "kvc":
                    mw.append((s_s.h, st_["kvc_done"]))
                else:
                    mw.append((s_s.h, st_["qcs_done"]))
                for qh in range(2):
                    for dc in range(nd):
                        if src == "xlnT":
                            rhs_full = xlnT_dc(dc)
                        elif src == "kvc":
                            rhs_full = kvc[:, dc * 1024:(dc + 1) * 1024]
                        else:
                            rhs_full = qcs[:, dc * 1024:(dc + 1) * 1024]
                        lhs = wbuf[wsl][:, dc * FW + fc * 128: dc * FW + fc * 128 + 128]
                        w_ = mw if qh == 0 else mwh
                        op("tensor", "matmul", pdst[:, qh * 512:(qh + 1) * 512], lhs,
                           rhs_full[:, qh * 512:(qh + 1) * 512],
                           start=(dc == 0), stop=(dc == nd - 1),
                           waits=(w_ if dc == 0 else ()))
                tinc("tensor", s_t.bump())
                mmsnap = s_t.v
                bap = sb_bias[:, BOFF[bnm] + fcg: BOFF[bnm] + fcg + 1]
                ew = [(s_t.h, mmsnap)]
                ssl = None
                if sname == "kvc":
                    edst = kvc[:, fcg * 1024:(fcg + 1) * 1024]
                elif sname in ("qcA", "qcB"):
                    edst = qcs[:, fcg * 1024:(fcg + 1) * 1024]
                elif sname in ("krA", "krB", "qr"):
                    edst = krr[fcg % 2]
                    if st_["kr_free"][fcg % 2] is not None:
                        ew.append((s_v.h, st_["kr_free"][fcg % 2]))
                elif sname == "v":
                    edst = vfm[fcg % 4]
                    if st_["vfm_free"][fcg % 4] is not None:
                        ew.append((s_t.h, st_["vfm_free"][fcg % 4]))
                else:
                    ssl = st_["stgi"] % 3
                    st_["stgi"] += 1
                    edst = stg[ssl]
                    ew.append(s_sh[bt].cur())
                    ew.append(s_stg.cur())
                op("scalar", "activation", out=edst, in_=pdst, func=AF.Identity,
                   bias=bap, scale=1.0, waits=ew, inc=s_s.bump(), nodrain=True)
                st_["proj_evac"][par] = s_s.v
                evsnap = s_s.v
                if sname == "kvc" and fcg == 3:
                    st_["kvc_done"] = evsnap
                if sname == "qcB" and fcg == 11:
                    st_["qcs_done"] = evsnap
                if sname == "kcnt":
                    op("sync", "dma_start", out=shard[bt][fcg * 128:(fcg + 1) * 128, :],
                       in_=edst, waits=[(s_s.h, evsnap)], inc=s_sh[bt].bump(16))
                elif sname in ("qcntA", "qcntB"):
                    op("sync", "dma_start",
                       out=qscr[fcg * 128:(fcg + 1) * 128, bt * TPB:(bt + 1) * TPB],
                       in_=edst, waits=[(s_s.h, evsnap)], inc=s_stg.bump(16))
                elif sname == "v":
                    st_["vfm_pend"].append((fcg, evsnap))
                    if len(st_["vfm_pend"]) == 4:
                        emit_v_transposes(bt)
                elif sname in ("krA", "krB", "qr"):
                    emit_rope(bt, sname, fcg, evsnap)
            st_["wfree"][wsl] = s_t.v

    defer_cc1 = do_cc and ("p3" in phases) and p1_passes > 1
    cc1_w = [None]
    for bt in range(p1_passes):
        p1_pass(bt)
        if do_cc:
            ccw = [s_sh[bt].cur()]
            if bt == 1 and defer_cc1:
                cc1_w[0] = ccw
                continue
            op("gpsimd", "collective", kind="AllGather",
               cins=[shard_t[bt].ap()], couts=[gath_t[bt].ap()],
               waits=ccw, inc=s_cc[bt].bump(1))
        if dbg:
            op("gpsimd", "dma_start", out=dbgsh_t[bt].ap(), in_=shard[bt],
               waits=[s_sh[bt].cur()], inc=s_dbg.bump(16))
    QS_TOT = s_stg.cur()
    if dbg:
        wt("gpsimd", s_dbg.cur())
    if "p3" not in phases:
        wt("sync", QS_TOT, s_sh[0].cur(), s_sh[1].cur())
        wt("vector", (s_v.h, s_v.v))
        wt("scalar", (s_s.h, s_s.v))
        wt("tensor", (s_t.h, s_t.v))
        if do_cc:
            wt("gpsimd", (s_cc[0].h, 1) if p1_passes > 0 else (s_cc[0].h, 0))
            if p1_passes > 1:
                wt("gpsimd", (s_cc[1].h, 1))

    if "p3" not in phases:
        return _finish(nc, es, ops, ENGS, NCORE, drains)
    AB.reset(); AFA.reset()
    AB.mark(); AFA.mark()
    qbuf = [AB.alloc(6 * 1024) for _ in range(2)]
    kvb = [AB.alloc(10240) for _ in range(2)]
    ptil = [AB.alloc(1024) for _ in range(6)]
    onstg = [AB.alloc(1024) for _ in range(4)]
    lsb16s = [AB.alloc(1024, parts=1) for _ in range(4)]
    osbb = AFA.alloc(4096)
    lsbb = AFA.alloc(4096, parts=1)

    OO, LL = 2048, 3072
    P1T, P1V, P1S = s_t.cur(), s_v.cur(), s_s.cur()
    wt("sync", P1T, P1V, P1S)
    wt("gpsimd", P1T, P1V, P1S)

    a_ = dict(scpar=0, ptc=0, pt_cons=[None] * 6, sc_evac=[None] * 4,
              ol_free=None, bc_evac=None, kv_free=[None, None], q_free=[None, None],
              oni=0, sweep=0, kvctr=0, pend_onst=[], lp_snap=[None, None], lp_free=None, ptil_guard=None)

    def kv_load(bt, g, bi, slot):
        h0 = g * 4
        w = [(s_cc[bt].h, 1)] if do_cc else []
        if a_["kv_free"][slot] is not None:
            w.append((s_t.h, a_["kv_free"][slot]))
        gb = gath[bt].rearrange("(r f) c -> r f c", f=5120)
        for hh in range(4):
            h = h0 + hh
            src = gb[:, h * 128:(h + 1) * 128, bi * 128:(bi + 1) * 128]
            dst = kvb[slot][:, hh * 1024:(hh + 1) * 1024].rearrange("p (r t) -> p r t", r=8)
            if hh < 2:
                op("sync", "dma_start", out=dst, in_=src.rearrange("r f c -> f r c"),
                   waits=(w if hh == 0 else ()), inc=s_kv[slot].bump(16))
            else:
                op("scalar", "dma_start", out=dst, in_=src.rearrange("r f c -> f r c"),
                   waits=(w if hh == 2 else ()), inc=s_kvs[slot].bump(16))
        for pp in range(2):
            pj = g * 2 + pp
            src = gb[:, 2048 + pj * 128: 2048 + (pj + 1) * 128, bi * 128:(bi + 1) * 128]
            dst = kvb[slot][:, 4096 + pp * 1024: 4096 + (pp + 1) * 1024].rearrange(
                "p (r t) -> p r t", r=8)
            op("gpsimd", "dma_start", out=dst, in_=src.rearrange("r f c -> f r c"),
               waits=(w if pp == 0 else ()), inc=s_kvg[slot].bump(16))
        dcol0 = h0 * 128
        half = dcol0 // 1024
        col0 = dcol0 % 1024
        src = gb[:, 3072 + 256 * bi: 3072 + 256 * (bi + 1), :].rearrange(
            "r (m two) c -> r m two c", two=2)[:, :, half, col0:col0 + 512]
        dst = kvb[slot][:, 6144:6144 + 4096].rearrange("p (r d) -> p r d", r=8)
        op("gpsimd", "dma_start", out=dst, in_=src.rearrange("r m d -> m r d"),
           inc=s_kvg[slot].bump(16))

    def q_load(bt, g, slot):
        w = [QS_TOT]
        if a_["q_free"][slot] is not None:
            w.append((s_t.h, a_["q_free"][slot]))
        for hh in range(4):
            h = g * 4 + hh
            op("sync", "dma_start", out=qbuf[slot][:, hh * 1024:(hh + 1) * 1024],
               in_=qscr[h * 128:(h + 1) * 128, bt * TPB:(bt + 1) * TPB],
               waits=(w if hh == 0 else ()), inc=s_q[slot].bump(16))
        for pp in range(2):
            pj = g * 2 + pp
            op("sync", "dma_start", out=qbuf[slot][:, 4096 + pp * 1024: 4096 + (pp + 1) * 1024],
               in_=qscr[2048 + pj * 128: 2048 + (pj + 1) * 128, bt * TPB:(bt + 1) * TPB],
               inc=s_q[slot].bump(16))

    def flush_onst():
        for (h, bt2, osl, onsnap) in a_["pend_onst"]:
            op("sync", "dma_start",
               out=onorm[h * 128:(h + 1) * 128, bt2 * TPB:(bt2 + 1) * TPB],
               in_=onstg[osl][:, :], waits=[(s_v.h, onsnap)], inc=s_onst.bump(16))
        a_["pend_onst"] = []

    def attention_sweep(bt, g):
        si = a_["sweep"]
        a_["sweep"] += 1
        qsl = si % 2
        q_load(bt, g, qsl)
        qsem = s_q[qsl].cur()
        for bi in range(8):
            if si == 0 and bi == 4 and cc1_w[0] is not None:
                op("gpsimd", "collective", kind="AllGather",
                   cins=[shard_t[1].ap()], couts=[gath_t[1].ap()],
                   waits=cc1_w[0], inc=s_cc[1].bump(1))
                cc1_w[0] = None
            ksl = a_["kvctr"] % 2
            a_["kvctr"] += 1
            kv_load(bt, g, bi, ksl)
            if bi == 1:
                flush_onst()
            kvsem = s_kv[ksl].cur()
            kvgsem = s_kvg[ksl].cur()
            kvssem = s_kvs[ksl].cur()
            Nq = 1024 - 128 * bi
            nqt = (Nq + 511) // 512
            qoff = 128 * bi
            for pair in range(2):
                for qh in range(nqt):
                    wq = min(512, Nq - qh * 512)
                    q0 = qoff + qh * 512
                    pend = []

                    def emit_attnv(j, lh, psl, msnap, wq=wq, q0=q0, sem=None):
                        hh = pair * 2 + lh
                        aw = [((sem or s_v).h, msnap)]
                        if j == 0 and a_["ol_free"] is not None:
                            aw.append((s_v.h, a_["ol_free"]))
                        vlhs = kvb[ksl][:, 6144 + j * 512 + hh * 128:
                                        6144 + j * 512 + hh * 128 + 128]
                        op("tensor", "matmul",
                           PS[0:1, 3072 + lh * 512: 3072 + lh * 512 + wq],
                           sb_ones[:, 0:1], ptil[psl][:, 0:wq],
                           start=(j == 0), stop=(j == 7), waits=aw)
                        op("tensor", "matmul",
                           PS[:, 2048 + lh * 512: 2048 + lh * 512 + wq],
                           vlhs, ptil[psl][:, 0:wq],
                           start=(j == 0), stop=(j == 7))
                        tinc("tensor", s_t.bump())
                        a_["pt_cons"][psl] = s_t.v

                    for j in range(8):
                        par = j % 2
                        w = [kvsem, kvgsem, kvssem, qsem]
                        for lh in range(2):
                            if a_["sc_evac"][lh * 2 + par] is not None:
                                w.append((s_s.h, a_["sc_evac"][lh * 2 + par]))
                        if a_["bc_evac"] is not None:
                            w.append((s_v.h, a_["bc_evac"]))
                        for lh in range(2):
                            hh = pair * 2 + lh
                            lhs_n = kvb[ksl][:, hh * 1024 + j * 128: hh * 1024 + j * 128 + 128]
                            rq_n = qbuf[qsl][:, hh * 1024 + q0: hh * 1024 + q0 + wq]
                            op("tensor", "matmul",
                               PS[:, (lh * 2 + par) * 512: (lh * 2 + par) * 512 + wq],
                               lhs_n, rq_n, start=True, stop=False,
                               waits=(w if lh == 0 else ()))
                        for lh in range(2):
                            hh = pair * 2 + lh
                            half = hh % 2
                            lhs_r = kvb[ksl][64 * half:64 * half + 64,
                                             4096 + pair * 1024 + j * 128:
                                             4096 + pair * 1024 + j * 128 + 128]
                            rq_r = qbuf[qsl][64 * half:64 * half + 64,
                                             4096 + pair * 1024 + q0:
                                             4096 + pair * 1024 + q0 + wq]
                            op("tensor", "matmul",
                               PS[:, (lh * 2 + par) * 512: (lh * 2 + par) * 512 + wq],
                               lhs_r, rq_r, start=False, stop=True)
                        tinc("tensor", s_t.bump())
                        scsnap = s_t.v
                        for lh in range(2):
                            psl = a_["ptc"] % 6
                            a_["ptc"] += 1
                            ew = [(s_t.h, scsnap)]
                            if a_["pt_cons"][psl] is not None:
                                ew.append((s_t.h, a_["pt_cons"][psl]))
                            op("scalar", "activation", out=ptil[psl][:, 0:wq],
                               in_=PS[:, (lh * 2 + par) * 512: (lh * 2 + par) * 512 + wq],
                               func=AF.Exp, bias=sb_sm[:, 33:34], scale=SCALE,
                               waits=ew, inc=s_s.bump(), nodrain=True)
                            a_["sc_evac"][lh * 2 + par] = s_s.v
                            expsnap = s_s.v
                            if qh == 0:
                                op("vector", "tensor_tensor", out=ptil[psl][:, 0:128],
                                   in0=ptil[psl][:, 0:128],
                                   in1=sb_mask[:, j * 128:(j + 1) * 128],
                                   op=ALU.mult, waits=[(s_s.h, expsnap)], inc=s_v.bump(),
                                   nodrain=True)
                                msnap = s_v.v
                            else:
                                msnap = None
                            pend.append((j, lh, psl, msnap if msnap is not None else expsnap))
                            if msnap is None:
                                pass
                        while len(pend) > 4:
                            jj, lh2, psl2, sn2 = pend.pop(0)
                            emit_attnv(jj, lh2, psl2, sn2,
                                       sem=(s_v if qh == 0 else s_s))
                    while pend:
                        jj, lh2, psl2, sn2 = pend.pop(0)
                        emit_attnv(jj, lh2, psl2, sn2,
                                   sem=(s_v if qh == 0 else s_s))
                    avsnap = s_t.v
                    o_out = osbb[:, pair * 2048: pair * 2048 + 2048].rearrange(
                        "p (g c) -> p g c", g=2)[:, :, q0:q0 + wq]
                    o_in = PS[:, 2048:3072].rearrange("p (g c) -> p g c", g=2)[:, :, 0:wq]
                    l_out = lsbb[0:1, pair * 2048: pair * 2048 + 2048].rearrange(
                        "p (g c) -> p g c", g=2)[:, :, q0:q0 + wq]
                    l_in = PS[0:1, 3072:4096].rearrange("p (g c) -> p g c", g=2)[:, :, 0:wq]
                    if bi == 0:
                        op("vector", "tensor_copy", out=o_out, in_=o_in,
                           waits=[(s_t.h, avsnap)], nodrain=True)
                        op("vector", "tensor_copy", out=l_out, in_=l_in,
                           inc=s_v.bump(), nodrain=True)
                    else:
                        op("vector", "tensor_tensor", out=o_out, in0=o_out, in1=o_in,
                           op=ALU.add, waits=[(s_t.h, avsnap)], nodrain=True)
                        op("vector", "tensor_tensor", out=l_out, in0=l_out, in1=l_in,
                           op=ALU.add, inc=s_v.bump(), nodrain=True)
                    a_["ol_free"] = s_v.v
            a_["kv_free"][ksl] = s_t.v
        a_["q_free"][qsl] = s_t.v
        rsnaps = []
        for hh in range(4):
            lview = lsbb[0:1, hh * 1024:(hh + 1) * 1024]
            op("vector", "reciprocal", lview, lview, nodrain=(hh > 0))
            op("vector", "tensor_copy", out=lsb16s[hh][0:1, :], in_=lview,
               inc=s_v.bump())
            rsnaps.append(s_v.v)
        mult_snap = [None, None]
        for hh in range(4):
            h = g * 4 + hh
            pb_off = (hh % 2) * 1024
            bw = [(s_v.h, rsnaps[hh])]
            if mult_snap[hh % 2] is not None:
                bw.append((s_v.h, mult_snap[hh % 2]))
            elif a_["bc_evac"] is not None:
                bw.append((s_v.h, a_["bc_evac"]))
            for k4 in range(4):
                if a_["sc_evac"][k4] is not None:
                    bw.append((s_s.h, a_["sc_evac"][k4]))
            for qt in range(2):
                op("tensor", "matmul", PS[:, pb_off + qt * 512: pb_off + qt * 512 + 512],
                   sb_ones[0:1, :], lsb16s[hh][0:1, qt * 512: qt * 512 + 512],
                   start=True, stop=True, waits=(bw if qt == 0 else ()))
            tinc("tensor", s_t.bump())
            bsnap = s_t.v
            osl = a_["oni"] % 4
            a_["oni"] += 1
            ow = [(s_t.h, bsnap), s_onst.cur()]
            op("vector", "tensor_tensor", out=onstg[osl][:, :],
               in0=osbb[:, hh * 1024:(hh + 1) * 1024],
               in1=PS[:, pb_off: pb_off + 1024], op=ALU.mult, waits=ow, inc=s_v.bump())
            mult_snap[hh % 2] = s_v.v
            a_["bc_evac"] = s_v.v
            onsnap = s_v.v
            a_["pend_onst"].append((h, bt, osl, onsnap))

    for bt in range(2):
        for g in range(4):
            attention_sweep(bt, g)
    flush_onst()
    ON_TOT = s_onst.cur()
    if "p4" not in phases:
        wt("sync", ON_TOT)

    if "p4" not in phases:
        return _finish(nc, es, ops, ENGS, NCORE, drains)
    AB.reset(); AFA.reset()
    AB.mark(); AFA.mark()
    woT = AB.alloc(16 * 2048)
    onb = [AB.alloc(2048) for _ in range(2)]
    x2l = [AB.alloc(2048) for _ in range(2)]
    x2T = AB.alloc(16 * 256)
    aTb = AB.alloc(16 * E * R)
    bwTb = AB.alloc(E * D, parts=64)
    wgTb = AB.alloc(16 * E)
    ghb = AB.alloc(E * 256, parts=64)
    gbc = AB.alloc(256, parts=64)
    gtb = AB.alloc(256, parts=8)
    scr8 = AB.alloc(8)
    xf4 = [AFA.alloc(2048) for _ in range(2)]
    xnw = [AFA.alloc(2048) for _ in range(2)]

    P3T, P3V, P3S = s_t.cur(), s_v.cur(), s_s.cur()
    wt("sync", P3T, P3V, P3S)
    wt("gpsimd", P3T, P3V, P3S)

    op("gpsimd", "dma_start", out=woT[:, 0:8 * 2048].rearrange("p (c d) -> p c d", c=8),
       in_=woTr.rearrange("(c p) d -> p c d", p=128)[:, 0:8, :], inc=s_w[0].bump(16))
    P4WA = s_w[0].cur()
    op("gpsimd", "dma_start", out=woT[:, 8 * 2048:].rearrange("p (c d) -> p c d", c=8),
       in_=woTr.rearrange("(c p) d -> p c d", p=128)[:, 8:16, :], inc=s_w[0].bump(16))
    P4WB = s_w[0].cur()
    op("gpsimd", "dma_start", out=aTb, in_=aTh, inc=s_w[0].bump(16))
    op("gpsimd", "dma_start", out=bwTb, in_=bwTh, inc=s_w[0].bump(16))
    op("gpsimd", "dma_start", out=wgTb, in_=wgTh, inc=s_w[0].bump(16))
    P4W = s_w[0].cur()

    f_ = dict(on_free=[None, None], x_free=[None, None], tp_evac=[None, None],
              x2l_free=[None, None], x2T_cons=None, logit_free=None,
              gt_free=None, gbc_free=None, h_free=[None, None],
              py_free=[None, None], xn_store=[None, None], op_free=None,
              y67_free=None, xnw_snap=[None, None])

    def p4_loads(bt, gg):
        for ii, tt in enumerate((gg * 2, gg * 2 + 1)):
            r0 = bt * TPB + tt * 128
            osl = tt % 2
            xsl = tt % 2
            w = [ON_TOT]
            if f_["on_free"][osl] is not None:
                w.append((s_t.h, f_["on_free"][osl]))
            src2 = onorm.rearrange("(c p) t -> p c t", p=128)[:, :, r0: r0 + 128]
            onv = onb[osl].rearrange("p (c t) -> p c t", c=16)
            op("sync", "dma_start", out=onv[:, 0:8, :], in_=src2[:, 0:8, :],
               waits=w, inc=s_on[osl].bump(16))
            op("sync", "dma_start", out=onv[:, 8:16, :], in_=src2[:, 8:16, :],
               inc=s_on[osl].bump(16))
            xw = []
            if f_["x_free"][xsl] is not None:
                xw.append((s_v.h, f_["x_free"][xsl]))
            op("sync", "dma_start", out=xf4[xsl], in_=x_sl[r0: r0 + 128, :], waits=xw,
               inc=s_x[xsl].bump(16))

    def p4_opmm(bt, gg, ii):
        tt = gg * 2 + ii
        osl = tt % 2
        mw = [s_on[osl].cur(), P4WA]
        if f_["op_free"] is not None:
            mw.append((s_v.h, f_["op_free"]))
        for hc in range(16):
            for dt in range(4):
                w_ = ()
                if hc == 0 and dt == 0:
                    w_ = mw
                elif hc == 8 and dt == 0:
                    w_ = [P4WB]
                op("tensor", "matmul", PS[:, dt * 512: dt * 512 + 512],
                   onb[osl][:, hc * 128:(hc + 1) * 128],
                   woT[:, hc * 2048 + dt * 512: hc * 2048 + dt * 512 + 512],
                   start=(hc == 0), stop=False,
                   waits=w_)
        for dt in range(4):
            op("tensor", "matmul", PS[:, dt * 512: dt * 512 + 512],
               sb_ones[0:1, 0:128], sb_bo16[0:1, dt * 512:(dt + 1) * 512],
               start=False, stop=True)
        tinc("tensor", s_t.bump())
        f_["on_free"][osl] = s_t.v
        f_["opmm_snap_%d" % ii] = s_t.v

    def p4_post_add(bt, gg, ii):
        tt = gg * 2 + ii
        xsl = tt % 2
        mmsnap = f_["opmm_snap_%d" % ii]
        if f_["xn_store"][ii] is not None:
            wt("vector", (s_out[ii].h, f_["xn_store"][ii]))
        vw = [(s_t.h, mmsnap), s_x[xsl].cur()]
        for dt in range(4):
            last = dt == 3
            op("vector", "tensor_tensor", out=xnw[ii][:, dt * 512:(dt + 1) * 512],
               in0=xf4[xsl][:, dt * 512:(dt + 1) * 512], in1=PS[:, dt * 512: dt * 512 + 512],
               op=ALU.add, waits=(vw if dt == 0 else ()),
               inc=(s_v.bump() if last else None), nodrain=(dt > 0))
        f_["x_free"][xsl] = s_v.v
        f_["xnw_snap"][ii] = s_v.v
        f_["op_free"] = s_v.v

    def p4_post_ln(bt, gg, ii):
        if f_.get("p4_ln_s") is not None:
            wt("vector", (s_s.h, f_["p4_ln_s"]))
        for k in range(4):
            op("vector", "bn_stats", bnst[:, 6 * k:6 * k + 6],
               xnw[ii][:, 512 * k:512 * (k + 1)], nodrain=(k > 0))
        op("vector", "bn_aggr", mv[:, :], bnst[:, :], inc=s_v.bump())
        wt("scalar", (s_v.h, s_v.v))
        op("scalar", "activation", out=rstd[:, :], in_=mv[:, 1:2], func=AF.Ln,
           bias=sb_sm[:, 32:33], scale=1.0)
        op("scalar", "activation", out=rstd[:, :], in_=rstd[:, :], func=AF.Exp, bias=0.0,
           scale=-0.5)
        op("scalar", "activation", out=negr[:, :], in_=rstd[:, :], func=AF.Identity,
           bias=sb_sm[:, 34:35], scale=-1.0)
        op("scalar", "activation", out=nmr[:, :], in_=mv[:, 0:1], func=AF.Identity,
           bias=sb_sm[:, 34:35], scale=negr[:, 0:1])
        x2w = []
        if f_["x2l_free"][ii] is not None:
            x2w.append((s_t.h, f_["x2l_free"][ii]))
        op("scalar", "activation", out=x2l[ii][:, :], in_=xnw[ii], func=AF.Identity,
           bias=nmr[:, 0:1], scale=rstd[:, 0:1], waits=x2w, inc=s_s.bump())
        x2snap = s_s.v
        f_["p4_ln_s"] = s_s.v
        for g4 in range(4):
            par = g4 % 2
            pb = PS[:, 2048 + par * 512: 2048 + par * 512 + 512].bitcast(BF16)
            tw = [(s_s.h, x2snap)]
            if f_["tp_evac"][par] is not None:
                tw.append((s_s.h, f_["tp_evac"][par]))
            if f_["h_free"][par] is not None:
                tw.append((s_v.h, f_["h_free"][par]))
            for j in range(4):
                dc = g4 * 4 + j
                op("tensor", "transpose", pb[:, j * 128:(j + 1) * 128],
                   x2l[ii][:, dc * 128:(dc + 1) * 128], sb_id[:, :],
                   waits=(tw if j == 0 else ()))
            tinc("tensor", s_t.bump())
            tsn = s_t.v
            dview = x2T.rearrange("p (c t) -> p c t", c=16)[:, g4 * 4: g4 * 4 + 4,
                                                            ii * 128:(ii + 1) * 128]
            ew = [(s_t.h, tsn)]
            if ii == 0 and g4 == 0 and f_["x2T_cons"] is not None:
                ew.append((s_t.h, f_["x2T_cons"]))
            op("scalar", "copy", out=dview,
               in_=pb[:, 0:512].rearrange("p (a b) -> p a b", a=4),
               waits=ew, inc=s_s.bump(), nodrain=True)
            f_["tp_evac"][par] = s_s.v
        f_["x2l_free"][ii] = s_t.v

    def p4_ffn(bt, gg, next_cb=None):
        tts = (gg * 2, gg * 2 + 1)
        x2T_snap = s_s.cur()
        for ii in range(2):
            lw = [x2T_snap, P4W]
            if f_["logit_free"] is not None:
                lw.append((s_v.h, f_["logit_free"]))
            if f_["y67_free"] is not None:
                lw.append((s_v.h, f_["y67_free"]))
            for dc in range(16):
                op("tensor", "matmul", PS[:, 3072:3080],
                   x2T.rearrange("p (c t) -> p c t", c=16)[:, dc, ii * 128:(ii + 1) * 128],
                   wgTb.rearrange("p (c e) -> p c e", c=16)[:, dc, :],
                   start=(dc == 0), stop=(dc == 15), waits=(lw if dc == 0 else ()))
            tinc("tensor", s_t.bump())
            lsnap = s_t.v
            op("vector", "tensor_reduce", sb_sm[:, 28:29], PS[:, 3072:3080],
               mybir.AxisListType.X, ALU.max, waits=[(s_t.h, lsnap)])
            op("vector", "tensor_scalar", out=sb_sm[:, 29:30], in0=sb_sm[:, 28:29],
               scalar1=-1.0, scalar2=None, op0=ALU.mult, inc=s_v.bump())
            wt("scalar", (s_v.h, s_v.v))
            op("scalar", "activation", out=scr8[:, :], in_=PS[:, 3072:3080], func=AF.Exp,
               bias=sb_sm[:, 29:30], scale=1.0, accum_out=sb_sm[:, 30:31], inc=s_s.bump())
            wt("vector", (s_s.h, s_s.v))
            op("vector", "reciprocal", sb_sm[:, 31:32], sb_sm[:, 30:31])
            op("vector", "tensor_scalar", out=scr8[:, :], in0=PS[:, 3072:3080],
               scalar1=sb_sm[:, 29:30], scalar2=0.0, op0=ALU.add, op1=ALU.is_equal)
            op("vector", "tensor_scalar", out=scr8[:, :], in0=scr8[:, :],
               scalar1=sb_sm[:, 31:32], scalar2=MOE_SCALING, op0=ALU.mult, op1=ALU.mult,
               inc=s_v.bump())
            f_["logit_free"] = s_v.v
            gsnap = s_v.v
            pgt = PS[:, 3584:3648].bitcast(BF16)
            gw = [(s_v.h, gsnap)]
            if f_["gt_free"] is not None:
                gw.append((s_v.h, f_["gt_free"]))
            if f_["y67_free"] is not None:
                gw.append((s_v.h, f_["y67_free"]))
            op("tensor", "transpose", pgt[0:8, 0:128], scr8[:, :], sb_id[:, :], waits=gw)
            tinc("tensor", s_t.bump())
            gtsn = s_t.v
            op("vector", "tensor_copy", out=gtb[0:8, ii * 128:(ii + 1) * 128],
               in_=pgt[0:8, 0:128], waits=[(s_t.h, gtsn)], inc=s_v.bump())
            f_["gt_free"] = s_v.v
        gT_snap = s_v.cur()

        if next_cb is not None:
            next_cb()

        for e in range(E):
            par = e % 2
            bcw = [gT_snap]
            if f_["gbc_free"] is not None:
                bcw.append((s_v.h, f_["gbc_free"]))
            if f_["y67_free"] is not None:
                bcw.append((s_v.h, f_["y67_free"]))
            op("tensor", "matmul", PS[0:64, 3584:3840], sb_oh[:, e * 64:(e + 1) * 64],
               gtb[0:8, 0:256], start=True, stop=True, waits=bcw)
            tinc("tensor", s_t.bump())
            bcsnap = s_t.v
            op("vector", "tensor_copy", out=gbc[0:64, :], in_=PS[0:64, 3584:3840],
               waits=[(s_t.h, bcsnap)], inc=s_v.bump())
            gbcsnap = s_v.v
            f_["gbc_free"] = gbcsnap
            hw = [(s_v.h, gbcsnap), P4W]
            if f_["h_free"][par] is not None:
                hw.append((s_v.h, f_["h_free"][par]))
            for dc in range(16):
                op("tensor", "matmul", PS[0:64, 2048 + par * 512: 2048 + par * 512 + 256],
                   aTb.rearrange("p (c e r) -> p c e r", c=16, e=E)[:, dc, e, :],
                   x2T.rearrange("p (c t) -> p c t", c=16)[:, dc, :],
                   start=(dc == 0), stop=(dc == 15), waits=(hw if dc == 0 else ()))
            tinc("tensor", s_t.bump())
            hsnap = s_t.v
            op("vector", "tensor_tensor", out=ghb[0:64, e * 256:(e + 1) * 256],
               in0=PS[0:64, 2048 + par * 512: 2048 + par * 512 + 256], in1=gbc[0:64, :],
               op=ALU.mult, waits=[(s_t.h, hsnap)], inc=s_v.bump())
            f_["h_free"][par] = s_v.v
        gh_snap = s_v.cur()
        f_["x2T_cons"] = s_t.v

        ycnt = 0
        for ii in range(2):
            tt = tts[ii]
            r0 = bt * TPB + tt * 128
            for dt in range(4):
                yb = 3072 + (ycnt % 2) * 512
                ycnt += 1
                yw = [gh_snap, P4W]
                if f_["py_free"][(ycnt - 1) % 2] is not None:
                    yw.append((s_v.h, f_["py_free"][(ycnt - 1) % 2]))
                for e in range(E):
                    op("tensor", "matmul", PS[:, yb: yb + 512],
                       ghb[0:64, e * 256 + ii * 128: e * 256 + ii * 128 + 128],
                       bwTb[0:64, e * 2048 + dt * 512: e * 2048 + dt * 512 + 512],
                       start=(e == 0), stop=(e == E - 1), waits=(yw if e == 0 else ()))
                tinc("tensor", s_t.bump())
                ysnap = s_t.v
                op("vector", "tensor_tensor", out=xnw[ii][:, dt * 512:(dt + 1) * 512],
                   in0=xnw[ii][:, dt * 512:(dt + 1) * 512],
                   in1=PS[:, yb: yb + 512],
                   op=ALU.add, waits=[(s_t.h, ysnap)], inc=s_v.bump(), nodrain=(dt > 0))
                f_["py_free"][(ycnt - 1) % 2] = s_v.v
                f_["y67_free"] = s_v.v
            donesnap = s_v.v
            op("sync", "dma_start", out=out_sl[r0: r0 + 128, :], in_=xnw[ii][:, :],
               waits=[(s_v.h, donesnap)], inc=s_out[ii].bump(16))
            f_["xn_store"][ii] = s_out[ii].v

    groups = [(bt, gg) for bt in range(2) for gg in range(4)]
    p4_loads(*groups[0])
    p4_opmm(*groups[0], 0)
    p4_post_add(*groups[0], 0)
    p4_opmm(*groups[0], 1)
    p4_post_ln(*groups[0], 0)
    p4_post_add(*groups[0], 1)
    p4_post_ln(*groups[0], 1)
    for gi, (bt, gg) in enumerate(groups):
        nxt = groups[gi + 1] if gi + 1 < len(groups) else None
        if nxt is not None:
            p4_loads(*nxt)

            def cb(nxt=nxt):
                p4_opmm(*nxt, 0)
            p4_ffn(bt, gg, next_cb=cb)
            p4_post_add(*nxt, 0)
            p4_opmm(*nxt, 1)
            p4_post_ln(*nxt, 0)
            p4_post_add(*nxt, 1)
            p4_post_ln(*nxt, 1)
        else:
            p4_ffn(bt, gg)

    wt("sync", s_out[0].cur(), s_out[1].cur())
    return _finish(nc, es, ops, ENGS, NCORE, drains)


def _finish(nc, es, ops, engs, ncore, drains=False):
    block = es.enter_context(nc.Block())

    def runner(lst, auto_drain=False):
        def go(e):
            waited = {}
            for meth, args, kw, waits, inc in lst:
                for (h, v) in waits:
                    k = id(h)
                    if waited.get(k, -1) >= v:
                        continue
                    e.wait_ge(h, v)
                    waited[k] = v
                if meth == "_wait":
                    continue
                if meth == "_drain":
                    e.drain()
                    continue
                if meth.startswith("!"):
                    meth = meth[1:]
                elif auto_drain and meth not in ("_nop", "dma_start"):
                    e.drain()
                if meth == "_nop":
                    ins = e.nop()
                elif meth == "collective":
                    ins = e.collective_compute(
                        kw["kind"], ALU.bypass,
                        replica_groups=[list(range(ncore))],
                        ins=[a.opt() for a in kw["cins"]],
                        outs=[a.opt() for a in kw["couts"]],
                    )
                    if inc is not None:
                        ins.then_inc(inc[0])
                    continue
                else:
                    ins = getattr(e, meth)(*args, **kw)
                if inc is not None:
                    ins.then_inc(inc[0], inc[2])
        return go

    @block.tensor
    def _(e):
        runner(ops["tensor"])(e)

    @block.vector
    def _(e):
        runner(ops["vector"], auto_drain=drains)(e)

    @block.scalar
    def _(e):
        runner(ops["scalar"], auto_drain=True)(e)

    @block.sync
    def _(e):
        runner(ops["sync"])(e)

    @block.gpsimd
    def _(e):
        runner(ops["gpsimd"])(e)

    es.close()
    return nc



def _prep_inputs(kw):
    f32 = np.float32
    bf = ml_dtypes.bfloat16
    x = np.asarray(kw["L0_attn_x"], f32)

    def T(a):
        return np.ascontiguousarray(np.asarray(a, f32).T)

    wkdT = T(kw["L0_attn_Wkd"])
    wqdT = T(kw["L0_attn_Wqd"])
    wkuT = T(kw["L0_attn_Wku"])
    wvuT = T(kw["L0_attn_Wvu"])
    wquT = T(kw["L0_attn_Wqu"])
    wkrT = np.ascontiguousarray(np.asarray(kw["L0_attn_Wkr"], f32)[ROPE_ORDER, :].T)
    wqrT = np.ascontiguousarray(np.asarray(kw["L0_attn_Wqr"], f32)[ROPE_ORDER, :].T)
    woTr = T(kw["L0_attn_Wo"])
    A = np.asarray(kw["L0_ffn_A"], f32)
    Bw = np.asarray(kw["L0_ffn_Bw"], f32)
    Wg = np.asarray(kw["L0_ffn_Wg"], f32)

    aTh = np.ascontiguousarray(
        A.transpose(2, 0, 1).reshape(16, 128, E, R).transpose(1, 0, 2, 3).reshape(128, 16 * E * R))
    bwTh = np.ascontiguousarray(Bw.transpose(2, 0, 1).reshape(64, E * D))
    wgTh = np.ascontiguousarray(
        Wg.T.reshape(16, 128, E).transpose(1, 0, 2).reshape(128, 16 * E))

    def b2(v, nf):
        return np.ascontiguousarray(np.asarray(v, f32).reshape(nf, 128).T)

    bkd2 = b2(kw["L0_attn_bkd"], 4)
    bqd2 = b2(kw["L0_attn_bqd"], 12)
    bku2 = b2(kw["L0_attn_bku"], 16)
    bvu2 = b2(kw["L0_attn_bvu"], 16)
    bqu2 = b2(kw["L0_attn_bqu"], 16)
    bkr2 = b2(np.asarray(kw["L0_attn_bkr"], f32)[ROPE_ORDER], 8)
    bqr2 = b2(np.asarray(kw["L0_attn_bqr"], f32)[ROPE_ORDER], 8)
    bo2 = np.ascontiguousarray(np.asarray(kw["L0_attn_bo"], f32).reshape(1, D))

    ident = np.eye(128, dtype=bf)
    ones = np.ones((128, 128), dtype=bf)
    onehot = np.zeros((8, E * 64), dtype=bf)
    for e in range(E):
        onehot[e, e * 64:(e + 1) * 64] = 1.0

    inv_freq = (1.0 / (10000.0 ** (np.arange(0, DR, 2, dtype=np.float64) / DR))).astype(f32)

    shared = dict(wkdT=wkdT, wqdT=wqdT, wkuT=wkuT, wvuT=wvuT, wquT=wquT,
                  wkrT=wkrT, wqrT=wqrT, woTr=woTr, aTh=aTh, bwTh=bwTh, wgTh=wgTh,
                  bkd2=bkd2, bqd2=bqd2, bku2=bku2, bvu2=bvu2, bqu2=bqu2,
                  bkr2=bkr2, bqr2=bqr2, bo2=bo2, identt=ident, onest=ones,
                  onehot=onehot)
    in_maps = []
    for c in range(NCORE):
        pos = (8 * np.arange(TPB, dtype=f32) + c).astype(f32)
        ang = pos[None, :] * inv_freq[:, None]
        cos64 = np.cos(ang).astype(f32)
        sin64 = np.sin(ang).astype(f32)
        cosp = np.empty((64, TPB), f32)
        sinp = np.empty((64, TPB), f32)
        for j, d in enumerate(PERM64):
            cosp[j] = cos64[d % 32]
            sinp[j] = sin64[d % 32] * (-1.0 if d < 32 else 1.0)
        costab = np.ascontiguousarray(np.concatenate([cosp, cosp], 0))
        sintb = np.ascontiguousarray(np.concatenate([sinp, sinp], 0))
        m = np.arange(128)
        jj = np.arange(8)
        mq = np.arange(128)
        mk = (8 * m[:, None, None] + jj[None, :, None] <= 8 * mq[None, None, :] + c)
        masktab = np.ascontiguousarray(mk.astype(bf).reshape(128, 8 * 128))
        x_c = np.ascontiguousarray(x[:, c::8, :].reshape(2 * TPB, D))
        im = dict(shared)
        im.update(x_sl=x_c, cost=costab, sint=sintb, maskt=masktab)
        in_maps.append(im)
    return in_maps


_NC_CACHE = {}


def kernel(**kw):
    in_maps = _prep_inputs(kw)
    if "nc" not in _NC_CACHE:
        _NC_CACHE["nc"] = build_nc(drains=True)
    nc = _NC_CACHE["nc"]
    res = run_bass_kernel_spmd(nc, in_maps, core_ids=list(range(NCORE)))
    out = np.empty((B, S, D), np.float32)
    for c in range(NCORE):
        o = np.asarray(res.results[c]["out_sl"], np.float32).reshape(B, TPB, D)
        out[:, c::8, :] = o
    return out


# version 26
# speedup vs baseline: 1.1348x; 1.0694x over previous
import math
import numpy as np
import ml_dtypes
import concourse.bass as bass
import concourse.mybir as mybir
from concourse.bass_utils import run_bass_kernel_spmd
from contextlib import ExitStack

F32 = mybir.dt.float32
BF16 = mybir.dt.bfloat16
AF = mybir.ActivationFunctionType
ALU = mybir.AluOpType

B, S, D = 2, 8192, 2048
H, DH, DR = 16, 128, 64
DCKV, DCQ = 512, 1536
E, R = 8, 64
NCORE = 8
TPB = S // NCORE
SCALE = 1.0 / math.sqrt(DH + DR)
EXPB = 12.0
EPS = 1e-5
MOE_SCALING = 4.0

PERM64 = list(range(0, 16)) + list(range(32, 48)) + list(range(16, 32)) + list(range(48, 64))
ROPE_ORDER = [h * 64 + j for h in range(H) for j in PERM64]

ENGS = ("tensor", "vector", "scalar", "sync", "gpsimd")
LN_CUT = None


class Sem:
    def __init__(self, nc, es, name):
        self.h = es.enter_context(nc.semaphore(name))
        self.v = 0

    def bump(self, n=1):
        self.v += n
        return (self.h, self.v, n)

    def cur(self):
        return (self.h, self.v)


class Arena:
    def __init__(self, t, cols):
        self.t, self.cols, self.off, self.stack = t, cols, 0, []

    def alloc(self, n, parts=128):
        assert self.off + n <= self.cols, ("arena overflow", self.off, n, self.cols)
        sl = self.t[0:parts, self.off:self.off + n]
        self.off += n
        return sl

    def mark(self):
        self.stack.append(self.off)

    def reset(self):
        self.off = self.stack.pop()


def build_nc(dbg=False, phases=("p1", "p3", "p4"), p1_subs=None, p1_passes=2, do_cc=True, drains=False):
    nc = bass.Bass("TRN2", target_bir_lowering=False, debug=False)

    def din(name, shape, dt=F32):
        return nc.dram_tensor(name, shape, dt, kind="ExternalInput").ap()

    x_sl = din("x_sl", [2 * TPB, D])
    wkdT = din("wkdT", [D, DCKV])
    wqdT = din("wqdT", [D, DCQ])
    wkuT = din("wkuT", [DCKV, H * DH])
    wvuT = din("wvuT", [DCKV, H * DH])
    wquT = din("wquT", [DCQ, H * DH])
    wkrT = din("wkrT", [D, H * DR])
    wqrT = din("wqrT", [DCQ, H * DR])
    woTr = din("woTr", [H * DH, D])
    aTh = din("aTh", [128, 16 * E * R])
    bwTh = din("bwTh", [64, E * D])
    wgTh = din("wgTh", [128, 16 * E])
    bkd2 = din("bkd2", [128, 4])
    bqd2 = din("bqd2", [128, 12])
    bku2 = din("bku2", [128, 16])
    bvu2 = din("bvu2", [128, 16])
    bqu2 = din("bqu2", [128, 16])
    bkr2 = din("bkr2", [128, 8])
    bqr2 = din("bqr2", [128, 8])
    bo2 = din("bo2", [1, D])
    cost = din("cost", [128, TPB])
    sintab = din("sint", [128, TPB])
    masktab = din("maskt", [128, 8 * 128], BF16)
    identt = din("identt", [128, 128], BF16)
    onest = din("onest", [128, 128], BF16)
    onehot = din("onehot", [8, E * 64], BF16)

    out_sl = nc.dram_tensor("out_sl", [2 * TPB, D], F32, kind="ExternalOutput").ap()

    kscr = "ExternalOutput" if dbg else "Internal"
    def dscr(name, shape):
        if kscr == "Internal":
            return nc.dram_tensor(name, shape, BF16)
        return nc.dram_tensor(name, shape, BF16, kind=kscr)

    qscr_t = dscr("qscr", [3 * 1024, 2 * TPB])
    shard_t = [nc.dram_tensor("shard%d" % b, [5 * 1024, TPB], BF16) for b in range(2)]
    dbgsh_t = None
    if dbg:
        dbgsh_t = [nc.dram_tensor("dbgsh%d" % b, [5 * 1024, TPB], BF16, kind="ExternalOutput")
                   for b in range(2)]
    gath_t = [nc.dram_tensor("gath%d" % b, [NCORE * 5 * 1024, TPB], BF16) for b in range(2)]
    onorm_t = dscr("onormscr", [H * DH, 2 * TPB])
    wscr_t = nc.dram_tensor("wscr", [128, 110592], BF16)
    wscr = wscr_t.ap()
    qscr = qscr_t.ap()
    shard = [t.ap() for t in shard_t]
    gath = [t.ap() for t in gath_t]
    onorm = onorm_t.ap()

    es = ExitStack()
    AB = Arena(es.enter_context(nc.sbuf_tensor("ab16", [128, 73728], BF16)), 73728)
    AFA = Arena(es.enter_context(nc.sbuf_tensor("af32", [128, 8704], F32)), 8704)
    PS = es.enter_context(nc.psum_tensor("ps", [128, 4096], F32))

    sb_id = es.enter_context(nc.sbuf_tensor("sb_id", [128, 128], BF16))
    sb_ones = es.enter_context(nc.sbuf_tensor("sb_ones", [128, 128], BF16))
    sb_mask = es.enter_context(nc.sbuf_tensor("sb_mask", [128, 1024], BF16))
    sb_cos = es.enter_context(nc.sbuf_tensor("sb_cos", [128, TPB], F32))
    sb_sin = es.enter_context(nc.sbuf_tensor("sb_sin", [128, TPB], F32))
    sb_bias = es.enter_context(nc.sbuf_tensor("sb_bias", [128, 80], F32))
    sb_bo = es.enter_context(nc.sbuf_tensor("sb_bo", [1, D], F32))
    sb_bo16 = es.enter_context(nc.sbuf_tensor("sb_bo16", [1, D], BF16))
    sb_oh = es.enter_context(nc.sbuf_tensor("sb_oh", [8, E * 64], BF16))
    sb_sm = es.enter_context(nc.sbuf_tensor("sb_sm", [128, 64], F32))

    BOFF = {}
    _o = 0
    for nm, wdt in [("bkd", 4), ("bqd", 12), ("bku", 16), ("bvu", 16), ("bqu", 16),
                    ("bkr", 8), ("bqr", 8)]:
        BOFF[nm] = _o
        _o += wdt

    sems = {}
    def mksem(name):
        s = Sem(nc, es, name)
        sems[name] = s
        return s
    s_cst = mksem("cst")
    s_x = [mksem("x0"), mksem("x1"), mksem("x2")]
    s_w = [mksem("w0"), mksem("w1")]
    s_kv = [mksem("kv%d" % i) for i in range(4)]
    s_kvg = [mksem("kvg%d" % i) for i in range(4)]
    s_kvs = [mksem("kvs%d" % i) for i in range(4)]
    s_wst = mksem("wst")
    s_w2 = [mksem("w20"), mksem("w21")]
    s_dbg = mksem("dbg")
    s_q = [mksem("q0"), mksem("q1")]
    s_on = [mksem("on0"), mksem("on1")]
    s_stg = mksem("stg")
    s_sh = [mksem("sh0"), mksem("sh1")]
    s_onst = mksem("onst")
    s_out = [mksem("outst0"), mksem("outst1")]
    s_cc = [mksem("cc0"), mksem("cc1")]
    s_t = mksem("t")
    s_v = mksem("v")
    s_s = mksem("s")

    ops = {e: [] for e in ENGS}

    def op(eng, meth, *args, waits=(), inc=None, nodrain=False, **kw):
        ops[eng].append((meth if not nodrain else "!" + meth, args, kw, tuple(waits), inc))

    def wt(eng, *waits):
        ops[eng].append(("_wait", (), {}, tuple(waits), None))

    def tinc(eng, bump):
        meth, args, kw, waits, inc = ops[eng][-1]
        assert inc is None and meth not in ("_wait",), (meth, inc)
        ops[eng][-1] = (meth, args, kw, waits, bump)

    for dst, src in [
        (sb_id[:, :], identt), (sb_ones[:, :], onest), (sb_mask[:, :], masktab),
        (sb_cos[:, :], cost), (sb_sin[:, :], sintab), (sb_bo[:, :], bo2),
        (sb_oh[:, :], onehot),
        (sb_bias[:, BOFF["bkd"]:BOFF["bkd"] + 4], bkd2),
        (sb_bias[:, BOFF["bqd"]:BOFF["bqd"] + 12], bqd2),
        (sb_bias[:, BOFF["bku"]:BOFF["bku"] + 16], bku2),
        (sb_bias[:, BOFF["bvu"]:BOFF["bvu"] + 16], bvu2),
        (sb_bias[:, BOFF["bqu"]:BOFF["bqu"] + 16], bqu2),
        (sb_bias[:, BOFF["bkr"]:BOFF["bkr"] + 8], bkr2),
        (sb_bias[:, BOFF["bqr"]:BOFF["bqr"] + 8], bqr2),
    ]:
        op("sync", "dma_start", out=dst, in_=src, inc=s_cst.bump(16))
    CST = s_cst.cur()
    for e in ("tensor", "scalar", "gpsimd", "vector"):
        wt(e, CST)
    op("vector", "memset", sb_sm[:, 32:33], EPS)
    op("vector", "memset", sb_sm[:, 33:34], -EXPB)
    op("vector", "memset", sb_sm[:, 34:35], 0.0)
    op("vector", "tensor_copy", out=sb_bo16[:, :], in_=sb_bo[:, :], inc=s_v.bump())

    AB.mark(); AFA.mark()
    xln = [AB.alloc(2048) for _ in range(2)]
    xlnT = AB.alloc(16 * 1024)
    kvc = AB.alloc(4 * 1024)
    qcs = AB.alloc(12 * 1024)
    stg = [AB.alloc(1024) for _ in range(3)]
    vfm = [AB.alloc(1024) for _ in range(4)]
    krr = [AB.alloc(1024) for _ in range(2)]
    krs = AB.alloc(1024)
    rop = [AB.alloc(1024) for _ in range(2)]
    wbuf = [AB.alloc(12288) for _ in range(2)]
    xf = [AFA.alloc(2048) for _ in range(3)]
    rtmp = [AFA.alloc(1024) for _ in range(2)]
    bnst = sb_sm[:, 0:24]
    mv = sb_sm[:, 24:26]
    rstd = sb_sm[:, 26:27]
    nmr = sb_sm[:, 27:28]
    negr = sb_sm[:, 35:36]

    def xlnT_dc(dc):
        return xlnT[:, dc * 1024:(dc + 1) * 1024]

    st_ = dict(
        x_snap=[None, None, None],
        xln_free=[None, None],
        tp_evac=[None, None],
        proj_evac=[None, None],
        kvc_done=None, qcs_done=None, xlnT_done=None,
        wfree=[None, None],
        wtog=0, pjpar=0, stgi=0,
        vfm_pend=[], vfm_free=[None] * 4,
        vt_par=0, vt_evac=[None, None], wscr_off=0, wscr_map={}, wscr_stc={},
        wst_slot=[None, None], ln_s_snap=None, xlnT_half0=0,
        kr_free=[None, None],
        ropi=0, rop_store_sem=[None, None],
    )

    def emit_v_transposes(bt):
        chunks = list(st_["vfm_pend"])
        st_["vfm_pend"].clear()
        fbase = chunks[0][0]
        lastsnap = chunks[-1][1]
        dcol0 = fbase * 128
        half = dcol0 // 1024
        col0 = dcol0 % 1024
        for tc in range(8):
            par = st_["vt_par"] % 2
            st_["vt_par"] += 1
            pb = PS[:, 2048 + par * 512: 2048 + par * 512 + 512].bitcast(BF16)
            tw = [(s_s.h, lastsnap)]
            if st_["vt_evac"][par] is not None:
                tw.append((s_s.h, st_["vt_evac"][par][1]))
            if st_["tp_evac"][par] is not None:
                tw.append((s_s.h, st_["tp_evac"][par]))
            for j, (fcg, _) in enumerate(chunks):
                op("tensor", "transpose", pb[:, j * 128:(j + 1) * 128],
                   vfm[fcg % 4][:, tc * 128:(tc + 1) * 128], sb_id[:, :],
                   waits=(tw if j == 0 else ()))
            tinc("tensor", s_t.bump())
            tsn = s_t.v
            if tc == 7:
                for fcg, _ in chunks:
                    st_["vfm_free"][fcg % 4] = tsn
            ssl = st_["stgi"] % 3
            st_["stgi"] += 1
            sdst = stg[ssl][:, 0:512]
            ew = [(s_t.h, tsn), s_sh[bt].cur(), s_stg.cur()]
            op("scalar", "copy", out=sdst, in_=pb[:, 0:512], waits=ew, inc=s_s.bump(),
               nodrain=True)
            st_["vt_evac"][par] = ("s", s_s.v)
            evs = s_s.v
            dview = shard[bt][3072 + 256 * tc: 3072 + 256 * (tc + 1), :].rearrange(
                "(m two) c -> m two c", two=2)[:, half, col0:col0 + 512]
            op("sync", "dma_start", out=dview, in_=sdst, waits=[(s_s.h, evs)],
               inc=s_sh[bt].bump(16))

    def emit_rope(bt, sname, fcg, evsnap):
        sl = fcg % 2
        rsl = st_["ropi"] % 2
        st_["ropi"] += 1
        shmask = [(i + 16) % 32 for i in range(32)]
        op("vector", "stream_shuffle", krs[:, :], krr[sl][:, :], shmask,
           waits=[(s_s.h, evsnap)])
        op("vector", "tensor_tensor", out=rtmp[0], in0=krr[sl][:, :], in1=sb_cos[:, :], op=ALU.mult,
           nodrain=True)
        op("vector", "tensor_tensor", out=rtmp[1], in0=krs[:, :], in1=sb_sin[:, :], op=ALU.mult)
        rw = []
        prev = st_["rop_store_sem"][rsl]
        if prev is not None:
            rw.append(prev.cur())
        op("vector", "tensor_tensor", out=rop[rsl][:, :], in0=rtmp[0], in1=rtmp[1], op=ALU.add,
           waits=rw, inc=s_v.bump())
        st_["kr_free"][sl] = s_v.v
        rsnap = s_v.v
        if sname in ("krA", "krB"):
            dview = shard[bt][2048 + fcg * 128: 2048 + (fcg + 1) * 128, :]
            sem = s_sh[bt]
        else:
            dview = qscr[2048 + fcg * 128: 2048 + (fcg + 1) * 128, bt * TPB:(bt + 1) * TPB]
            sem = s_stg
        st_["rop_store_sem"][rsl] = sem
        op("sync", "dma_start", out=dview, in_=rop[rsl][:, :], waits=[(s_v.h, rsnap)],
           inc=sem.bump(16))

    SUBS = [
        ("kvc", wkdT, 16, 0, 512, "xlnT", "bkd"),
        ("qcA", wqdT, 16, 0, 768, "xlnT", "bqd"),
        ("qcB", wqdT, 16, 768, 768, "xlnT", "bqd"),
        ("krA", wkrT, 16, 0, 512, "xlnT", "bkr"),
        ("krB", wkrT, 16, 512, 512, "xlnT", "bkr"),
        ("qr", wqrT, 12, 0, 1024, "qcs", "bqr"),
        ("v", wvuT, 4, 0, 2048, "kvc", "bvu"),
        ("kcnt", wkuT, 4, 0, 2048, "kvc", "bku"),
        ("qcntA", wquT, 12, 0, 1024, "qcs", "bqu"),
        ("qcntB", wquT, 12, 1024, 1024, "qcs", "bqu"),
    ]

    def p0_precast():
        off = 0
        for k, (sname, wap, nd, fcol0, FW, _src, _b) in enumerate(SUBS):
            wcols = nd * FW
            st_["wscr_map"][sname] = off
            st_["wscr_stc"][sname] = 16 * (k + 1)
            wsl = k % 2
            ww = []
            if k >= 2:
                ww.append(s_wst.cur())
            ndh = max(1, nd // 2)
            op("gpsimd", "dma_start",
               out=wbuf[wsl][:, 0:ndh * FW].rearrange("p (g f) -> p g f", g=ndh),
               in_=wap.rearrange("(g p) f -> p g f", p=128)[:, 0:ndh, fcol0:fcol0 + FW],
               waits=ww, inc=s_w[wsl].bump(16))
            if nd > ndh:
                op("gpsimd", "dma_start",
                   out=wbuf[wsl][:, ndh * FW:nd * FW].rearrange("p (g f) -> p g f", g=nd - ndh),
                   in_=wap.rearrange("(g p) f -> p g f", p=128)[:, ndh:nd, fcol0:fcol0 + FW],
                   waits=ww, inc=s_w[wsl].bump(16))
            op("gpsimd", "dma_start", out=wscr[:, off:off + wcols],
               in_=wbuf[wsl][:, 0:wcols],
               waits=[s_w[wsl].cur(), s_wst.cur()], inc=s_wst.bump(16))
            off += wcols

    def p1_pass(bt):
        for tc in range(8):
            sl = tc % 3
            sl2 = tc % 2
            w = []
            if st_["x_snap"][sl] is not None:
                w.append((s_s.h, st_["x_snap"][sl][1]))
            op("sync", "dma_start", out=xf[sl],
               in_=x_sl[bt * TPB + tc * 128: bt * TPB + (tc + 1) * 128, :],
               waits=w, inc=s_x[sl].bump(16))
            if LN_CUT == "x":
                st_["x_snap"][sl] = None
                continue
            wt("vector", s_x[sl].cur())
            if st_["ln_s_snap"] is not None:
                wt("vector", (s_s.h, st_["ln_s_snap"]))
            for k in range(4):
                op("vector", "bn_stats", bnst[:, 6 * k:6 * k + 6], xf[sl][:, 512 * k:512 * (k + 1)],
                   nodrain=(k > 0))
            op("vector", "bn_aggr", mv[:, :], bnst[:, :], inc=s_v.bump())
            if LN_CUT == "bn":
                st_["x_snap"][sl] = s_v.v
                continue
            wt("scalar", (s_v.h, s_v.v))
            op("scalar", "activation", out=rstd[:, :], in_=mv[:, 1:2], func=AF.Ln, bias=sb_sm[:, 32:33], scale=1.0)
            op("scalar", "activation", out=rstd[:, :], in_=rstd[:, :], func=AF.Exp, bias=0.0,
               scale=-0.5)
            op("scalar", "activation", out=negr[:, :], in_=rstd[:, :], func=AF.Identity,
               bias=sb_sm[:, 34:35], scale=-1.0)
            op("scalar", "activation", out=nmr[:, :], in_=mv[:, 0:1], func=AF.Identity,
               bias=sb_sm[:, 34:35], scale=negr[:, 0:1])
            xw = []
            if st_["xln_free"][sl2] is not None:
                xw.append((s_t.h, st_["xln_free"][sl2]))
            op("scalar", "activation", out=xln[sl2][:, :], in_=xf[sl], func=AF.Identity,
               bias=nmr[:, 0:1], scale=rstd[:, 0:1], waits=xw, inc=s_s.bump())
            st_["x_snap"][sl] = ("s", s_s.v)
            st_["ln_s_snap"] = s_s.v
            xlnsnap = ("s", s_s.v)
            if LN_CUT == "xln":
                st_["xln_free"][sl2] = None
                continue
            for g4 in range(4):
                par = g4 % 2
                pb = PS[:, 2048 + par * 512: 2048 + par * 512 + 512].bitcast(BF16)
                tw = [(s_s.h, xlnsnap[1])]
                if st_["tp_evac"][par] is not None:
                    tw.append((s_s.h, st_["tp_evac"][par]))
                if st_["vt_evac"][par] is not None:
                    tw.append((s_s.h, st_["vt_evac"][par][1]))
                for j in range(4):
                    dc = g4 * 4 + j
                    op("tensor", "transpose", pb[:, j * 128:(j + 1) * 128],
                       xln[sl2][:, dc * 128:(dc + 1) * 128], sb_id[:, :],
                       waits=(tw if j == 0 else ()))
                tinc("tensor", s_t.bump())
                tsn = s_t.v
                if LN_CUT == "tpv":
                    op("vector", "tensor_copy", out=xlnT[:, g4 * 512:(g4 + 1) * 512],
                       in_=pb[:, 0:512], waits=[(s_t.h, tsn)], inc=s_v.bump())
                    st_["tp_evac"][par] = None
                    stv_snap = s_v.v
                    op("scalar", "_wait_only") if False else None
                    st_["vt_evac"][par] = stv_snap
                    continue
                if LN_CUT == "tps":
                    op("scalar", "copy", out=xlnT[:, g4 * 512:(g4 + 1) * 512],
                       in_=pb[:, 0:512], waits=[(s_t.h, tsn)], inc=s_s.bump())
                    st_["tp_evac"][par] = s_s.v
                    continue
                dview = xlnT.rearrange("p (c t) -> p c t", c=16)[:, g4 * 4:g4 * 4 + 4,
                                                                tc * 128:(tc + 1) * 128]
                op("scalar", "copy", out=dview,
                   in_=pb[:, 0:512].rearrange("p (a b) -> p a b", a=4),
                   waits=[(s_t.h, tsn)], inc=s_s.bump(), nodrain=True)
                st_["tp_evac"][par] = s_s.v
            st_["xln_free"][sl2] = s_t.v
            if tc == 3:
                st_["xlnT_half0"] = s_s.v
        st_["xlnT_done"] = s_s.v

        if p1_subs is not None and len(p1_subs) == 0:
            return
        subs = list(SUBS)
        if p1_subs is not None:
            subs = [s for s in subs if s[0] in p1_subs]
        for sname, wap, nd, fcol0, FW, src, bnm in subs:
            wsl = st_["wtog"] % 2
            st_["wtog"] += 1
            ww = []
            if st_["wfree"][wsl] is not None:
                ww.append((s_t.h, st_["wfree"][wsl]))
            wcols = nd * FW
            ndh = max(1, nd // 2)
            op("gpsimd", "dma_start",
               out=wbuf[wsl][:, 0:ndh * FW].rearrange("p (g f) -> p g f", g=ndh),
               in_=wap.rearrange("(g p) f -> p g f", p=128)[:, 0:ndh, fcol0:fcol0 + FW],
               waits=ww, inc=s_w[wsl].bump(16))
            if nd > ndh:
                op("gpsimd", "dma_start",
                   out=wbuf[wsl][:, ndh * FW:nd * FW].rearrange("p (g f) -> p g f", g=nd - ndh),
                   in_=wap.rearrange("(g p) f -> p g f", p=128)[:, ndh:nd, fcol0:fcol0 + FW],
                   waits=ww, inc=s_w[wsl].bump(16))
            wsem = s_w[wsl].cur()
            nf = FW // 128
            for fc in range(nf):
                fcg = (fcol0 // 128) + fc
                par = st_["pjpar"] % 2
                st_["pjpar"] += 1
                pdst = PS[:, par * 1024:(par + 1) * 1024]
                mw = [wsem]
                if st_["proj_evac"][par] is not None:
                    mw.append((s_s.h, st_["proj_evac"][par]))
                mwh = list(mw)
                if src == "xlnT":
                    mw.append((s_s.h, st_["xlnT_half0"]))
                    mwh.append((s_s.h, st_["xlnT_done"]))
                elif src == "kvc":
                    mw.append((s_s.h, st_["kvc_done"]))
                else:
                    mw.append((s_s.h, st_["qcs_done"]))
                for qh in range(2):
                    for dc in range(nd):
                        if src == "xlnT":
                            rhs_full = xlnT_dc(dc)
                        elif src == "kvc":
                            rhs_full = kvc[:, dc * 1024:(dc + 1) * 1024]
                        else:
                            rhs_full = qcs[:, dc * 1024:(dc + 1) * 1024]
                        lhs = wbuf[wsl][:, dc * FW + fc * 128: dc * FW + fc * 128 + 128]
                        w_ = mw if qh == 0 else mwh
                        op("tensor", "matmul", pdst[:, qh * 512:(qh + 1) * 512], lhs,
                           rhs_full[:, qh * 512:(qh + 1) * 512],
                           start=(dc == 0), stop=(dc == nd - 1),
                           waits=(w_ if dc == 0 else ()))
                tinc("tensor", s_t.bump())
                mmsnap = s_t.v
                bap = sb_bias[:, BOFF[bnm] + fcg: BOFF[bnm] + fcg + 1]
                ew = [(s_t.h, mmsnap)]
                ssl = None
                if sname == "kvc":
                    edst = kvc[:, fcg * 1024:(fcg + 1) * 1024]
                elif sname in ("qcA", "qcB"):
                    edst = qcs[:, fcg * 1024:(fcg + 1) * 1024]
                elif sname in ("krA", "krB", "qr"):
                    edst = krr[fcg % 2]
                    if st_["kr_free"][fcg % 2] is not None:
                        ew.append((s_v.h, st_["kr_free"][fcg % 2]))
                elif sname == "v":
                    edst = vfm[fcg % 4]
                    if st_["vfm_free"][fcg % 4] is not None:
                        ew.append((s_t.h, st_["vfm_free"][fcg % 4]))
                else:
                    ssl = st_["stgi"] % 3
                    st_["stgi"] += 1
                    edst = stg[ssl]
                    ew.append(s_sh[bt].cur())
                    ew.append(s_stg.cur())
                op("scalar", "activation", out=edst, in_=pdst, func=AF.Identity,
                   bias=bap, scale=1.0, waits=ew, inc=s_s.bump(), nodrain=True)
                st_["proj_evac"][par] = s_s.v
                evsnap = s_s.v
                if sname == "kvc" and fcg == 3:
                    st_["kvc_done"] = evsnap
                if sname == "qcB" and fcg == 11:
                    st_["qcs_done"] = evsnap
                if sname == "kcnt":
                    op("sync", "dma_start", out=shard[bt][fcg * 128:(fcg + 1) * 128, :],
                       in_=edst, waits=[(s_s.h, evsnap)], inc=s_sh[bt].bump(16))
                elif sname in ("qcntA", "qcntB"):
                    op("sync", "dma_start",
                       out=qscr[fcg * 128:(fcg + 1) * 128, bt * TPB:(bt + 1) * TPB],
                       in_=edst, waits=[(s_s.h, evsnap)], inc=s_stg.bump(16))
                elif sname == "v":
                    st_["vfm_pend"].append((fcg, evsnap))
                    if len(st_["vfm_pend"]) == 4:
                        emit_v_transposes(bt)
                elif sname in ("krA", "krB", "qr"):
                    emit_rope(bt, sname, fcg, evsnap)
            st_["wfree"][wsl] = s_t.v

    defer_cc1 = do_cc and ("p3" in phases) and p1_passes > 1
    cc1_w = [None]
    for bt in range(p1_passes):
        p1_pass(bt)
        if do_cc:
            ccw = [s_sh[bt].cur()]
            if bt == 1 and defer_cc1:
                cc1_w[0] = ccw
                continue
            op("gpsimd", "collective", kind="AllGather",
               cins=[shard_t[bt].ap()], couts=[gath_t[bt].ap()],
               waits=ccw, inc=s_cc[bt].bump(1))
        if dbg:
            op("gpsimd", "dma_start", out=dbgsh_t[bt].ap(), in_=shard[bt],
               waits=[s_sh[bt].cur()], inc=s_dbg.bump(16))
    QS_TOT = s_stg.cur()
    if dbg:
        wt("gpsimd", s_dbg.cur())
    if "p3" not in phases:
        wt("sync", QS_TOT, s_sh[0].cur(), s_sh[1].cur())
        wt("vector", (s_v.h, s_v.v))
        wt("scalar", (s_s.h, s_s.v))
        wt("tensor", (s_t.h, s_t.v))
        if do_cc:
            wt("gpsimd", (s_cc[0].h, 1) if p1_passes > 0 else (s_cc[0].h, 0))
            if p1_passes > 1:
                wt("gpsimd", (s_cc[1].h, 1))

    if "p3" not in phases:
        return _finish(nc, es, ops, ENGS, NCORE, drains)
    AB.reset(); AFA.reset()
    AB.mark(); AFA.mark()
    qbuf = [AB.alloc(6 * 1024) for _ in range(2)]
    kvb = [AB.alloc(10240) for _ in range(4)]
    ptil = [AB.alloc(1024) for _ in range(6)]
    onstg = [AB.alloc(1024) for _ in range(4)]
    lsb16s = [AB.alloc(1024, parts=1) for _ in range(4)]
    osbb = AFA.alloc(4096)
    lsbb = AFA.alloc(4096, parts=1)

    OO, LL = 2048, 3072
    P1T, P1V, P1S = s_t.cur(), s_v.cur(), s_s.cur()
    wt("sync", P1T, P1V, P1S)
    wt("gpsimd", P1T, P1V, P1S)

    a_ = dict(scpar=0, ptc=0, pt_cons=[None] * 6, sc_evac=[None] * 4,
              ol_free=None, bc_evac=None, kv_free=[None] * 4, q_free=[None, None],
              kv_snap=[None] * 4, q_snap=[None, None],
              oni=0, sweep=0, kvctr=0, pend_onst=[], lp_snap=[None, None], lp_free=None, ptil_guard=None)
    SWEEPS = [(bt, g) for bt in range(2) for g in range(4)]
    KVPOS = [(bt, g, bi) for (bt, g) in SWEEPS for bi in range(8)]

    def kv_load(bt, g, bi, slot):
        h0 = g * 4
        w = [(s_cc[bt].h, 1)] if do_cc else []
        if a_["kv_free"][slot] is not None:
            w.append((s_t.h, a_["kv_free"][slot]))
        gb = gath[bt].rearrange("(r f) c -> r f c", f=5120)
        for hh in range(4):
            h = h0 + hh
            src = gb[:, h * 128:(h + 1) * 128, bi * 128:(bi + 1) * 128]
            dst = kvb[slot][:, hh * 1024:(hh + 1) * 1024].rearrange("p (r t) -> p r t", r=8)
            if hh < 2:
                op("sync", "dma_start", out=dst, in_=src.rearrange("r f c -> f r c"),
                   waits=(w if hh == 0 else ()), inc=s_kv[slot].bump(16))
            else:
                op("sync", "dma_start", out=dst, in_=src.rearrange("r f c -> f r c"),
                   waits=(w if hh == 2 else ()), inc=s_kvs[slot].bump(16))
        for pp in range(2):
            pj = g * 2 + pp
            src = gb[:, 2048 + pj * 128: 2048 + (pj + 1) * 128, bi * 128:(bi + 1) * 128]
            dst = kvb[slot][:, 4096 + pp * 1024: 4096 + (pp + 1) * 1024].rearrange(
                "p (r t) -> p r t", r=8)
            op("gpsimd", "dma_start", out=dst, in_=src.rearrange("r f c -> f r c"),
               waits=(w if pp == 0 else ()), inc=s_kvg[slot].bump(16))
        dcol0 = h0 * 128
        half = dcol0 // 1024
        col0 = dcol0 % 1024
        src = gb[:, 3072 + 256 * bi: 3072 + 256 * (bi + 1), :].rearrange(
            "r (m two) c -> r m two c", two=2)[:, :, half, col0:col0 + 512]
        dst = kvb[slot][:, 6144:6144 + 4096].rearrange("p (r d) -> p r d", r=8)
        op("gpsimd", "dma_start", out=dst, in_=src.rearrange("r m d -> m r d"),
           inc=s_kvg[slot].bump(16))
        a_["kv_snap"][slot] = (s_kv[slot].cur(), s_kvg[slot].cur(), s_kvs[slot].cur())

    def q_load(bt, g, slot):
        w = [QS_TOT]
        if a_["q_free"][slot] is not None:
            w.append((s_t.h, a_["q_free"][slot]))
        for hh in range(4):
            h = g * 4 + hh
            op("sync", "dma_start", out=qbuf[slot][:, hh * 1024:(hh + 1) * 1024],
               in_=qscr[h * 128:(h + 1) * 128, bt * TPB:(bt + 1) * TPB],
               waits=(w if hh == 0 else ()), inc=s_q[slot].bump(16))
        for pp in range(2):
            pj = g * 2 + pp
            op("sync", "dma_start", out=qbuf[slot][:, 4096 + pp * 1024: 4096 + (pp + 1) * 1024],
               in_=qscr[2048 + pj * 128: 2048 + (pj + 1) * 128, bt * TPB:(bt + 1) * TPB],
               inc=s_q[slot].bump(16))
        a_["q_snap"][slot] = s_q[slot].cur()

    def flush_onst():
        for (h, bt2, osl, onsnap) in a_["pend_onst"]:
            op("sync", "dma_start",
               out=onorm[h * 128:(h + 1) * 128, bt2 * TPB:(bt2 + 1) * TPB],
               in_=onstg[osl][:, :], waits=[(s_v.h, onsnap)], inc=s_onst.bump(16))
            if bt2 == 0:
                a_["onst_b0"] = s_onst.v
        a_["pend_onst"] = []

    def attention_sweep(bt, g, nxt=None):
        si = a_["sweep"]
        a_["sweep"] += 1
        qsl = si % 2
        if si == 0:
            q_load(bt, g, qsl)
            kv_load(*KVPOS[0], 0)
            kv_load(*KVPOS[1], 1)
            kv_load(*KVPOS[2], 2)
        qsem = a_["q_snap"][qsl]
        for bi in range(8):
            if si == 1 and bi == 0 and cc1_w[0] is not None:
                op("gpsimd", "collective", kind="AllGather",
                   cins=[shard_t[1].ap()], couts=[gath_t[1].ap()],
                   waits=cc1_w[0], inc=s_cc[1].bump(1))
                cc1_w[0] = None
            pf = si * 8 + bi + 3
            if pf < len(KVPOS):
                kv_load(*KVPOS[pf], pf % 4)
            if bi == 5 and nxt is not None:
                q_load(nxt[0], nxt[1], (si + 1) % 2)
            ksl = (si * 8 + bi) % 4
            if bi == 1:
                flush_onst()
            kvsem, kvgsem, kvssem = a_["kv_snap"][ksl]
            Nq = 1024 - 128 * bi
            nqt = (Nq + 511) // 512
            qoff = 128 * bi
            for pair in range(2):
                for qh in range(nqt):
                    wq = min(512, Nq - qh * 512)
                    q0 = qoff + qh * 512
                    pend = []

                    def emit_attnv(j, lh, psl, msnap, wq=wq, q0=q0, sem=None):
                        hh = pair * 2 + lh
                        aw = [((sem or s_v).h, msnap)]
                        if j == 0 and a_["ol_free"] is not None:
                            aw.append((s_v.h, a_["ol_free"]))
                        vlhs = kvb[ksl][:, 6144 + j * 512 + hh * 128:
                                        6144 + j * 512 + hh * 128 + 128]
                        op("tensor", "matmul",
                           PS[0:1, 3072 + lh * 512: 3072 + lh * 512 + wq],
                           sb_ones[:, 0:1], ptil[psl][:, 0:wq],
                           start=(j == 0), stop=(j == 7), waits=aw)
                        op("tensor", "matmul",
                           PS[:, 2048 + lh * 512: 2048 + lh * 512 + wq],
                           vlhs, ptil[psl][:, 0:wq],
                           start=(j == 0), stop=(j == 7))
                        tinc("tensor", s_t.bump())
                        a_["pt_cons"][psl] = s_t.v

                    for j in range(8):
                        par = j % 2
                        w = [kvsem, kvgsem, kvssem, qsem]
                        for lh in range(2):
                            if a_["sc_evac"][lh * 2 + par] is not None:
                                w.append((s_s.h, a_["sc_evac"][lh * 2 + par]))
                        if a_["bc_evac"] is not None:
                            w.append((s_v.h, a_["bc_evac"]))
                        for lh in range(2):
                            hh = pair * 2 + lh
                            lhs_n = kvb[ksl][:, hh * 1024 + j * 128: hh * 1024 + j * 128 + 128]
                            rq_n = qbuf[qsl][:, hh * 1024 + q0: hh * 1024 + q0 + wq]
                            op("tensor", "matmul",
                               PS[:, (lh * 2 + par) * 512: (lh * 2 + par) * 512 + wq],
                               lhs_n, rq_n, start=True, stop=False,
                               waits=(w if lh == 0 else ()))
                        for lh in range(2):
                            hh = pair * 2 + lh
                            half = hh % 2
                            lhs_r = kvb[ksl][64 * half:64 * half + 64,
                                             4096 + pair * 1024 + j * 128:
                                             4096 + pair * 1024 + j * 128 + 128]
                            rq_r = qbuf[qsl][64 * half:64 * half + 64,
                                             4096 + pair * 1024 + q0:
                                             4096 + pair * 1024 + q0 + wq]
                            op("tensor", "matmul",
                               PS[:, (lh * 2 + par) * 512: (lh * 2 + par) * 512 + wq],
                               lhs_r, rq_r, start=False, stop=True)
                        tinc("tensor", s_t.bump())
                        scsnap = s_t.v
                        for lh in range(2):
                            psl = a_["ptc"] % 6
                            a_["ptc"] += 1
                            ew = [(s_t.h, scsnap)]
                            if a_["pt_cons"][psl] is not None:
                                ew.append((s_t.h, a_["pt_cons"][psl]))
                            op("scalar", "activation", out=ptil[psl][:, 0:wq],
                               in_=PS[:, (lh * 2 + par) * 512: (lh * 2 + par) * 512 + wq],
                               func=AF.Exp, bias=sb_sm[:, 33:34], scale=SCALE,
                               waits=ew, inc=s_s.bump(), nodrain=True)
                            a_["sc_evac"][lh * 2 + par] = s_s.v
                            expsnap = s_s.v
                            if qh == 0:
                                op("vector", "tensor_tensor", out=ptil[psl][:, 0:128],
                                   in0=ptil[psl][:, 0:128],
                                   in1=sb_mask[:, j * 128:(j + 1) * 128],
                                   op=ALU.mult, waits=[(s_s.h, expsnap)], inc=s_v.bump(),
                                   nodrain=True)
                                msnap = s_v.v
                            else:
                                msnap = None
                            pend.append((j, lh, psl, msnap if msnap is not None else expsnap))
                            if msnap is None:
                                pass
                        while len(pend) > 4:
                            jj, lh2, psl2, sn2 = pend.pop(0)
                            emit_attnv(jj, lh2, psl2, sn2,
                                       sem=(s_v if qh == 0 else s_s))
                    while pend:
                        jj, lh2, psl2, sn2 = pend.pop(0)
                        emit_attnv(jj, lh2, psl2, sn2,
                                   sem=(s_v if qh == 0 else s_s))
                    avsnap = s_t.v
                    o_out = osbb[:, pair * 2048: pair * 2048 + 2048].rearrange(
                        "p (g c) -> p g c", g=2)[:, :, q0:q0 + wq]
                    o_in = PS[:, 2048:3072].rearrange("p (g c) -> p g c", g=2)[:, :, 0:wq]
                    l_out = lsbb[0:1, pair * 2048: pair * 2048 + 2048].rearrange(
                        "p (g c) -> p g c", g=2)[:, :, q0:q0 + wq]
                    l_in = PS[0:1, 3072:4096].rearrange("p (g c) -> p g c", g=2)[:, :, 0:wq]
                    if bi == 0:
                        op("vector", "tensor_copy", out=o_out, in_=o_in,
                           waits=[(s_t.h, avsnap)], nodrain=True)
                        op("vector", "tensor_copy", out=l_out, in_=l_in,
                           inc=s_v.bump(), nodrain=True)
                    else:
                        op("vector", "tensor_tensor", out=o_out, in0=o_out, in1=o_in,
                           op=ALU.add, waits=[(s_t.h, avsnap)], nodrain=True)
                        op("vector", "tensor_tensor", out=l_out, in0=l_out, in1=l_in,
                           op=ALU.add, inc=s_v.bump(), nodrain=True)
                    a_["ol_free"] = s_v.v
            a_["kv_free"][ksl] = s_t.v
        a_["q_free"][qsl] = s_t.v
        rsnaps = []
        for hh in range(4):
            lview = lsbb[0:1, hh * 1024:(hh + 1) * 1024]
            op("vector", "reciprocal", lview, lview, nodrain=(hh > 0))
            op("vector", "tensor_copy", out=lsb16s[hh][0:1, :], in_=lview,
               inc=s_v.bump())
            rsnaps.append(s_v.v)
        mult_snap = [None, None]
        for hh in range(4):
            h = g * 4 + hh
            pb_off = (hh % 2) * 1024
            bw = [(s_v.h, rsnaps[hh])]
            if mult_snap[hh % 2] is not None:
                bw.append((s_v.h, mult_snap[hh % 2]))
            elif a_["bc_evac"] is not None:
                bw.append((s_v.h, a_["bc_evac"]))
            for k4 in range(4):
                if a_["sc_evac"][k4] is not None:
                    bw.append((s_s.h, a_["sc_evac"][k4]))
            for qt in range(2):
                op("tensor", "matmul", PS[:, pb_off + qt * 512: pb_off + qt * 512 + 512],
                   sb_ones[0:1, :], lsb16s[hh][0:1, qt * 512: qt * 512 + 512],
                   start=True, stop=True, waits=(bw if qt == 0 else ()))
            tinc("tensor", s_t.bump())
            bsnap = s_t.v
            osl = a_["oni"] % 4
            a_["oni"] += 1
            ow = [(s_t.h, bsnap), s_onst.cur()]
            op("vector", "tensor_tensor", out=onstg[osl][:, :],
               in0=osbb[:, hh * 1024:(hh + 1) * 1024],
               in1=PS[:, pb_off: pb_off + 1024], op=ALU.mult, waits=ow, inc=s_v.bump())
            mult_snap[hh % 2] = s_v.v
            a_["bc_evac"] = s_v.v
            onsnap = s_v.v
            a_["pend_onst"].append((h, bt, osl, onsnap))

    for sidx, (bt, g) in enumerate(SWEEPS):
        nxt = SWEEPS[sidx + 1] if sidx + 1 < len(SWEEPS) else None
        attention_sweep(bt, g, nxt=nxt)
    flush_onst()
    ON_TOT = s_onst.cur()
    ON_B0 = (s_onst.h, a_.get("onst_b0", s_onst.v))
    if "p4" not in phases:
        wt("sync", ON_TOT)

    if "p4" not in phases:
        return _finish(nc, es, ops, ENGS, NCORE, drains)
    AB.reset(); AFA.reset()
    AB.mark(); AFA.mark()
    woT = AB.alloc(16 * 2048)
    onb = [AB.alloc(2048) for _ in range(2)]
    x2l = [AB.alloc(2048) for _ in range(2)]
    x2T = AB.alloc(16 * 256)
    aTb = AB.alloc(16 * E * R)
    bwTb = AB.alloc(E * D, parts=64)
    wgTb = AB.alloc(16 * E)
    ghb = AB.alloc(E * 256, parts=64)
    gbc = AB.alloc(256, parts=64)
    gtb = AB.alloc(256, parts=8)
    scr8 = AB.alloc(8)
    xf4 = [AFA.alloc(2048) for _ in range(2)]
    xnw = [AFA.alloc(2048) for _ in range(2)]

    P3T, P3V, P3S = s_t.cur(), s_v.cur(), s_s.cur()
    wt("sync", P3T, P3V, P3S)
    wt("gpsimd", P3T, P3V, P3S)

    op("gpsimd", "dma_start", out=woT[:, 0:8 * 2048].rearrange("p (c d) -> p c d", c=8),
       in_=woTr.rearrange("(c p) d -> p c d", p=128)[:, 0:8, :], inc=s_w[0].bump(16))
    P4WA = s_w[0].cur()
    op("gpsimd", "dma_start", out=woT[:, 8 * 2048:].rearrange("p (c d) -> p c d", c=8),
       in_=woTr.rearrange("(c p) d -> p c d", p=128)[:, 8:16, :], inc=s_w[0].bump(16))
    P4WB = s_w[0].cur()
    op("gpsimd", "dma_start", out=aTb, in_=aTh, inc=s_w[0].bump(16))
    op("gpsimd", "dma_start", out=bwTb, in_=bwTh, inc=s_w[0].bump(16))
    op("gpsimd", "dma_start", out=wgTb, in_=wgTh, inc=s_w[0].bump(16))
    P4W = s_w[0].cur()

    f_ = dict(on_free=[None, None], x_free=[None, None], tp_evac=[None, None],
              x2l_free=[None, None], x2T_cons=None, logit_free=None,
              gt_free=None, gbc_free=None, h_free=[None, None],
              py_free=[None, None], xn_store=[None, None], op_free=None,
              y67_free=None, xnw_snap=[None, None])

    def p4_loads(bt, gg):
        for ii, tt in enumerate((gg * 2, gg * 2 + 1)):
            r0 = bt * TPB + tt * 128
            osl = tt % 2
            xsl = tt % 2
            w = [ON_B0 if bt == 0 else ON_TOT]
            if f_["on_free"][osl] is not None:
                w.append((s_t.h, f_["on_free"][osl]))
            src2 = onorm.rearrange("(c p) t -> p c t", p=128)[:, :, r0: r0 + 128]
            onv = onb[osl].rearrange("p (c t) -> p c t", c=16)
            op("sync", "dma_start", out=onv[:, 0:8, :], in_=src2[:, 0:8, :],
               waits=w, inc=s_on[osl].bump(16))
            op("sync", "dma_start", out=onv[:, 8:16, :], in_=src2[:, 8:16, :],
               inc=s_on[osl].bump(16))
            xw = []
            if f_["x_free"][xsl] is not None:
                xw.append((s_v.h, f_["x_free"][xsl]))
            op("sync", "dma_start", out=xf4[xsl], in_=x_sl[r0: r0 + 128, :], waits=xw,
               inc=s_x[xsl].bump(16))

    def p4_opmm(bt, gg, ii):
        tt = gg * 2 + ii
        osl = tt % 2
        mw = [s_on[osl].cur(), P4WA]
        if f_["op_free"] is not None:
            mw.append((s_v.h, f_["op_free"]))
        for hc in range(16):
            for dt in range(4):
                w_ = ()
                if hc == 0 and dt == 0:
                    w_ = mw
                elif hc == 8 and dt == 0:
                    w_ = [P4WB]
                op("tensor", "matmul", PS[:, dt * 512: dt * 512 + 512],
                   onb[osl][:, hc * 128:(hc + 1) * 128],
                   woT[:, hc * 2048 + dt * 512: hc * 2048 + dt * 512 + 512],
                   start=(hc == 0), stop=False,
                   waits=w_)
        for dt in range(4):
            op("tensor", "matmul", PS[:, dt * 512: dt * 512 + 512],
               sb_ones[0:1, 0:128], sb_bo16[0:1, dt * 512:(dt + 1) * 512],
               start=False, stop=True)
        tinc("tensor", s_t.bump())
        f_["on_free"][osl] = s_t.v
        f_["opmm_snap_%d" % ii] = s_t.v

    def p4_post_add(bt, gg, ii):
        tt = gg * 2 + ii
        xsl = tt % 2
        mmsnap = f_["opmm_snap_%d" % ii]
        if f_["xn_store"][ii] is not None:
            wt("vector", (s_out[ii].h, f_["xn_store"][ii]))
        vw = [(s_t.h, mmsnap), s_x[xsl].cur()]
        for dt in range(4):
            last = dt == 3
            op("vector", "tensor_tensor", out=xnw[ii][:, dt * 512:(dt + 1) * 512],
               in0=xf4[xsl][:, dt * 512:(dt + 1) * 512], in1=PS[:, dt * 512: dt * 512 + 512],
               op=ALU.add, waits=(vw if dt == 0 else ()),
               inc=(s_v.bump() if last else None), nodrain=(dt > 0))
        f_["x_free"][xsl] = s_v.v
        f_["xnw_snap"][ii] = s_v.v
        f_["op_free"] = s_v.v

    def p4_post_ln(bt, gg, ii):
        if f_.get("p4_ln_s") is not None:
            wt("vector", (s_s.h, f_["p4_ln_s"]))
        for k in range(4):
            op("vector", "bn_stats", bnst[:, 6 * k:6 * k + 6],
               xnw[ii][:, 512 * k:512 * (k + 1)], nodrain=(k > 0))
        op("vector", "bn_aggr", mv[:, :], bnst[:, :], inc=s_v.bump())
        wt("scalar", (s_v.h, s_v.v))
        op("scalar", "activation", out=rstd[:, :], in_=mv[:, 1:2], func=AF.Ln,
           bias=sb_sm[:, 32:33], scale=1.0)
        op("scalar", "activation", out=rstd[:, :], in_=rstd[:, :], func=AF.Exp, bias=0.0,
           scale=-0.5)
        op("scalar", "activation", out=negr[:, :], in_=rstd[:, :], func=AF.Identity,
           bias=sb_sm[:, 34:35], scale=-1.0)
        op("scalar", "activation", out=nmr[:, :], in_=mv[:, 0:1], func=AF.Identity,
           bias=sb_sm[:, 34:35], scale=negr[:, 0:1])
        x2w = []
        if f_["x2l_free"][ii] is not None:
            x2w.append((s_t.h, f_["x2l_free"][ii]))
        op("scalar", "activation", out=x2l[ii][:, :], in_=xnw[ii], func=AF.Identity,
           bias=nmr[:, 0:1], scale=rstd[:, 0:1], waits=x2w, inc=s_s.bump())
        x2snap = s_s.v
        f_["p4_ln_s"] = s_s.v
        for g4 in range(4):
            par = g4 % 2
            pb = PS[:, 2048 + par * 512: 2048 + par * 512 + 512].bitcast(BF16)
            tw = [(s_s.h, x2snap)]
            if f_["tp_evac"][par] is not None:
                tw.append((s_s.h, f_["tp_evac"][par]))
            if f_["h_free"][par] is not None:
                tw.append((s_v.h, f_["h_free"][par]))
            for j in range(4):
                dc = g4 * 4 + j
                op("tensor", "transpose", pb[:, j * 128:(j + 1) * 128],
                   x2l[ii][:, dc * 128:(dc + 1) * 128], sb_id[:, :],
                   waits=(tw if j == 0 else ()))
            tinc("tensor", s_t.bump())
            tsn = s_t.v
            dview = x2T.rearrange("p (c t) -> p c t", c=16)[:, g4 * 4: g4 * 4 + 4,
                                                            ii * 128:(ii + 1) * 128]
            ew = [(s_t.h, tsn)]
            if ii == 0 and g4 == 0 and f_["x2T_cons"] is not None:
                ew.append((s_t.h, f_["x2T_cons"]))
            op("scalar", "copy", out=dview,
               in_=pb[:, 0:512].rearrange("p (a b) -> p a b", a=4),
               waits=ew, inc=s_s.bump(), nodrain=True)
            f_["tp_evac"][par] = s_s.v
        f_["x2l_free"][ii] = s_t.v

    def p4_ffn(bt, gg, next_cb=None):
        tts = (gg * 2, gg * 2 + 1)
        x2T_snap = s_s.cur()
        for ii in range(2):
            lw = [x2T_snap, P4W]
            if f_["logit_free"] is not None:
                lw.append((s_v.h, f_["logit_free"]))
            if f_["y67_free"] is not None:
                lw.append((s_v.h, f_["y67_free"]))
            for dc in range(16):
                op("tensor", "matmul", PS[:, 3072:3080],
                   x2T.rearrange("p (c t) -> p c t", c=16)[:, dc, ii * 128:(ii + 1) * 128],
                   wgTb.rearrange("p (c e) -> p c e", c=16)[:, dc, :],
                   start=(dc == 0), stop=(dc == 15), waits=(lw if dc == 0 else ()))
            tinc("tensor", s_t.bump())
            lsnap = s_t.v
            op("vector", "tensor_reduce", sb_sm[:, 28:29], PS[:, 3072:3080],
               mybir.AxisListType.X, ALU.max, waits=[(s_t.h, lsnap)])
            op("vector", "tensor_scalar", out=sb_sm[:, 29:30], in0=sb_sm[:, 28:29],
               scalar1=-1.0, scalar2=None, op0=ALU.mult, inc=s_v.bump())
            wt("scalar", (s_v.h, s_v.v))
            op("scalar", "activation", out=scr8[:, :], in_=PS[:, 3072:3080], func=AF.Exp,
               bias=sb_sm[:, 29:30], scale=1.0, accum_out=sb_sm[:, 30:31], inc=s_s.bump())
            wt("vector", (s_s.h, s_s.v))
            op("vector", "reciprocal", sb_sm[:, 31:32], sb_sm[:, 30:31])
            op("vector", "tensor_scalar", out=scr8[:, :], in0=PS[:, 3072:3080],
               scalar1=sb_sm[:, 29:30], scalar2=0.0, op0=ALU.add, op1=ALU.is_equal)
            op("vector", "tensor_scalar", out=scr8[:, :], in0=scr8[:, :],
               scalar1=sb_sm[:, 31:32], scalar2=MOE_SCALING, op0=ALU.mult, op1=ALU.mult,
               inc=s_v.bump())
            f_["logit_free"] = s_v.v
            gsnap = s_v.v
            pgt = PS[:, 3584:3648].bitcast(BF16)
            gw = [(s_v.h, gsnap)]
            if f_["gt_free"] is not None:
                gw.append((s_v.h, f_["gt_free"]))
            if f_["y67_free"] is not None:
                gw.append((s_v.h, f_["y67_free"]))
            op("tensor", "transpose", pgt[0:8, 0:128], scr8[:, :], sb_id[:, :], waits=gw)
            tinc("tensor", s_t.bump())
            gtsn = s_t.v
            op("vector", "tensor_copy", out=gtb[0:8, ii * 128:(ii + 1) * 128],
               in_=pgt[0:8, 0:128], waits=[(s_t.h, gtsn)], inc=s_v.bump())
            f_["gt_free"] = s_v.v
        gT_snap = s_v.cur()

        if next_cb is not None:
            next_cb()

        for e in range(E):
            par = e % 2
            bcw = [gT_snap]
            if f_["gbc_free"] is not None:
                bcw.append((s_v.h, f_["gbc_free"]))
            if f_["y67_free"] is not None:
                bcw.append((s_v.h, f_["y67_free"]))
            op("tensor", "matmul", PS[0:64, 3584:3840], sb_oh[:, e * 64:(e + 1) * 64],
               gtb[0:8, 0:256], start=True, stop=True, waits=bcw)
            tinc("tensor", s_t.bump())
            bcsnap = s_t.v
            op("vector", "tensor_copy", out=gbc[0:64, :], in_=PS[0:64, 3584:3840],
               waits=[(s_t.h, bcsnap)], inc=s_v.bump())
            gbcsnap = s_v.v
            f_["gbc_free"] = gbcsnap
            hw = [(s_v.h, gbcsnap), P4W]
            if f_["h_free"][par] is not None:
                hw.append((s_v.h, f_["h_free"][par]))
            for dc in range(16):
                op("tensor", "matmul", PS[0:64, 2048 + par * 512: 2048 + par * 512 + 256],
                   aTb.rearrange("p (c e r) -> p c e r", c=16, e=E)[:, dc, e, :],
                   x2T.rearrange("p (c t) -> p c t", c=16)[:, dc, :],
                   start=(dc == 0), stop=(dc == 15), waits=(hw if dc == 0 else ()))
            tinc("tensor", s_t.bump())
            hsnap = s_t.v
            op("vector", "tensor_tensor", out=ghb[0:64, e * 256:(e + 1) * 256],
               in0=PS[0:64, 2048 + par * 512: 2048 + par * 512 + 256], in1=gbc[0:64, :],
               op=ALU.mult, waits=[(s_t.h, hsnap)], inc=s_v.bump())
            f_["h_free"][par] = s_v.v
        gh_snap = s_v.cur()
        f_["x2T_cons"] = s_t.v

        ycnt = 0
        for ii in range(2):
            tt = tts[ii]
            r0 = bt * TPB + tt * 128
            for dt in range(4):
                yb = 3072 + (ycnt % 2) * 512
                ycnt += 1
                yw = [gh_snap, P4W]
                if f_["py_free"][(ycnt - 1) % 2] is not None:
                    yw.append((s_v.h, f_["py_free"][(ycnt - 1) % 2]))
                for e in range(E):
                    op("tensor", "matmul", PS[:, yb: yb + 512],
                       ghb[0:64, e * 256 + ii * 128: e * 256 + ii * 128 + 128],
                       bwTb[0:64, e * 2048 + dt * 512: e * 2048 + dt * 512 + 512],
                       start=(e == 0), stop=(e == E - 1), waits=(yw if e == 0 else ()))
                tinc("tensor", s_t.bump())
                ysnap = s_t.v
                op("vector", "tensor_tensor", out=xnw[ii][:, dt * 512:(dt + 1) * 512],
                   in0=xnw[ii][:, dt * 512:(dt + 1) * 512],
                   in1=PS[:, yb: yb + 512],
                   op=ALU.add, waits=[(s_t.h, ysnap)], inc=s_v.bump(), nodrain=(dt > 0))
                f_["py_free"][(ycnt - 1) % 2] = s_v.v
                f_["y67_free"] = s_v.v
            donesnap = s_v.v
            op("sync", "dma_start", out=out_sl[r0: r0 + 128, :], in_=xnw[ii][:, :],
               waits=[(s_v.h, donesnap)], inc=s_out[ii].bump(16))
            f_["xn_store"][ii] = s_out[ii].v

    groups = [(bt, gg) for bt in range(2) for gg in range(4)]
    p4_loads(*groups[0])
    p4_opmm(*groups[0], 0)
    p4_post_add(*groups[0], 0)
    p4_opmm(*groups[0], 1)
    p4_post_ln(*groups[0], 0)
    p4_post_add(*groups[0], 1)
    p4_post_ln(*groups[0], 1)
    for gi, (bt, gg) in enumerate(groups):
        nxt = groups[gi + 1] if gi + 1 < len(groups) else None
        if nxt is not None:
            p4_loads(*nxt)

            def cb(nxt=nxt):
                p4_opmm(*nxt, 0)
            p4_ffn(bt, gg, next_cb=cb)
            p4_post_add(*nxt, 0)
            p4_opmm(*nxt, 1)
            p4_post_ln(*nxt, 0)
            p4_post_add(*nxt, 1)
            p4_post_ln(*nxt, 1)
        else:
            p4_ffn(bt, gg)

    wt("sync", s_out[0].cur(), s_out[1].cur())
    return _finish(nc, es, ops, ENGS, NCORE, drains)


def _finish(nc, es, ops, engs, ncore, drains=False):
    block = es.enter_context(nc.Block())

    def runner(lst, auto_drain=False):
        def go(e):
            waited = {}
            for meth, args, kw, waits, inc in lst:
                for (h, v) in waits:
                    k = id(h)
                    if waited.get(k, -1) >= v:
                        continue
                    e.wait_ge(h, v)
                    waited[k] = v
                if meth == "_wait":
                    continue
                if meth == "_drain":
                    e.drain()
                    continue
                if meth.startswith("!"):
                    meth = meth[1:]
                elif auto_drain and meth not in ("_nop", "dma_start"):
                    e.drain()
                if meth == "_nop":
                    ins = e.nop()
                elif meth == "collective":
                    ins = e.collective_compute(
                        kw["kind"], ALU.bypass,
                        replica_groups=[list(range(ncore))],
                        ins=[a.opt() for a in kw["cins"]],
                        outs=[a.opt() for a in kw["couts"]],
                    )
                    if inc is not None:
                        ins.then_inc(inc[0])
                    continue
                else:
                    ins = getattr(e, meth)(*args, **kw)
                if inc is not None:
                    ins.then_inc(inc[0], inc[2])
        return go

    @block.tensor
    def _(e):
        runner(ops["tensor"])(e)

    @block.vector
    def _(e):
        runner(ops["vector"], auto_drain=drains)(e)

    @block.scalar
    def _(e):
        runner(ops["scalar"], auto_drain=True)(e)

    @block.sync
    def _(e):
        runner(ops["sync"])(e)

    @block.gpsimd
    def _(e):
        runner(ops["gpsimd"])(e)

    es.close()
    return nc



def _prep_inputs(kw):
    f32 = np.float32
    bf = ml_dtypes.bfloat16
    x = np.asarray(kw["L0_attn_x"], f32)

    def T(a):
        return np.ascontiguousarray(np.asarray(a, f32).T)

    wkdT = T(kw["L0_attn_Wkd"])
    wqdT = T(kw["L0_attn_Wqd"])
    wkuT = T(kw["L0_attn_Wku"])
    wvuT = T(kw["L0_attn_Wvu"])
    wquT = T(kw["L0_attn_Wqu"])
    wkrT = np.ascontiguousarray(np.asarray(kw["L0_attn_Wkr"], f32)[ROPE_ORDER, :].T)
    wqrT = np.ascontiguousarray(np.asarray(kw["L0_attn_Wqr"], f32)[ROPE_ORDER, :].T)
    woTr = T(kw["L0_attn_Wo"])
    A = np.asarray(kw["L0_ffn_A"], f32)
    Bw = np.asarray(kw["L0_ffn_Bw"], f32)
    Wg = np.asarray(kw["L0_ffn_Wg"], f32)

    aTh = np.ascontiguousarray(
        A.transpose(2, 0, 1).reshape(16, 128, E, R).transpose(1, 0, 2, 3).reshape(128, 16 * E * R))
    bwTh = np.ascontiguousarray(Bw.transpose(2, 0, 1).reshape(64, E * D))
    wgTh = np.ascontiguousarray(
        Wg.T.reshape(16, 128, E).transpose(1, 0, 2).reshape(128, 16 * E))

    def b2(v, nf):
        return np.ascontiguousarray(np.asarray(v, f32).reshape(nf, 128).T)

    bkd2 = b2(kw["L0_attn_bkd"], 4)
    bqd2 = b2(kw["L0_attn_bqd"], 12)
    bku2 = b2(kw["L0_attn_bku"], 16)
    bvu2 = b2(kw["L0_attn_bvu"], 16)
    bqu2 = b2(kw["L0_attn_bqu"], 16)
    bkr2 = b2(np.asarray(kw["L0_attn_bkr"], f32)[ROPE_ORDER], 8)
    bqr2 = b2(np.asarray(kw["L0_attn_bqr"], f32)[ROPE_ORDER], 8)
    bo2 = np.ascontiguousarray(np.asarray(kw["L0_attn_bo"], f32).reshape(1, D))

    ident = np.eye(128, dtype=bf)
    ones = np.ones((128, 128), dtype=bf)
    onehot = np.zeros((8, E * 64), dtype=bf)
    for e in range(E):
        onehot[e, e * 64:(e + 1) * 64] = 1.0

    inv_freq = (1.0 / (10000.0 ** (np.arange(0, DR, 2, dtype=np.float64) / DR))).astype(f32)

    shared = dict(wkdT=wkdT, wqdT=wqdT, wkuT=wkuT, wvuT=wvuT, wquT=wquT,
                  wkrT=wkrT, wqrT=wqrT, woTr=woTr, aTh=aTh, bwTh=bwTh, wgTh=wgTh,
                  bkd2=bkd2, bqd2=bqd2, bku2=bku2, bvu2=bvu2, bqu2=bqu2,
                  bkr2=bkr2, bqr2=bqr2, bo2=bo2, identt=ident, onest=ones,
                  onehot=onehot)
    in_maps = []
    for c in range(NCORE):
        pos = (8 * np.arange(TPB, dtype=f32) + c).astype(f32)
        ang = pos[None, :] * inv_freq[:, None]
        cos64 = np.cos(ang).astype(f32)
        sin64 = np.sin(ang).astype(f32)
        cosp = np.empty((64, TPB), f32)
        sinp = np.empty((64, TPB), f32)
        for j, d in enumerate(PERM64):
            cosp[j] = cos64[d % 32]
            sinp[j] = sin64[d % 32] * (-1.0 if d < 32 else 1.0)
        costab = np.ascontiguousarray(np.concatenate([cosp, cosp], 0))
        sintb = np.ascontiguousarray(np.concatenate([sinp, sinp], 0))
        m = np.arange(128)
        jj = np.arange(8)
        mq = np.arange(128)
        mk = (8 * m[:, None, None] + jj[None, :, None] <= 8 * mq[None, None, :] + c)
        masktab = np.ascontiguousarray(mk.astype(bf).reshape(128, 8 * 128))
        x_c = np.ascontiguousarray(x[:, c::8, :].reshape(2 * TPB, D))
        im = dict(shared)
        im.update(x_sl=x_c, cost=costab, sint=sintb, maskt=masktab)
        in_maps.append(im)
    return in_maps


_NC_CACHE = {}


def kernel(**kw):
    in_maps = _prep_inputs(kw)
    if "nc" not in _NC_CACHE:
        _NC_CACHE["nc"] = build_nc(drains=True)
    nc = _NC_CACHE["nc"]
    res = run_bass_kernel_spmd(nc, in_maps, core_ids=list(range(NCORE)))
    out = np.empty((B, S, D), np.float32)
    for c in range(NCORE):
        o = np.asarray(res.results[c]["out_sl"], np.float32).reshape(B, TPB, D)
        out[:, c::8, :] = o
    return out


# version 27
# speedup vs baseline: 1.1567x; 1.0192x over previous
import math
import numpy as np
import ml_dtypes
import concourse.bass as bass
import concourse.mybir as mybir
from concourse.bass_utils import run_bass_kernel_spmd
from contextlib import ExitStack

F32 = mybir.dt.float32
BF16 = mybir.dt.bfloat16
AF = mybir.ActivationFunctionType
ALU = mybir.AluOpType

B, S, D = 2, 8192, 2048
H, DH, DR = 16, 128, 64
DCKV, DCQ = 512, 1536
E, R = 8, 64
NCORE = 8
TPB = S // NCORE
SCALE = 1.0 / math.sqrt(DH + DR)
EXPB = 12.0
EPS = 1e-5
MOE_SCALING = 4.0

PERM64 = list(range(0, 16)) + list(range(32, 48)) + list(range(16, 32)) + list(range(48, 64))
ROPE_ORDER = [h * 64 + j for h in range(H) for j in PERM64]

ENGS = ("tensor", "vector", "scalar", "sync", "gpsimd")
LN_CUT = None


class Sem:
    def __init__(self, nc, es, name):
        self.h = es.enter_context(nc.semaphore(name))
        self.v = 0

    def bump(self, n=1):
        self.v += n
        return (self.h, self.v, n)

    def cur(self):
        return (self.h, self.v)


class Arena:
    def __init__(self, t, cols):
        self.t, self.cols, self.off, self.stack = t, cols, 0, []

    def alloc(self, n, parts=128):
        assert self.off + n <= self.cols, ("arena overflow", self.off, n, self.cols)
        sl = self.t[0:parts, self.off:self.off + n]
        self.off += n
        return sl

    def mark(self):
        self.stack.append(self.off)

    def reset(self):
        self.off = self.stack.pop()


def build_nc(dbg=False, phases=("p1", "p3", "p4"), p1_subs=None, p1_passes=2, do_cc=True, drains=False):
    nc = bass.Bass("TRN2", target_bir_lowering=False, debug=False)

    def din(name, shape, dt=F32):
        return nc.dram_tensor(name, shape, dt, kind="ExternalInput").ap()

    x_sl = din("x_sl", [2 * TPB, D])
    wkdT = din("wkdT", [D, DCKV])
    wqdT = din("wqdT", [D, DCQ])
    wkuT = din("wkuT", [DCKV, H * DH])
    wvuT = din("wvuT", [DCKV, H * DH])
    wquT = din("wquT", [DCQ, H * DH])
    wkrT = din("wkrT", [D, H * DR])
    wqrT = din("wqrT", [DCQ, H * DR])
    woTr = din("woTr", [H * DH, D])
    aTh = din("aTh", [128, 16 * E * R])
    bwTh = din("bwTh", [64, E * D])
    wgTh = din("wgTh", [128, 16 * E])
    bkd2 = din("bkd2", [128, 4])
    bqd2 = din("bqd2", [128, 12])
    bku2 = din("bku2", [128, 16])
    bvu2 = din("bvu2", [128, 16])
    bqu2 = din("bqu2", [128, 16])
    bkr2 = din("bkr2", [128, 8])
    bqr2 = din("bqr2", [128, 8])
    bo2 = din("bo2", [1, D])
    cost = din("cost", [128, TPB])
    sintab = din("sint", [128, TPB])
    masktab = din("maskt", [128, 8 * 128], BF16)
    identt = din("identt", [128, 128], BF16)
    onest = din("onest", [128, 128], BF16)
    onehot = din("onehot", [8, E * 64], BF16)

    out_sl = nc.dram_tensor("out_sl", [2 * TPB, D], F32, kind="ExternalOutput").ap()

    kscr = "ExternalOutput" if dbg else "Internal"
    def dscr(name, shape):
        if kscr == "Internal":
            return nc.dram_tensor(name, shape, BF16)
        return nc.dram_tensor(name, shape, BF16, kind=kscr)

    qscr_t = dscr("qscr", [3 * 1024, 2 * TPB])
    shard_t = [nc.dram_tensor("shard%d" % b, [5 * 1024, TPB], BF16) for b in range(2)]
    dbgsh_t = None
    if dbg:
        dbgsh_t = [nc.dram_tensor("dbgsh%d" % b, [5 * 1024, TPB], BF16, kind="ExternalOutput")
                   for b in range(2)]
    gath_t = [nc.dram_tensor("gath%d" % b, [NCORE * 5 * 1024, TPB], BF16) for b in range(2)]
    onorm_t = dscr("onormscr", [H * DH, 2 * TPB])
    wscr_t = nc.dram_tensor("wscr", [128, 110592], BF16)
    wscr = wscr_t.ap()
    qscr = qscr_t.ap()
    shard = [t.ap() for t in shard_t]
    gath = [t.ap() for t in gath_t]
    onorm = onorm_t.ap()

    es = ExitStack()
    AB = Arena(es.enter_context(nc.sbuf_tensor("ab16", [128, 73728], BF16)), 73728)
    AFA = Arena(es.enter_context(nc.sbuf_tensor("af32", [128, 8704], F32)), 8704)
    PS = es.enter_context(nc.psum_tensor("ps", [128, 4096], F32))

    sb_id = es.enter_context(nc.sbuf_tensor("sb_id", [128, 128], BF16))
    sb_ones = es.enter_context(nc.sbuf_tensor("sb_ones", [128, 128], BF16))
    sb_mask = es.enter_context(nc.sbuf_tensor("sb_mask", [128, 1024], BF16))
    sb_cos = es.enter_context(nc.sbuf_tensor("sb_cos", [128, TPB], F32))
    sb_sin = es.enter_context(nc.sbuf_tensor("sb_sin", [128, TPB], F32))
    sb_bias = es.enter_context(nc.sbuf_tensor("sb_bias", [128, 80], F32))
    sb_bo = es.enter_context(nc.sbuf_tensor("sb_bo", [1, D], F32))
    sb_bo16 = es.enter_context(nc.sbuf_tensor("sb_bo16", [1, D], BF16))
    sb_oh = es.enter_context(nc.sbuf_tensor("sb_oh", [8, E * 64], BF16))
    sb_sm = es.enter_context(nc.sbuf_tensor("sb_sm", [128, 64], F32))

    BOFF = {}
    _o = 0
    for nm, wdt in [("bkd", 4), ("bqd", 12), ("bku", 16), ("bvu", 16), ("bqu", 16),
                    ("bkr", 8), ("bqr", 8)]:
        BOFF[nm] = _o
        _o += wdt

    sems = {}
    def mksem(name):
        s = Sem(nc, es, name)
        sems[name] = s
        return s
    s_cst = mksem("cst")
    s_x = [mksem("x0"), mksem("x1"), mksem("x2")]
    s_w = [mksem("w0"), mksem("w1")]
    s_kv = [mksem("kv%d" % i) for i in range(4)]
    s_kvg = [mksem("kvg%d" % i) for i in range(4)]
    s_kvs = [mksem("kvs%d" % i) for i in range(4)]
    s_wst = mksem("wst")
    s_w2 = [mksem("w20"), mksem("w21")]
    s_dbg = mksem("dbg")
    s_q = [mksem("q0"), mksem("q1")]
    s_on = [mksem("on0"), mksem("on1")]
    s_stg = mksem("stg")
    s_sh = [mksem("sh0"), mksem("sh1")]
    s_onst = mksem("onst")
    s_out = [mksem("outst0"), mksem("outst1")]
    s_cc = [mksem("cc0"), mksem("cc1")]
    s_t = mksem("t")
    s_v = mksem("v")
    s_s = mksem("s")

    ops = {e: [] for e in ENGS}

    def op(eng, meth, *args, waits=(), inc=None, nodrain=False, **kw):
        ops[eng].append((meth if not nodrain else "!" + meth, args, kw, tuple(waits), inc))

    def wt(eng, *waits):
        ops[eng].append(("_wait", (), {}, tuple(waits), None))

    def tinc(eng, bump):
        meth, args, kw, waits, inc = ops[eng][-1]
        assert inc is None and meth not in ("_wait",), (meth, inc)
        ops[eng][-1] = (meth, args, kw, waits, bump)

    for dst, src in [
        (sb_id[:, :], identt), (sb_ones[:, :], onest), (sb_mask[:, :], masktab),
        (sb_cos[:, :], cost), (sb_sin[:, :], sintab), (sb_bo[:, :], bo2),
        (sb_oh[:, :], onehot),
        (sb_bias[:, BOFF["bkd"]:BOFF["bkd"] + 4], bkd2),
        (sb_bias[:, BOFF["bqd"]:BOFF["bqd"] + 12], bqd2),
        (sb_bias[:, BOFF["bku"]:BOFF["bku"] + 16], bku2),
        (sb_bias[:, BOFF["bvu"]:BOFF["bvu"] + 16], bvu2),
        (sb_bias[:, BOFF["bqu"]:BOFF["bqu"] + 16], bqu2),
        (sb_bias[:, BOFF["bkr"]:BOFF["bkr"] + 8], bkr2),
        (sb_bias[:, BOFF["bqr"]:BOFF["bqr"] + 8], bqr2),
    ]:
        op("sync", "dma_start", out=dst, in_=src, inc=s_cst.bump(16))
    CST = s_cst.cur()
    for e in ("tensor", "scalar", "gpsimd", "vector"):
        wt(e, CST)
    op("vector", "memset", sb_sm[:, 32:33], EPS)
    op("vector", "memset", sb_sm[:, 33:34], -EXPB)
    op("vector", "memset", sb_sm[:, 34:35], 0.0)
    op("vector", "tensor_copy", out=sb_bo16[:, :], in_=sb_bo[:, :], inc=s_v.bump())

    AB.mark(); AFA.mark()
    xln = [AB.alloc(2048) for _ in range(2)]
    xlnT = AB.alloc(16 * 1024)
    kvc = AB.alloc(4 * 1024)
    qcs = AB.alloc(12 * 1024)
    stg = [AB.alloc(1024) for _ in range(3)]
    vfm = [AB.alloc(1024) for _ in range(4)]
    krr = [AB.alloc(1024) for _ in range(2)]
    krs = AB.alloc(1024)
    rop = [AB.alloc(1024) for _ in range(2)]
    wbuf = [AB.alloc(12288) for _ in range(2)]
    xf = [AFA.alloc(2048) for _ in range(3)]
    rtmp = [AFA.alloc(1024) for _ in range(2)]
    bnst = sb_sm[:, 0:24]
    mv = sb_sm[:, 24:26]
    rstd = sb_sm[:, 26:27]
    nmr = sb_sm[:, 27:28]
    negr = sb_sm[:, 35:36]

    def xlnT_dc(dc):
        return xlnT[:, dc * 1024:(dc + 1) * 1024]

    st_ = dict(
        x_snap=[None, None, None],
        xln_free=[None, None],
        tp_evac=[None, None],
        proj_evac=[None, None],
        kvc_done=None, qcs_done=None, xlnT_done=None,
        wfree=[None, None],
        wtog=0, pjpar=0, stgi=0,
        vfm_pend=[], vfm_free=[None] * 4,
        vt_par=0, vt_evac=[None, None], wscr_off=0, wscr_map={}, wscr_stc={},
        wst_slot=[None, None], ln_s_snap=None, xlnT_half0=0,
        kr_free=[None, None],
        ropi=0, rop_store_sem=[None, None],
    )

    def emit_v_transposes(bt):
        chunks = list(st_["vfm_pend"])
        st_["vfm_pend"].clear()
        fbase = chunks[0][0]
        lastsnap = chunks[-1][1]
        dcol0 = fbase * 128
        half = dcol0 // 1024
        col0 = dcol0 % 1024
        for tc in range(8):
            par = st_["vt_par"] % 2
            st_["vt_par"] += 1
            pb = PS[:, 2048 + par * 512: 2048 + par * 512 + 512].bitcast(BF16)
            tw = [(s_s.h, lastsnap)]
            if st_["vt_evac"][par] is not None:
                tw.append((s_s.h, st_["vt_evac"][par][1]))
            if st_["tp_evac"][par] is not None:
                tw.append((s_s.h, st_["tp_evac"][par]))
            for j, (fcg, _) in enumerate(chunks):
                op("tensor", "transpose", pb[:, j * 128:(j + 1) * 128],
                   vfm[fcg % 4][:, tc * 128:(tc + 1) * 128], sb_id[:, :],
                   waits=(tw if j == 0 else ()))
            tinc("tensor", s_t.bump())
            tsn = s_t.v
            if tc == 7:
                for fcg, _ in chunks:
                    st_["vfm_free"][fcg % 4] = tsn
            ssl = st_["stgi"] % 3
            st_["stgi"] += 1
            sdst = stg[ssl][:, 0:512]
            ew = [(s_t.h, tsn), s_sh[bt].cur(), s_stg.cur()]
            op("scalar", "copy", out=sdst, in_=pb[:, 0:512], waits=ew, inc=s_s.bump(),
               nodrain=True)
            st_["vt_evac"][par] = ("s", s_s.v)
            evs = s_s.v
            dview = shard[bt][3072 + 256 * tc: 3072 + 256 * (tc + 1), :].rearrange(
                "(m two) c -> m two c", two=2)[:, half, col0:col0 + 512]
            op("sync", "dma_start", out=dview, in_=sdst, waits=[(s_s.h, evs)],
               inc=s_sh[bt].bump(16))

    def emit_rope(bt, sname, fcg, evsnap):
        sl = fcg % 2
        rsl = st_["ropi"] % 2
        st_["ropi"] += 1
        shmask = [(i + 16) % 32 for i in range(32)]
        op("vector", "stream_shuffle", krs[:, :], krr[sl][:, :], shmask,
           waits=[(s_s.h, evsnap)])
        op("vector", "tensor_tensor", out=rtmp[0], in0=krr[sl][:, :], in1=sb_cos[:, :], op=ALU.mult,
           nodrain=True)
        op("vector", "tensor_tensor", out=rtmp[1], in0=krs[:, :], in1=sb_sin[:, :], op=ALU.mult)
        rw = []
        prev = st_["rop_store_sem"][rsl]
        if prev is not None:
            rw.append(prev.cur())
        op("vector", "tensor_tensor", out=rop[rsl][:, :], in0=rtmp[0], in1=rtmp[1], op=ALU.add,
           waits=rw, inc=s_v.bump())
        st_["kr_free"][sl] = s_v.v
        rsnap = s_v.v
        if sname in ("krA", "krB"):
            dview = shard[bt][2048 + fcg * 128: 2048 + (fcg + 1) * 128, :]
            sem = s_sh[bt]
        else:
            dview = qscr[2048 + fcg * 128: 2048 + (fcg + 1) * 128, bt * TPB:(bt + 1) * TPB]
            sem = s_stg
        st_["rop_store_sem"][rsl] = sem
        op("sync", "dma_start", out=dview, in_=rop[rsl][:, :], waits=[(s_v.h, rsnap)],
           inc=sem.bump(16))

    SUBS = [
        ("kvc", wkdT, 16, 0, 512, "xlnT", "bkd"),
        ("qcA", wqdT, 16, 0, 768, "xlnT", "bqd"),
        ("qcB", wqdT, 16, 768, 768, "xlnT", "bqd"),
        ("krA", wkrT, 16, 0, 512, "xlnT", "bkr"),
        ("krB", wkrT, 16, 512, 512, "xlnT", "bkr"),
        ("qr", wqrT, 12, 0, 1024, "qcs", "bqr"),
        ("v", wvuT, 4, 0, 2048, "kvc", "bvu"),
        ("kcnt", wkuT, 4, 0, 2048, "kvc", "bku"),
        ("qcntA", wquT, 12, 0, 1024, "qcs", "bqu"),
        ("qcntB", wquT, 12, 1024, 1024, "qcs", "bqu"),
    ]

    def p0_precast():
        off = 0
        for k, (sname, wap, nd, fcol0, FW, _src, _b) in enumerate(SUBS):
            wcols = nd * FW
            st_["wscr_map"][sname] = off
            st_["wscr_stc"][sname] = 16 * (k + 1)
            wsl = k % 2
            ww = []
            if k >= 2:
                ww.append(s_wst.cur())
            ndh = max(1, nd // 2)
            op("gpsimd", "dma_start",
               out=wbuf[wsl][:, 0:ndh * FW].rearrange("p (g f) -> p g f", g=ndh),
               in_=wap.rearrange("(g p) f -> p g f", p=128)[:, 0:ndh, fcol0:fcol0 + FW],
               waits=ww, inc=s_w[wsl].bump(16))
            if nd > ndh:
                op("gpsimd", "dma_start",
                   out=wbuf[wsl][:, ndh * FW:nd * FW].rearrange("p (g f) -> p g f", g=nd - ndh),
                   in_=wap.rearrange("(g p) f -> p g f", p=128)[:, ndh:nd, fcol0:fcol0 + FW],
                   waits=ww, inc=s_w[wsl].bump(16))
            op("gpsimd", "dma_start", out=wscr[:, off:off + wcols],
               in_=wbuf[wsl][:, 0:wcols],
               waits=[s_w[wsl].cur(), s_wst.cur()], inc=s_wst.bump(16))
            off += wcols

    def p1_pass(bt):
        for tc in range(8):
            sl = tc % 3
            sl2 = tc % 2
            w = []
            if st_["x_snap"][sl] is not None:
                w.append((s_s.h, st_["x_snap"][sl][1]))
            op("sync", "dma_start", out=xf[sl],
               in_=x_sl[bt * TPB + tc * 128: bt * TPB + (tc + 1) * 128, :],
               waits=w, inc=s_x[sl].bump(16))
            if LN_CUT == "x":
                st_["x_snap"][sl] = None
                continue
            wt("vector", s_x[sl].cur())
            if st_["ln_s_snap"] is not None:
                wt("vector", (s_s.h, st_["ln_s_snap"]))
            for k in range(4):
                op("vector", "bn_stats", bnst[:, 6 * k:6 * k + 6], xf[sl][:, 512 * k:512 * (k + 1)],
                   nodrain=(k > 0))
            op("vector", "bn_aggr", mv[:, :], bnst[:, :], inc=s_v.bump())
            if LN_CUT == "bn":
                st_["x_snap"][sl] = s_v.v
                continue
            wt("scalar", (s_v.h, s_v.v))
            op("scalar", "activation", out=rstd[:, :], in_=mv[:, 1:2], func=AF.Ln, bias=sb_sm[:, 32:33], scale=1.0)
            op("scalar", "activation", out=rstd[:, :], in_=rstd[:, :], func=AF.Exp, bias=0.0,
               scale=-0.5)
            op("scalar", "activation", out=negr[:, :], in_=rstd[:, :], func=AF.Identity,
               bias=sb_sm[:, 34:35], scale=-1.0)
            op("scalar", "activation", out=nmr[:, :], in_=mv[:, 0:1], func=AF.Identity,
               bias=sb_sm[:, 34:35], scale=negr[:, 0:1])
            xw = []
            if st_["xln_free"][sl2] is not None:
                xw.append((s_t.h, st_["xln_free"][sl2]))
            op("scalar", "activation", out=xln[sl2][:, :], in_=xf[sl], func=AF.Identity,
               bias=nmr[:, 0:1], scale=rstd[:, 0:1], waits=xw, inc=s_s.bump())
            st_["x_snap"][sl] = ("s", s_s.v)
            st_["ln_s_snap"] = s_s.v
            xlnsnap = ("s", s_s.v)
            if LN_CUT == "xln":
                st_["xln_free"][sl2] = None
                continue
            for g4 in range(4):
                par = g4 % 2
                pb = PS[:, 2048 + par * 512: 2048 + par * 512 + 512].bitcast(BF16)
                tw = [(s_s.h, xlnsnap[1])]
                if st_["tp_evac"][par] is not None:
                    tw.append((s_s.h, st_["tp_evac"][par]))
                if st_["vt_evac"][par] is not None:
                    tw.append((s_s.h, st_["vt_evac"][par][1]))
                for j in range(4):
                    dc = g4 * 4 + j
                    op("tensor", "transpose", pb[:, j * 128:(j + 1) * 128],
                       xln[sl2][:, dc * 128:(dc + 1) * 128], sb_id[:, :],
                       waits=(tw if j == 0 else ()))
                tinc("tensor", s_t.bump())
                tsn = s_t.v
                if LN_CUT == "tpv":
                    op("vector", "tensor_copy", out=xlnT[:, g4 * 512:(g4 + 1) * 512],
                       in_=pb[:, 0:512], waits=[(s_t.h, tsn)], inc=s_v.bump())
                    st_["tp_evac"][par] = None
                    stv_snap = s_v.v
                    op("scalar", "_wait_only") if False else None
                    st_["vt_evac"][par] = stv_snap
                    continue
                if LN_CUT == "tps":
                    op("scalar", "copy", out=xlnT[:, g4 * 512:(g4 + 1) * 512],
                       in_=pb[:, 0:512], waits=[(s_t.h, tsn)], inc=s_s.bump())
                    st_["tp_evac"][par] = s_s.v
                    continue
                dview = xlnT.rearrange("p (c t) -> p c t", c=16)[:, g4 * 4:g4 * 4 + 4,
                                                                tc * 128:(tc + 1) * 128]
                op("scalar", "copy", out=dview,
                   in_=pb[:, 0:512].rearrange("p (a b) -> p a b", a=4),
                   waits=[(s_t.h, tsn)], inc=s_s.bump(), nodrain=True)
                st_["tp_evac"][par] = s_s.v
            st_["xln_free"][sl2] = s_t.v
            if tc == 3:
                st_["xlnT_half0"] = s_s.v
        st_["xlnT_done"] = s_s.v

        if p1_subs is not None and len(p1_subs) == 0:
            return
        subs = list(SUBS)
        if p1_subs is not None:
            subs = [s for s in subs if s[0] in p1_subs]
        for sname, wap, nd, fcol0, FW, src, bnm in subs:
            wsl = st_["wtog"] % 2
            st_["wtog"] += 1
            ww = []
            if st_["wfree"][wsl] is not None:
                ww.append((s_t.h, st_["wfree"][wsl]))
            wcols = nd * FW
            ndh = max(1, nd // 2)
            op("gpsimd", "dma_start",
               out=wbuf[wsl][:, 0:ndh * FW].rearrange("p (g f) -> p g f", g=ndh),
               in_=wap.rearrange("(g p) f -> p g f", p=128)[:, 0:ndh, fcol0:fcol0 + FW],
               waits=ww, inc=s_w[wsl].bump(16))
            if nd > ndh:
                op("gpsimd", "dma_start",
                   out=wbuf[wsl][:, ndh * FW:nd * FW].rearrange("p (g f) -> p g f", g=nd - ndh),
                   in_=wap.rearrange("(g p) f -> p g f", p=128)[:, ndh:nd, fcol0:fcol0 + FW],
                   waits=ww, inc=s_w[wsl].bump(16))
            wsem = s_w[wsl].cur()
            nf = FW // 128
            for fc in range(nf):
                fcg = (fcol0 // 128) + fc
                par = st_["pjpar"] % 2
                st_["pjpar"] += 1
                pdst = PS[:, par * 1024:(par + 1) * 1024]
                mw = [wsem]
                if st_["proj_evac"][par] is not None:
                    mw.append((s_s.h, st_["proj_evac"][par]))
                mwh = list(mw)
                if src == "xlnT":
                    mw.append((s_s.h, st_["xlnT_half0"]))
                    mwh.append((s_s.h, st_["xlnT_done"]))
                elif src == "kvc":
                    mw.append((s_s.h, st_["kvc_done"]))
                else:
                    mw.append((s_s.h, st_["qcs_done"]))
                for qh in range(2):
                    for dc in range(nd):
                        if src == "xlnT":
                            rhs_full = xlnT_dc(dc)
                        elif src == "kvc":
                            rhs_full = kvc[:, dc * 1024:(dc + 1) * 1024]
                        else:
                            rhs_full = qcs[:, dc * 1024:(dc + 1) * 1024]
                        lhs = wbuf[wsl][:, dc * FW + fc * 128: dc * FW + fc * 128 + 128]
                        w_ = mw if qh == 0 else mwh
                        op("tensor", "matmul", pdst[:, qh * 512:(qh + 1) * 512], lhs,
                           rhs_full[:, qh * 512:(qh + 1) * 512],
                           start=(dc == 0), stop=(dc == nd - 1),
                           waits=(w_ if dc == 0 else ()))
                tinc("tensor", s_t.bump())
                mmsnap = s_t.v
                bap = sb_bias[:, BOFF[bnm] + fcg: BOFF[bnm] + fcg + 1]
                ew = [(s_t.h, mmsnap)]
                ssl = None
                if sname == "kvc":
                    edst = kvc[:, fcg * 1024:(fcg + 1) * 1024]
                elif sname in ("qcA", "qcB"):
                    edst = qcs[:, fcg * 1024:(fcg + 1) * 1024]
                elif sname in ("krA", "krB", "qr"):
                    edst = krr[fcg % 2]
                    if st_["kr_free"][fcg % 2] is not None:
                        ew.append((s_v.h, st_["kr_free"][fcg % 2]))
                elif sname == "v":
                    edst = vfm[fcg % 4]
                    if st_["vfm_free"][fcg % 4] is not None:
                        ew.append((s_t.h, st_["vfm_free"][fcg % 4]))
                else:
                    ssl = st_["stgi"] % 3
                    st_["stgi"] += 1
                    edst = stg[ssl]
                    ew.append(s_sh[bt].cur())
                    ew.append(s_stg.cur())
                op("scalar", "activation", out=edst, in_=pdst, func=AF.Identity,
                   bias=bap, scale=1.0, waits=ew, inc=s_s.bump(), nodrain=True)
                st_["proj_evac"][par] = s_s.v
                evsnap = s_s.v
                if sname == "kvc" and fcg == 3:
                    st_["kvc_done"] = evsnap
                if sname == "qcB" and fcg == 11:
                    st_["qcs_done"] = evsnap
                if sname == "kcnt":
                    op("sync", "dma_start", out=shard[bt][fcg * 128:(fcg + 1) * 128, :],
                       in_=edst, waits=[(s_s.h, evsnap)], inc=s_sh[bt].bump(16))
                elif sname in ("qcntA", "qcntB"):
                    op("sync", "dma_start",
                       out=qscr[fcg * 128:(fcg + 1) * 128, bt * TPB:(bt + 1) * TPB],
                       in_=edst, waits=[(s_s.h, evsnap)], inc=s_stg.bump(16))
                elif sname == "v":
                    st_["vfm_pend"].append((fcg, evsnap))
                    if len(st_["vfm_pend"]) == 4:
                        emit_v_transposes(bt)
                elif sname in ("krA", "krB", "qr"):
                    emit_rope(bt, sname, fcg, evsnap)
            st_["wfree"][wsl] = s_t.v

    defer_cc1 = do_cc and ("p3" in phases) and p1_passes > 1
    cc1_w = [None]

    def emit_cc(bt, ccw):
        for (r0, r1) in ((0, 2560), (2560, 5120)):
            op("gpsimd", "collective", kind="AllGather",
               cins=[shard_t[bt].ap()[r0:r1, :]],
               couts=[gath_t[bt].ap()[8 * r0:8 * r1, :]],
               waits=ccw, inc=s_cc[bt].bump(1))

    for bt in range(p1_passes):
        p1_pass(bt)
        if do_cc:
            ccw = [s_sh[bt].cur()]
            if bt == 1 and defer_cc1:
                cc1_w[0] = ccw
                continue
            emit_cc(bt, ccw)
        if dbg:
            op("gpsimd", "dma_start", out=dbgsh_t[bt].ap(), in_=shard[bt],
               waits=[s_sh[bt].cur()], inc=s_dbg.bump(16))
    QS_TOT = s_stg.cur()
    if dbg:
        wt("gpsimd", s_dbg.cur())
    if "p3" not in phases:
        wt("sync", QS_TOT, s_sh[0].cur(), s_sh[1].cur())
        wt("vector", (s_v.h, s_v.v))
        wt("scalar", (s_s.h, s_s.v))
        wt("tensor", (s_t.h, s_t.v))
        if do_cc:
            wt("gpsimd", (s_cc[0].h, 1) if p1_passes > 0 else (s_cc[0].h, 0))
            if p1_passes > 1:
                wt("gpsimd", (s_cc[1].h, 1))

    if "p3" not in phases:
        return _finish(nc, es, ops, ENGS, NCORE, drains)
    AB.reset(); AFA.reset()
    AB.mark(); AFA.mark()
    qbuf = [AB.alloc(6 * 1024) for _ in range(2)]
    kvb = [AB.alloc(10240) for _ in range(4)]
    ptil = [AB.alloc(1024) for _ in range(6)]
    onstg = [AB.alloc(1024) for _ in range(4)]
    lsb16s = [AB.alloc(1024, parts=1) for _ in range(4)]
    osbb = AFA.alloc(4096)
    lsbb = AFA.alloc(4096, parts=1)

    OO, LL = 2048, 3072
    P1T, P1V, P1S = s_t.cur(), s_v.cur(), s_s.cur()
    wt("sync", P1T, P1V, P1S)
    wt("gpsimd", P1T, P1V, P1S)

    a_ = dict(scpar=0, ptc=0, pt_cons=[None] * 6, sc_evac=[None] * 4,
              ol_free=None, bc_evac=None, kv_free=[None] * 4, q_free=[None, None],
              kv_snap=[None] * 4, q_snap=[None, None],
              oni=0, sweep=0, kvctr=0, pend_onst=[], lp_snap=[None, None], lp_free=None, ptil_guard=None)
    SWEEPS = [(bt, g) for bt in range(2) for g in range(4)]
    KVPOS = [(bt, g, bi) for (bt, g) in SWEEPS for bi in range(8)]

    def kv_load(bt, g, bi, slot):
        h0 = g * 4
        w = [(s_cc[bt].h, 2)] if do_cc else []
        if a_["kv_free"][slot] is not None:
            w.append((s_t.h, a_["kv_free"][slot]))
        gbA = gath[bt][0:8 * 2560, :].rearrange("(r f) c -> r f c", f=2560)
        gbB = gath[bt][8 * 2560:, :].rearrange("(r f) c -> r f c", f=2560)
        for hh in range(4):
            h = h0 + hh
            src = gbA[:, h * 128:(h + 1) * 128, bi * 128:(bi + 1) * 128]
            dst = kvb[slot][:, hh * 1024:(hh + 1) * 1024].rearrange("p (r t) -> p r t", r=8)
            if hh < 2:
                op("sync", "dma_start", out=dst, in_=src.rearrange("r f c -> f r c"),
                   waits=(w if hh == 0 else ()), inc=s_kv[slot].bump(16))
            else:
                op("sync", "dma_start", out=dst, in_=src.rearrange("r f c -> f r c"),
                   waits=(w if hh == 2 else ()), inc=s_kvs[slot].bump(16))
        for pp in range(2):
            pj = g * 2 + pp
            row = 2048 + pj * 128
            if row < 2560:
                src = gbA[:, row: row + 128, bi * 128:(bi + 1) * 128]
            else:
                src = gbB[:, row - 2560: row - 2560 + 128, bi * 128:(bi + 1) * 128]
            dst = kvb[slot][:, 4096 + pp * 1024: 4096 + (pp + 1) * 1024].rearrange(
                "p (r t) -> p r t", r=8)
            op("gpsimd", "dma_start", out=dst, in_=src.rearrange("r f c -> f r c"),
               waits=(w if pp == 0 else ()), inc=s_kvg[slot].bump(16))
        dcol0 = h0 * 128
        half = dcol0 // 1024
        col0 = dcol0 % 1024
        vrow = 3072 - 2560
        src = gbB[:, vrow + 256 * bi: vrow + 256 * (bi + 1), :].rearrange(
            "r (m two) c -> r m two c", two=2)[:, :, half, col0:col0 + 512]
        dst = kvb[slot][:, 6144:6144 + 4096].rearrange("p (r d) -> p r d", r=8)
        op("gpsimd", "dma_start", out=dst, in_=src.rearrange("r m d -> m r d"),
           inc=s_kvg[slot].bump(16))
        a_["kv_snap"][slot] = (s_kv[slot].cur(), s_kvg[slot].cur(), s_kvs[slot].cur())

    def q_load(bt, g, slot):
        w = [QS_TOT]
        if a_["q_free"][slot] is not None:
            w.append((s_t.h, a_["q_free"][slot]))
        for hh in range(4):
            h = g * 4 + hh
            op("sync", "dma_start", out=qbuf[slot][:, hh * 1024:(hh + 1) * 1024],
               in_=qscr[h * 128:(h + 1) * 128, bt * TPB:(bt + 1) * TPB],
               waits=(w if hh == 0 else ()), inc=s_q[slot].bump(16))
        for pp in range(2):
            pj = g * 2 + pp
            op("sync", "dma_start", out=qbuf[slot][:, 4096 + pp * 1024: 4096 + (pp + 1) * 1024],
               in_=qscr[2048 + pj * 128: 2048 + (pj + 1) * 128, bt * TPB:(bt + 1) * TPB],
               inc=s_q[slot].bump(16))
        a_["q_snap"][slot] = s_q[slot].cur()

    def flush_onst():
        for (h, bt2, osl, onsnap) in a_["pend_onst"]:
            op("sync", "dma_start",
               out=onorm[h * 128:(h + 1) * 128, bt2 * TPB:(bt2 + 1) * TPB],
               in_=onstg[osl][:, :], waits=[(s_v.h, onsnap)], inc=s_onst.bump(16))
            if bt2 == 0:
                a_["onst_b0"] = s_onst.v
        a_["pend_onst"] = []

    def attention_sweep(bt, g, nxt=None):
        si = a_["sweep"]
        a_["sweep"] += 1
        qsl = si % 2
        if si == 0:
            q_load(bt, g, qsl)
            kv_load(*KVPOS[0], 0)
            kv_load(*KVPOS[1], 1)
            kv_load(*KVPOS[2], 2)
        qsem = a_["q_snap"][qsl]
        for bi in range(8):
            if si == 1 and bi == 0 and cc1_w[0] is not None:
                emit_cc(1, cc1_w[0])
                cc1_w[0] = None
            pf = si * 8 + bi + 3
            if pf < len(KVPOS):
                kv_load(*KVPOS[pf], pf % 4)
            if bi == 5 and nxt is not None:
                q_load(nxt[0], nxt[1], (si + 1) % 2)
            ksl = (si * 8 + bi) % 4
            if bi == 1:
                flush_onst()
            kvsem, kvgsem, kvssem = a_["kv_snap"][ksl]
            Nq = 1024 - 128 * bi
            nqt = (Nq + 511) // 512
            qoff = 128 * bi
            for pair in range(2):
                for qh in range(nqt):
                    wq = min(512, Nq - qh * 512)
                    q0 = qoff + qh * 512
                    pend = []

                    def emit_attnv(j, psl, msnap, wq=wq, q0=q0, sem=None):
                        aw = [((sem or s_v).h, msnap)]
                        if j == 0 and a_["ol_free"] is not None:
                            aw.append((s_v.h, a_["ol_free"]))
                        for lh in range(2):
                            hh = pair * 2 + lh
                            vlhs = kvb[ksl][:, 6144 + j * 512 + hh * 128:
                                            6144 + j * 512 + hh * 128 + 128]
                            rhs = ptil[psl][:, lh * 512: lh * 512 + wq]
                            op("tensor", "matmul",
                               PS[0:1, 3072 + lh * 512: 3072 + lh * 512 + wq],
                               sb_ones[:, 0:1], rhs,
                               start=(j == 0), stop=(j == 7),
                               waits=(aw if lh == 0 else ()))
                            op("tensor", "matmul",
                               PS[:, 2048 + lh * 512: 2048 + lh * 512 + wq],
                               vlhs, rhs,
                               start=(j == 0), stop=(j == 7))
                        tinc("tensor", s_t.bump())
                        a_["pt_cons"][psl] = s_t.v

                    for j in range(8):
                        par = j % 2
                        w = [kvsem, kvgsem, kvssem, qsem]
                        if a_["sc_evac"][par] is not None:
                            w.append((s_s.h, a_["sc_evac"][par]))
                        if a_["bc_evac"] is not None:
                            w.append((s_v.h, a_["bc_evac"]))
                        for lh in range(2):
                            hh = pair * 2 + lh
                            lhs_n = kvb[ksl][:, hh * 1024 + j * 128: hh * 1024 + j * 128 + 128]
                            rq_n = qbuf[qsl][:, hh * 1024 + q0: hh * 1024 + q0 + wq]
                            op("tensor", "matmul",
                               PS[:, par * 1024 + lh * 512: par * 1024 + lh * 512 + wq],
                               lhs_n, rq_n, start=True, stop=False,
                               waits=(w if lh == 0 else ()))
                        for lh in range(2):
                            hh = pair * 2 + lh
                            half = hh % 2
                            lhs_r = kvb[ksl][64 * half:64 * half + 64,
                                             4096 + pair * 1024 + j * 128:
                                             4096 + pair * 1024 + j * 128 + 128]
                            rq_r = qbuf[qsl][64 * half:64 * half + 64,
                                             4096 + pair * 1024 + q0:
                                             4096 + pair * 1024 + q0 + wq]
                            op("tensor", "matmul",
                               PS[:, par * 1024 + lh * 512: par * 1024 + lh * 512 + wq],
                               lhs_r, rq_r, start=False, stop=True)
                        tinc("tensor", s_t.bump())
                        scsnap = s_t.v
                        psl = a_["ptc"] % 6
                        a_["ptc"] += 1
                        ew = [(s_t.h, scsnap)]
                        if a_["pt_cons"][psl] is not None:
                            ew.append((s_t.h, a_["pt_cons"][psl]))
                        op("scalar", "activation", out=ptil[psl][:, 0:512 + wq],
                           in_=PS[:, par * 1024: par * 1024 + 512 + wq],
                           func=AF.Exp, bias=sb_sm[:, 33:34], scale=SCALE,
                           waits=ew, inc=s_s.bump(), nodrain=True)
                        a_["sc_evac"][par] = s_s.v
                        expsnap = s_s.v
                        if qh == 0:
                            op("vector", "tensor_tensor", out=ptil[psl][:, 0:128],
                               in0=ptil[psl][:, 0:128],
                               in1=sb_mask[:, j * 128:(j + 1) * 128],
                               op=ALU.mult, waits=[(s_s.h, expsnap)], nodrain=True)
                            op("vector", "tensor_tensor", out=ptil[psl][:, 512:640],
                               in0=ptil[psl][:, 512:640],
                               in1=sb_mask[:, j * 128:(j + 1) * 128],
                               op=ALU.mult, inc=s_v.bump(), nodrain=True)
                            msnap = s_v.v
                        else:
                            msnap = None
                        pend.append((j, psl, msnap if msnap is not None else expsnap))
                        while len(pend) > 2:
                            jj, psl2, sn2 = pend.pop(0)
                            emit_attnv(jj, psl2, sn2,
                                       sem=(s_v if qh == 0 else s_s))
                    while pend:
                        jj, psl2, sn2 = pend.pop(0)
                        emit_attnv(jj, psl2, sn2,
                                   sem=(s_v if qh == 0 else s_s))
                    avsnap = s_t.v
                    o_out = osbb[:, pair * 2048: pair * 2048 + 2048].rearrange(
                        "p (g c) -> p g c", g=2)[:, :, q0:q0 + wq]
                    o_in = PS[:, 2048:3072].rearrange("p (g c) -> p g c", g=2)[:, :, 0:wq]
                    l_out = lsbb[0:1, pair * 2048: pair * 2048 + 2048].rearrange(
                        "p (g c) -> p g c", g=2)[:, :, q0:q0 + wq]
                    l_in = PS[0:1, 3072:4096].rearrange("p (g c) -> p g c", g=2)[:, :, 0:wq]
                    if bi == 0:
                        op("vector", "tensor_copy", out=o_out, in_=o_in,
                           waits=[(s_t.h, avsnap)], nodrain=True)
                        op("vector", "tensor_copy", out=l_out, in_=l_in,
                           inc=s_v.bump(), nodrain=True)
                    else:
                        op("vector", "tensor_tensor", out=o_out, in0=o_out, in1=o_in,
                           op=ALU.add, waits=[(s_t.h, avsnap)], nodrain=True)
                        op("vector", "tensor_tensor", out=l_out, in0=l_out, in1=l_in,
                           op=ALU.add, inc=s_v.bump(), nodrain=True)
                    a_["ol_free"] = s_v.v
            a_["kv_free"][ksl] = s_t.v
        a_["q_free"][qsl] = s_t.v
        rsnaps = []
        for hh in range(4):
            lview = lsbb[0:1, hh * 1024:(hh + 1) * 1024]
            op("vector", "reciprocal", lview, lview, nodrain=(hh > 0))
            op("vector", "tensor_copy", out=lsb16s[hh][0:1, :], in_=lview,
               inc=s_v.bump())
            rsnaps.append(s_v.v)
        mult_snap = [None, None]
        for hh in range(4):
            h = g * 4 + hh
            pb_off = (hh % 2) * 1024
            bw = [(s_v.h, rsnaps[hh])]
            if mult_snap[hh % 2] is not None:
                bw.append((s_v.h, mult_snap[hh % 2]))
            elif a_["bc_evac"] is not None:
                bw.append((s_v.h, a_["bc_evac"]))
            for k4 in range(4):
                if a_["sc_evac"][k4] is not None:
                    bw.append((s_s.h, a_["sc_evac"][k4]))
            for qt in range(2):
                op("tensor", "matmul", PS[:, pb_off + qt * 512: pb_off + qt * 512 + 512],
                   sb_ones[0:1, :], lsb16s[hh][0:1, qt * 512: qt * 512 + 512],
                   start=True, stop=True, waits=(bw if qt == 0 else ()))
            tinc("tensor", s_t.bump())
            bsnap = s_t.v
            osl = a_["oni"] % 4
            a_["oni"] += 1
            ow = [(s_t.h, bsnap), s_onst.cur()]
            op("vector", "tensor_tensor", out=onstg[osl][:, :],
               in0=osbb[:, hh * 1024:(hh + 1) * 1024],
               in1=PS[:, pb_off: pb_off + 1024], op=ALU.mult, waits=ow, inc=s_v.bump())
            mult_snap[hh % 2] = s_v.v
            a_["bc_evac"] = s_v.v
            onsnap = s_v.v
            a_["pend_onst"].append((h, bt, osl, onsnap))

    for sidx, (bt, g) in enumerate(SWEEPS):
        nxt = SWEEPS[sidx + 1] if sidx + 1 < len(SWEEPS) else None
        attention_sweep(bt, g, nxt=nxt)
    flush_onst()
    ON_TOT = s_onst.cur()
    ON_B0 = (s_onst.h, a_.get("onst_b0", s_onst.v))
    if "p4" not in phases:
        wt("sync", ON_TOT)

    if "p4" not in phases:
        return _finish(nc, es, ops, ENGS, NCORE, drains)
    AB.reset(); AFA.reset()
    AB.mark(); AFA.mark()
    woT = AB.alloc(16 * 2048)
    onb = [AB.alloc(2048) for _ in range(2)]
    x2l = [AB.alloc(2048) for _ in range(2)]
    x2T = AB.alloc(16 * 256)
    aTb = AB.alloc(16 * E * R)
    bwTb = AB.alloc(E * D, parts=64)
    wgTb = AB.alloc(16 * E)
    ghb = AB.alloc(E * 256, parts=64)
    gbc = AB.alloc(256, parts=64)
    gtb = AB.alloc(256, parts=8)
    scr8 = AB.alloc(8)
    xf4 = [AFA.alloc(2048) for _ in range(2)]
    xnw = [AFA.alloc(2048) for _ in range(2)]

    P3T, P3V, P3S = s_t.cur(), s_v.cur(), s_s.cur()
    wt("sync", P3T, P3V, P3S)
    wt("gpsimd", P3T, P3V, P3S)

    op("gpsimd", "dma_start", out=woT[:, 0:8 * 2048].rearrange("p (c d) -> p c d", c=8),
       in_=woTr.rearrange("(c p) d -> p c d", p=128)[:, 0:8, :], inc=s_w[0].bump(16))
    P4WA = s_w[0].cur()
    op("gpsimd", "dma_start", out=woT[:, 8 * 2048:].rearrange("p (c d) -> p c d", c=8),
       in_=woTr.rearrange("(c p) d -> p c d", p=128)[:, 8:16, :], inc=s_w[0].bump(16))
    P4WB = s_w[0].cur()
    op("gpsimd", "dma_start", out=aTb, in_=aTh, inc=s_w[0].bump(16))
    op("gpsimd", "dma_start", out=bwTb, in_=bwTh, inc=s_w[0].bump(16))
    op("gpsimd", "dma_start", out=wgTb, in_=wgTh, inc=s_w[0].bump(16))
    P4W = s_w[0].cur()

    f_ = dict(on_free=[None, None], x_free=[None, None], tp_evac=[None, None],
              x2l_free=[None, None], x2T_cons=None, logit_free=None,
              gt_free=None, gbc_free=None, h_free=[None, None],
              py_free=[None, None], xn_store=[None, None], op_free=None,
              y67_free=None, xnw_snap=[None, None])

    def p4_loads(bt, gg):
        for ii, tt in enumerate((gg * 2, gg * 2 + 1)):
            r0 = bt * TPB + tt * 128
            osl = tt % 2
            xsl = tt % 2
            w = [ON_B0 if bt == 0 else ON_TOT]
            if f_["on_free"][osl] is not None:
                w.append((s_t.h, f_["on_free"][osl]))
            src2 = onorm.rearrange("(c p) t -> p c t", p=128)[:, :, r0: r0 + 128]
            onv = onb[osl].rearrange("p (c t) -> p c t", c=16)
            op("sync", "dma_start", out=onv[:, 0:8, :], in_=src2[:, 0:8, :],
               waits=w, inc=s_on[osl].bump(16))
            op("sync", "dma_start", out=onv[:, 8:16, :], in_=src2[:, 8:16, :],
               inc=s_on[osl].bump(16))
            xw = []
            if f_["x_free"][xsl] is not None:
                xw.append((s_v.h, f_["x_free"][xsl]))
            op("sync", "dma_start", out=xf4[xsl], in_=x_sl[r0: r0 + 128, :], waits=xw,
               inc=s_x[xsl].bump(16))

    def p4_opmm(bt, gg, ii):
        tt = gg * 2 + ii
        osl = tt % 2
        mw = [s_on[osl].cur(), P4WA]
        if f_["op_free"] is not None:
            mw.append((s_v.h, f_["op_free"]))
        for hc in range(16):
            for dt in range(4):
                w_ = ()
                if hc == 0 and dt == 0:
                    w_ = mw
                elif hc == 8 and dt == 0:
                    w_ = [P4WB]
                op("tensor", "matmul", PS[:, dt * 512: dt * 512 + 512],
                   onb[osl][:, hc * 128:(hc + 1) * 128],
                   woT[:, hc * 2048 + dt * 512: hc * 2048 + dt * 512 + 512],
                   start=(hc == 0), stop=False,
                   waits=w_)
        for dt in range(4):
            op("tensor", "matmul", PS[:, dt * 512: dt * 512 + 512],
               sb_ones[0:1, 0:128], sb_bo16[0:1, dt * 512:(dt + 1) * 512],
               start=False, stop=True)
        tinc("tensor", s_t.bump())
        f_["on_free"][osl] = s_t.v
        f_["opmm_snap_%d" % ii] = s_t.v

    def p4_post_add(bt, gg, ii):
        tt = gg * 2 + ii
        xsl = tt % 2
        mmsnap = f_["opmm_snap_%d" % ii]
        if f_["xn_store"][ii] is not None:
            wt("vector", (s_out[ii].h, f_["xn_store"][ii]))
        vw = [(s_t.h, mmsnap), s_x[xsl].cur()]
        for dt in range(4):
            last = dt == 3
            op("vector", "tensor_tensor", out=xnw[ii][:, dt * 512:(dt + 1) * 512],
               in0=xf4[xsl][:, dt * 512:(dt + 1) * 512], in1=PS[:, dt * 512: dt * 512 + 512],
               op=ALU.add, waits=(vw if dt == 0 else ()),
               inc=(s_v.bump() if last else None), nodrain=(dt > 0))
        f_["x_free"][xsl] = s_v.v
        f_["xnw_snap"][ii] = s_v.v
        f_["op_free"] = s_v.v

    def p4_post_ln(bt, gg, ii):
        if f_.get("p4_ln_s") is not None:
            wt("vector", (s_s.h, f_["p4_ln_s"]))
        for k in range(4):
            op("vector", "bn_stats", bnst[:, 6 * k:6 * k + 6],
               xnw[ii][:, 512 * k:512 * (k + 1)], nodrain=(k > 0))
        op("vector", "bn_aggr", mv[:, :], bnst[:, :], inc=s_v.bump())
        wt("scalar", (s_v.h, s_v.v))
        op("scalar", "activation", out=rstd[:, :], in_=mv[:, 1:2], func=AF.Ln,
           bias=sb_sm[:, 32:33], scale=1.0)
        op("scalar", "activation", out=rstd[:, :], in_=rstd[:, :], func=AF.Exp, bias=0.0,
           scale=-0.5)
        op("scalar", "activation", out=negr[:, :], in_=rstd[:, :], func=AF.Identity,
           bias=sb_sm[:, 34:35], scale=-1.0)
        op("scalar", "activation", out=nmr[:, :], in_=mv[:, 0:1], func=AF.Identity,
           bias=sb_sm[:, 34:35], scale=negr[:, 0:1])
        x2w = []
        if f_["x2l_free"][ii] is not None:
            x2w.append((s_t.h, f_["x2l_free"][ii]))
        op("scalar", "activation", out=x2l[ii][:, :], in_=xnw[ii], func=AF.Identity,
           bias=nmr[:, 0:1], scale=rstd[:, 0:1], waits=x2w, inc=s_s.bump())
        x2snap = s_s.v
        f_["p4_ln_s"] = s_s.v
        for g4 in range(4):
            par = g4 % 2
            pb = PS[:, 2048 + par * 512: 2048 + par * 512 + 512].bitcast(BF16)
            tw = [(s_s.h, x2snap)]
            if f_["tp_evac"][par] is not None:
                tw.append((s_s.h, f_["tp_evac"][par]))
            if f_["h_free"][par] is not None:
                tw.append((s_v.h, f_["h_free"][par]))
            for j in range(4):
                dc = g4 * 4 + j
                op("tensor", "transpose", pb[:, j * 128:(j + 1) * 128],
                   x2l[ii][:, dc * 128:(dc + 1) * 128], sb_id[:, :],
                   waits=(tw if j == 0 else ()))
            tinc("tensor", s_t.bump())
            tsn = s_t.v
            dview = x2T.rearrange("p (c t) -> p c t", c=16)[:, g4 * 4: g4 * 4 + 4,
                                                            ii * 128:(ii + 1) * 128]
            ew = [(s_t.h, tsn)]
            if ii == 0 and g4 == 0 and f_["x2T_cons"] is not None:
                ew.append((s_t.h, f_["x2T_cons"]))
            op("scalar", "copy", out=dview,
               in_=pb[:, 0:512].rearrange("p (a b) -> p a b", a=4),
               waits=ew, inc=s_s.bump(), nodrain=True)
            f_["tp_evac"][par] = s_s.v
        f_["x2l_free"][ii] = s_t.v

    def p4_ffn(bt, gg, next_cb=None):
        tts = (gg * 2, gg * 2 + 1)
        x2T_snap = s_s.cur()
        for ii in range(2):
            lw = [x2T_snap, P4W]
            if f_["logit_free"] is not None:
                lw.append((s_v.h, f_["logit_free"]))
            if f_["y67_free"] is not None:
                lw.append((s_v.h, f_["y67_free"]))
            for dc in range(16):
                op("tensor", "matmul", PS[:, 3072:3080],
                   x2T.rearrange("p (c t) -> p c t", c=16)[:, dc, ii * 128:(ii + 1) * 128],
                   wgTb.rearrange("p (c e) -> p c e", c=16)[:, dc, :],
                   start=(dc == 0), stop=(dc == 15), waits=(lw if dc == 0 else ()))
            tinc("tensor", s_t.bump())
            lsnap = s_t.v
            op("vector", "tensor_reduce", sb_sm[:, 28:29], PS[:, 3072:3080],
               mybir.AxisListType.X, ALU.max, waits=[(s_t.h, lsnap)])
            op("vector", "tensor_scalar", out=sb_sm[:, 29:30], in0=sb_sm[:, 28:29],
               scalar1=-1.0, scalar2=None, op0=ALU.mult, inc=s_v.bump())
            wt("scalar", (s_v.h, s_v.v))
            op("scalar", "activation", out=scr8[:, :], in_=PS[:, 3072:3080], func=AF.Exp,
               bias=sb_sm[:, 29:30], scale=1.0, accum_out=sb_sm[:, 30:31], inc=s_s.bump())
            wt("vector", (s_s.h, s_s.v))
            op("vector", "reciprocal", sb_sm[:, 31:32], sb_sm[:, 30:31])
            op("vector", "tensor_scalar", out=scr8[:, :], in0=PS[:, 3072:3080],
               scalar1=sb_sm[:, 29:30], scalar2=0.0, op0=ALU.add, op1=ALU.is_equal)
            op("vector", "tensor_scalar", out=scr8[:, :], in0=scr8[:, :],
               scalar1=sb_sm[:, 31:32], scalar2=MOE_SCALING, op0=ALU.mult, op1=ALU.mult,
               inc=s_v.bump())
            f_["logit_free"] = s_v.v
            gsnap = s_v.v
            pgt = PS[:, 3584:3648].bitcast(BF16)
            gw = [(s_v.h, gsnap)]
            if f_["gt_free"] is not None:
                gw.append((s_v.h, f_["gt_free"]))
            if f_["y67_free"] is not None:
                gw.append((s_v.h, f_["y67_free"]))
            op("tensor", "transpose", pgt[0:8, 0:128], scr8[:, :], sb_id[:, :], waits=gw)
            tinc("tensor", s_t.bump())
            gtsn = s_t.v
            op("vector", "tensor_copy", out=gtb[0:8, ii * 128:(ii + 1) * 128],
               in_=pgt[0:8, 0:128], waits=[(s_t.h, gtsn)], inc=s_v.bump())
            f_["gt_free"] = s_v.v
        gT_snap = s_v.cur()

        if next_cb is not None:
            next_cb()

        for e in range(E):
            par = e % 2
            bcw = [gT_snap]
            if f_["gbc_free"] is not None:
                bcw.append((s_v.h, f_["gbc_free"]))
            if f_["y67_free"] is not None:
                bcw.append((s_v.h, f_["y67_free"]))
            op("tensor", "matmul", PS[0:64, 3584:3840], sb_oh[:, e * 64:(e + 1) * 64],
               gtb[0:8, 0:256], start=True, stop=True, waits=bcw)
            tinc("tensor", s_t.bump())
            bcsnap = s_t.v
            op("vector", "tensor_copy", out=gbc[0:64, :], in_=PS[0:64, 3584:3840],
               waits=[(s_t.h, bcsnap)], inc=s_v.bump())
            gbcsnap = s_v.v
            f_["gbc_free"] = gbcsnap
            hw = [(s_v.h, gbcsnap), P4W]
            if f_["h_free"][par] is not None:
                hw.append((s_v.h, f_["h_free"][par]))
            for dc in range(16):
                op("tensor", "matmul", PS[0:64, 2048 + par * 512: 2048 + par * 512 + 256],
                   aTb.rearrange("p (c e r) -> p c e r", c=16, e=E)[:, dc, e, :],
                   x2T.rearrange("p (c t) -> p c t", c=16)[:, dc, :],
                   start=(dc == 0), stop=(dc == 15), waits=(hw if dc == 0 else ()))
            tinc("tensor", s_t.bump())
            hsnap = s_t.v
            op("vector", "tensor_tensor", out=ghb[0:64, e * 256:(e + 1) * 256],
               in0=PS[0:64, 2048 + par * 512: 2048 + par * 512 + 256], in1=gbc[0:64, :],
               op=ALU.mult, waits=[(s_t.h, hsnap)], inc=s_v.bump())
            f_["h_free"][par] = s_v.v
        gh_snap = s_v.cur()
        f_["x2T_cons"] = s_t.v

        ycnt = 0
        for ii in range(2):
            tt = tts[ii]
            r0 = bt * TPB + tt * 128
            for dt in range(4):
                yb = 3072 + (ycnt % 2) * 512
                ycnt += 1
                yw = [gh_snap, P4W]
                if f_["py_free"][(ycnt - 1) % 2] is not None:
                    yw.append((s_v.h, f_["py_free"][(ycnt - 1) % 2]))
                for e in range(E):
                    op("tensor", "matmul", PS[:, yb: yb + 512],
                       ghb[0:64, e * 256 + ii * 128: e * 256 + ii * 128 + 128],
                       bwTb[0:64, e * 2048 + dt * 512: e * 2048 + dt * 512 + 512],
                       start=(e == 0), stop=(e == E - 1), waits=(yw if e == 0 else ()))
                tinc("tensor", s_t.bump())
                ysnap = s_t.v
                op("vector", "tensor_tensor", out=xnw[ii][:, dt * 512:(dt + 1) * 512],
                   in0=xnw[ii][:, dt * 512:(dt + 1) * 512],
                   in1=PS[:, yb: yb + 512],
                   op=ALU.add, waits=[(s_t.h, ysnap)], inc=s_v.bump(), nodrain=(dt > 0))
                f_["py_free"][(ycnt - 1) % 2] = s_v.v
                f_["y67_free"] = s_v.v
            donesnap = s_v.v
            op("sync", "dma_start", out=out_sl[r0: r0 + 128, :], in_=xnw[ii][:, :],
               waits=[(s_v.h, donesnap)], inc=s_out[ii].bump(16))
            f_["xn_store"][ii] = s_out[ii].v

    groups = [(bt, gg) for bt in range(2) for gg in range(4)]
    p4_loads(*groups[0])
    p4_opmm(*groups[0], 0)
    p4_post_add(*groups[0], 0)
    p4_opmm(*groups[0], 1)
    p4_post_ln(*groups[0], 0)
    p4_post_add(*groups[0], 1)
    p4_post_ln(*groups[0], 1)
    for gi, (bt, gg) in enumerate(groups):
        nxt = groups[gi + 1] if gi + 1 < len(groups) else None
        if nxt is not None:
            p4_loads(*nxt)

            def cb(nxt=nxt):
                p4_opmm(*nxt, 0)
            p4_ffn(bt, gg, next_cb=cb)
            p4_post_add(*nxt, 0)
            p4_opmm(*nxt, 1)
            p4_post_ln(*nxt, 0)
            p4_post_add(*nxt, 1)
            p4_post_ln(*nxt, 1)
        else:
            p4_ffn(bt, gg)

    wt("sync", s_out[0].cur(), s_out[1].cur())
    return _finish(nc, es, ops, ENGS, NCORE, drains)


def _finish(nc, es, ops, engs, ncore, drains=False):
    block = es.enter_context(nc.Block())

    def runner(lst, auto_drain=False):
        def go(e):
            waited = {}
            for meth, args, kw, waits, inc in lst:
                for (h, v) in waits:
                    k = id(h)
                    if waited.get(k, -1) >= v:
                        continue
                    e.wait_ge(h, v)
                    waited[k] = v
                if meth == "_wait":
                    continue
                if meth == "_drain":
                    e.drain()
                    continue
                if meth.startswith("!"):
                    meth = meth[1:]
                elif auto_drain and meth not in ("_nop", "dma_start"):
                    e.drain()
                if meth == "_nop":
                    ins = e.nop()
                elif meth == "collective":
                    ins = e.collective_compute(
                        kw["kind"], ALU.bypass,
                        replica_groups=[list(range(ncore))],
                        ins=[a.opt() for a in kw["cins"]],
                        outs=[a.opt() for a in kw["couts"]],
                    )
                    if inc is not None:
                        ins.then_inc(inc[0])
                    continue
                else:
                    ins = getattr(e, meth)(*args, **kw)
                if inc is not None:
                    ins.then_inc(inc[0], inc[2])
        return go

    @block.tensor
    def _(e):
        runner(ops["tensor"])(e)

    @block.vector
    def _(e):
        runner(ops["vector"], auto_drain=drains)(e)

    @block.scalar
    def _(e):
        runner(ops["scalar"], auto_drain=True)(e)

    @block.sync
    def _(e):
        runner(ops["sync"])(e)

    @block.gpsimd
    def _(e):
        runner(ops["gpsimd"])(e)

    es.close()
    return nc



def _prep_inputs(kw):
    f32 = np.float32
    bf = ml_dtypes.bfloat16
    x = np.asarray(kw["L0_attn_x"], f32)

    def T(a):
        return np.ascontiguousarray(np.asarray(a, f32).T)

    wkdT = T(kw["L0_attn_Wkd"])
    wqdT = T(kw["L0_attn_Wqd"])
    wkuT = T(kw["L0_attn_Wku"])
    wvuT = T(kw["L0_attn_Wvu"])
    wquT = T(kw["L0_attn_Wqu"])
    wkrT = np.ascontiguousarray(np.asarray(kw["L0_attn_Wkr"], f32)[ROPE_ORDER, :].T)
    wqrT = np.ascontiguousarray(np.asarray(kw["L0_attn_Wqr"], f32)[ROPE_ORDER, :].T)
    woTr = T(kw["L0_attn_Wo"])
    A = np.asarray(kw["L0_ffn_A"], f32)
    Bw = np.asarray(kw["L0_ffn_Bw"], f32)
    Wg = np.asarray(kw["L0_ffn_Wg"], f32)

    aTh = np.ascontiguousarray(
        A.transpose(2, 0, 1).reshape(16, 128, E, R).transpose(1, 0, 2, 3).reshape(128, 16 * E * R))
    bwTh = np.ascontiguousarray(Bw.transpose(2, 0, 1).reshape(64, E * D))
    wgTh = np.ascontiguousarray(
        Wg.T.reshape(16, 128, E).transpose(1, 0, 2).reshape(128, 16 * E))

    def b2(v, nf):
        return np.ascontiguousarray(np.asarray(v, f32).reshape(nf, 128).T)

    bkd2 = b2(kw["L0_attn_bkd"], 4)
    bqd2 = b2(kw["L0_attn_bqd"], 12)
    bku2 = b2(kw["L0_attn_bku"], 16)
    bvu2 = b2(kw["L0_attn_bvu"], 16)
    bqu2 = b2(kw["L0_attn_bqu"], 16)
    bkr2 = b2(np.asarray(kw["L0_attn_bkr"], f32)[ROPE_ORDER], 8)
    bqr2 = b2(np.asarray(kw["L0_attn_bqr"], f32)[ROPE_ORDER], 8)
    bo2 = np.ascontiguousarray(np.asarray(kw["L0_attn_bo"], f32).reshape(1, D))

    ident = np.eye(128, dtype=bf)
    ones = np.ones((128, 128), dtype=bf)
    onehot = np.zeros((8, E * 64), dtype=bf)
    for e in range(E):
        onehot[e, e * 64:(e + 1) * 64] = 1.0

    inv_freq = (1.0 / (10000.0 ** (np.arange(0, DR, 2, dtype=np.float64) / DR))).astype(f32)

    shared = dict(wkdT=wkdT, wqdT=wqdT, wkuT=wkuT, wvuT=wvuT, wquT=wquT,
                  wkrT=wkrT, wqrT=wqrT, woTr=woTr, aTh=aTh, bwTh=bwTh, wgTh=wgTh,
                  bkd2=bkd2, bqd2=bqd2, bku2=bku2, bvu2=bvu2, bqu2=bqu2,
                  bkr2=bkr2, bqr2=bqr2, bo2=bo2, identt=ident, onest=ones,
                  onehot=onehot)
    in_maps = []
    for c in range(NCORE):
        pos = (8 * np.arange(TPB, dtype=f32) + c).astype(f32)
        ang = pos[None, :] * inv_freq[:, None]
        cos64 = np.cos(ang).astype(f32)
        sin64 = np.sin(ang).astype(f32)
        cosp = np.empty((64, TPB), f32)
        sinp = np.empty((64, TPB), f32)
        for j, d in enumerate(PERM64):
            cosp[j] = cos64[d % 32]
            sinp[j] = sin64[d % 32] * (-1.0 if d < 32 else 1.0)
        costab = np.ascontiguousarray(np.concatenate([cosp, cosp], 0))
        sintb = np.ascontiguousarray(np.concatenate([sinp, sinp], 0))
        m = np.arange(128)
        jj = np.arange(8)
        mq = np.arange(128)
        mk = (8 * m[:, None, None] + jj[None, :, None] <= 8 * mq[None, None, :] + c)
        masktab = np.ascontiguousarray(mk.astype(bf).reshape(128, 8 * 128))
        x_c = np.ascontiguousarray(x[:, c::8, :].reshape(2 * TPB, D))
        im = dict(shared)
        im.update(x_sl=x_c, cost=costab, sint=sintb, maskt=masktab)
        in_maps.append(im)
    return in_maps


_NC_CACHE = {}


def kernel(**kw):
    in_maps = _prep_inputs(kw)
    if "nc" not in _NC_CACHE:
        _NC_CACHE["nc"] = build_nc(drains=True)
    nc = _NC_CACHE["nc"]
    res = run_bass_kernel_spmd(nc, in_maps, core_ids=list(range(NCORE)))
    out = np.empty((B, S, D), np.float32)
    for c in range(NCORE):
        o = np.asarray(res.results[c]["out_sl"], np.float32).reshape(B, TPB, D)
        out[:, c::8, :] = o
    return out


# version 28 (final)
# speedup vs baseline: 1.2036x; 1.0405x over previous
import math
import numpy as np
import ml_dtypes
import concourse.bass as bass
import concourse.mybir as mybir
from concourse.bass_utils import run_bass_kernel_spmd
from contextlib import ExitStack

F32 = mybir.dt.float32
BF16 = mybir.dt.bfloat16
AF = mybir.ActivationFunctionType
ALU = mybir.AluOpType

B, S, D = 2, 8192, 2048
H, DH, DR = 16, 128, 64
DCKV, DCQ = 512, 1536
E, R = 8, 64
NCORE = 8
TPB = S // NCORE
SCALE = 1.0 / math.sqrt(DH + DR)
EXPB = 12.0
EPS = 1e-5
MOE_SCALING = 4.0

PERM64 = list(range(0, 16)) + list(range(32, 48)) + list(range(16, 32)) + list(range(48, 64))
ROPE_ORDER = [h * 64 + j for h in range(H) for j in PERM64]

ENGS = ("tensor", "vector", "scalar", "sync", "gpsimd")
LN_CUT = None


class Sem:
    def __init__(self, nc, es, name):
        self.h = es.enter_context(nc.semaphore(name))
        self.v = 0

    def bump(self, n=1):
        self.v += n
        return (self.h, self.v, n)

    def cur(self):
        return (self.h, self.v)


class Arena:
    def __init__(self, t, cols):
        self.t, self.cols, self.off, self.stack = t, cols, 0, []

    def alloc(self, n, parts=128):
        assert self.off + n <= self.cols, ("arena overflow", self.off, n, self.cols)
        sl = self.t[0:parts, self.off:self.off + n]
        self.off += n
        return sl

    def mark(self):
        self.stack.append(self.off)

    def reset(self):
        self.off = self.stack.pop()


def build_nc(dbg=False, phases=("p1", "p3", "p4"), p1_subs=None, p1_passes=2, do_cc=True, drains=False):
    nc = bass.Bass("TRN2", target_bir_lowering=False, debug=False)

    def din(name, shape, dt=F32):
        return nc.dram_tensor(name, shape, dt, kind="ExternalInput").ap()

    x_sl = din("x_sl", [2 * TPB, D])
    wkdT = din("wkdT", [D, DCKV])
    wqdT = din("wqdT", [D, DCQ])
    wkuT = din("wkuT", [DCKV, H * DH])
    wvuT = din("wvuT", [DCKV, H * DH])
    wquT = din("wquT", [DCQ, H * DH])
    wkrT = din("wkrT", [D, H * DR])
    wqrT = din("wqrT", [DCQ, H * DR])
    woTr = din("woTr", [H * DH, D])
    aTh = din("aTh", [128, 16 * E * R])
    bwTh = din("bwTh", [64, E * D])
    wgTh = din("wgTh", [128, 16 * E])
    bkd2 = din("bkd2", [128, 4])
    bqd2 = din("bqd2", [128, 12])
    bku2 = din("bku2", [128, 16])
    bvu2 = din("bvu2", [128, 16])
    bqu2 = din("bqu2", [128, 16])
    bkr2 = din("bkr2", [128, 8])
    bqr2 = din("bqr2", [128, 8])
    bo2 = din("bo2", [1, D])
    cost = din("cost", [128, TPB])
    sintab = din("sint", [128, TPB])
    masktab = din("maskt", [128, 8 * 128], BF16)
    identt = din("identt", [128, 128], BF16)
    onest = din("onest", [128, 128], BF16)
    onehot = din("onehot", [8, E * 64], BF16)

    out_sl = nc.dram_tensor("out_sl", [2 * TPB, D], F32, kind="ExternalOutput").ap()

    kscr = "ExternalOutput" if dbg else "Internal"
    def dscr(name, shape):
        if kscr == "Internal":
            return nc.dram_tensor(name, shape, BF16)
        return nc.dram_tensor(name, shape, BF16, kind=kscr)

    qscr_t = dscr("qscr", [3 * 1024, 2 * TPB])
    shard_t = [nc.dram_tensor("shard%d" % b, [5 * 1024, TPB], BF16) for b in range(2)]
    dbgsh_t = None
    if dbg:
        dbgsh_t = [nc.dram_tensor("dbgsh%d" % b, [5 * 1024, TPB], BF16, kind="ExternalOutput")
                   for b in range(2)]
    gath_t = [nc.dram_tensor("gath%d" % b, [NCORE * 5 * 1024, TPB], BF16) for b in range(2)]
    onorm_t = dscr("onormscr", [H * DH, 2 * TPB])
    wscr_t = nc.dram_tensor("wscr", [128, 110592], BF16)
    wscr = wscr_t.ap()
    qscr = qscr_t.ap()
    shard = [t.ap() for t in shard_t]
    gath = [t.ap() for t in gath_t]
    onorm = onorm_t.ap()

    es = ExitStack()
    AB = Arena(es.enter_context(nc.sbuf_tensor("ab16", [128, 73728], BF16)), 73728)
    AFA = Arena(es.enter_context(nc.sbuf_tensor("af32", [128, 8704], F32)), 8704)
    PS = es.enter_context(nc.psum_tensor("ps", [128, 4096], F32))

    sb_id = es.enter_context(nc.sbuf_tensor("sb_id", [128, 128], BF16))
    sb_ones = es.enter_context(nc.sbuf_tensor("sb_ones", [128, 128], BF16))
    sb_mask = es.enter_context(nc.sbuf_tensor("sb_mask", [128, 1024], BF16))
    sb_cos = es.enter_context(nc.sbuf_tensor("sb_cos", [128, TPB], F32))
    sb_sin = es.enter_context(nc.sbuf_tensor("sb_sin", [128, TPB], F32))
    sb_bias = es.enter_context(nc.sbuf_tensor("sb_bias", [128, 80], F32))
    sb_bo = es.enter_context(nc.sbuf_tensor("sb_bo", [1, D], F32))
    sb_bo16 = es.enter_context(nc.sbuf_tensor("sb_bo16", [1, D], BF16))
    sb_oh = es.enter_context(nc.sbuf_tensor("sb_oh", [8, E * 64], BF16))
    sb_sm = es.enter_context(nc.sbuf_tensor("sb_sm", [128, 64], F32))

    BOFF = {}
    _o = 0
    for nm, wdt in [("bkd", 4), ("bqd", 12), ("bku", 16), ("bvu", 16), ("bqu", 16),
                    ("bkr", 8), ("bqr", 8)]:
        BOFF[nm] = _o
        _o += wdt

    sems = {}
    def mksem(name):
        s = Sem(nc, es, name)
        sems[name] = s
        return s
    s_cst = mksem("cst")
    s_x = [mksem("x0"), mksem("x1"), mksem("x2")]
    s_w = [mksem("w0"), mksem("w1")]
    s_kv = [mksem("kv%d" % i) for i in range(4)]
    s_kvg = [mksem("kvg%d" % i) for i in range(4)]
    s_kvs = [mksem("kvs%d" % i) for i in range(4)]
    s_wst = mksem("wst")
    s_w2 = [mksem("w20"), mksem("w21")]
    s_dbg = mksem("dbg")
    s_q = [mksem("q0"), mksem("q1")]
    s_on = [mksem("on0"), mksem("on1")]
    s_stg = mksem("stg")
    s_sh = [mksem("sh0"), mksem("sh1")]
    s_onst = mksem("onst")
    s_out = [mksem("outst0"), mksem("outst1")]
    s_cc = [mksem("cc0"), mksem("cc1")]
    s_t = mksem("t")
    s_v = mksem("v")
    s_s = mksem("s")

    ops = {e: [] for e in ENGS}

    def op(eng, meth, *args, waits=(), inc=None, nodrain=False, **kw):
        ops[eng].append((meth if not nodrain else "!" + meth, args, kw, tuple(waits), inc))

    def wt(eng, *waits):
        ops[eng].append(("_wait", (), {}, tuple(waits), None))

    def tinc(eng, bump):
        meth, args, kw, waits, inc = ops[eng][-1]
        assert inc is None and meth not in ("_wait",), (meth, inc)
        ops[eng][-1] = (meth, args, kw, waits, bump)

    for dst, src in [
        (sb_id[:, :], identt), (sb_ones[:, :], onest), (sb_mask[:, :], masktab),
        (sb_cos[:, :], cost), (sb_sin[:, :], sintab), (sb_bo[:, :], bo2),
        (sb_oh[:, :], onehot),
        (sb_bias[:, BOFF["bkd"]:BOFF["bkd"] + 4], bkd2),
        (sb_bias[:, BOFF["bqd"]:BOFF["bqd"] + 12], bqd2),
        (sb_bias[:, BOFF["bku"]:BOFF["bku"] + 16], bku2),
        (sb_bias[:, BOFF["bvu"]:BOFF["bvu"] + 16], bvu2),
        (sb_bias[:, BOFF["bqu"]:BOFF["bqu"] + 16], bqu2),
        (sb_bias[:, BOFF["bkr"]:BOFF["bkr"] + 8], bkr2),
        (sb_bias[:, BOFF["bqr"]:BOFF["bqr"] + 8], bqr2),
    ]:
        op("sync", "dma_start", out=dst, in_=src, inc=s_cst.bump(16))
    CST = s_cst.cur()
    for e in ("tensor", "scalar", "gpsimd", "vector"):
        wt(e, CST)
    op("vector", "memset", sb_sm[:, 32:33], EPS)
    op("vector", "memset", sb_sm[:, 33:34], -EXPB)
    op("vector", "memset", sb_sm[:, 34:35], 0.0)
    op("vector", "tensor_copy", out=sb_bo16[:, :], in_=sb_bo[:, :], inc=s_v.bump())

    AB.mark(); AFA.mark()
    xln = [AB.alloc(2048) for _ in range(2)]
    xlnT = AB.alloc(16 * 1024)
    kvc = AB.alloc(4 * 1024)
    qcs = AB.alloc(12 * 1024)
    stg = [AB.alloc(1024) for _ in range(3)]
    vfm = [AB.alloc(1024) for _ in range(4)]
    krr = [AB.alloc(1024) for _ in range(2)]
    krs = AB.alloc(1024)
    rop = [AB.alloc(1024) for _ in range(2)]
    wbuf = [AB.alloc(12288) for _ in range(2)]
    xf = [AFA.alloc(2048) for _ in range(3)]
    rtmp = [AFA.alloc(1024) for _ in range(2)]
    bnst = sb_sm[:, 0:24]
    mv = sb_sm[:, 24:26]
    rstd = sb_sm[:, 26:27]
    nmr = sb_sm[:, 27:28]
    negr = sb_sm[:, 35:36]

    def xlnT_dc(dc):
        return xlnT[:, dc * 1024:(dc + 1) * 1024]

    st_ = dict(
        x_snap=[None, None, None],
        xln_free=[None, None],
        tp_evac=[None, None],
        proj_evac=[None, None],
        kvc_done=None, qcs_done=None, xlnT_done=None,
        wfree=[None, None],
        wtog=0, pjpar=0, stgi=0,
        vfm_pend=[], vfm_free=[None] * 4,
        vt_par=0, vt_evac=[None, None], wscr_off=0, wscr_map={}, wscr_stc={},
        wst_slot=[None, None], ln_s_snap=None, xlnT_half0=0,
        kr_free=[None, None],
        ropi=0, rop_store_sem=[None, None],
    )

    def emit_v_transposes(bt):
        chunks = list(st_["vfm_pend"])
        st_["vfm_pend"].clear()
        fbase = chunks[0][0]
        lastsnap = chunks[-1][1]
        dcol0 = fbase * 128
        half = dcol0 // 1024
        col0 = dcol0 % 1024
        for tc in range(8):
            par = st_["vt_par"] % 2
            st_["vt_par"] += 1
            pb = PS[:, 2048 + par * 512: 2048 + par * 512 + 512].bitcast(BF16)
            tw = [(s_s.h, lastsnap)]
            if st_["vt_evac"][par] is not None:
                tw.append((s_s.h, st_["vt_evac"][par][1]))
            if st_["tp_evac"][par] is not None:
                tw.append((s_s.h, st_["tp_evac"][par]))
            for j, (fcg, _) in enumerate(chunks):
                op("tensor", "transpose", pb[:, j * 128:(j + 1) * 128],
                   vfm[fcg % 4][:, tc * 128:(tc + 1) * 128], sb_id[:, :],
                   waits=(tw if j == 0 else ()))
            tinc("tensor", s_t.bump())
            tsn = s_t.v
            if tc == 7:
                for fcg, _ in chunks:
                    st_["vfm_free"][fcg % 4] = tsn
            ssl = st_["stgi"] % 3
            st_["stgi"] += 1
            sdst = stg[ssl][:, 0:512]
            ew = [(s_t.h, tsn), s_sh[bt].cur(), s_stg.cur()]
            op("scalar", "copy", out=sdst, in_=pb[:, 0:512], waits=ew, inc=s_s.bump(),
               nodrain=True)
            st_["vt_evac"][par] = ("s", s_s.v)
            evs = s_s.v
            dview = shard[bt][3072 + 256 * tc: 3072 + 256 * (tc + 1), :].rearrange(
                "(m two) c -> m two c", two=2)[:, half, col0:col0 + 512]
            op("sync", "dma_start", out=dview, in_=sdst, waits=[(s_s.h, evs)],
               inc=s_sh[bt].bump(16))

    def emit_rope(bt, sname, fcg, evsnap):
        sl = fcg % 2
        rsl = st_["ropi"] % 2
        st_["ropi"] += 1
        shmask = [(i + 16) % 32 for i in range(32)]
        op("vector", "stream_shuffle", krs[:, :], krr[sl][:, :], shmask,
           waits=[(s_s.h, evsnap)])
        op("vector", "tensor_tensor", out=rtmp[0], in0=krr[sl][:, :], in1=sb_cos[:, :], op=ALU.mult,
           nodrain=True)
        op("vector", "tensor_tensor", out=rtmp[1], in0=krs[:, :], in1=sb_sin[:, :], op=ALU.mult)
        rw = []
        prev = st_["rop_store_sem"][rsl]
        if prev is not None:
            rw.append(prev.cur())
        op("vector", "tensor_tensor", out=rop[rsl][:, :], in0=rtmp[0], in1=rtmp[1], op=ALU.add,
           waits=rw, inc=s_v.bump())
        st_["kr_free"][sl] = s_v.v
        rsnap = s_v.v
        if sname in ("krA", "krB"):
            dview = shard[bt][2048 + fcg * 128: 2048 + (fcg + 1) * 128, :]
            sem = s_sh[bt]
        else:
            dview = qscr[2048 + fcg * 128: 2048 + (fcg + 1) * 128, bt * TPB:(bt + 1) * TPB]
            sem = s_stg
        st_["rop_store_sem"][rsl] = sem
        op("sync", "dma_start", out=dview, in_=rop[rsl][:, :], waits=[(s_v.h, rsnap)],
           inc=sem.bump(16))

    SUBS = [
        ("kvc", wkdT, 16, 0, 512, "xlnT", "bkd"),
        ("qcA", wqdT, 16, 0, 768, "xlnT", "bqd"),
        ("qcB", wqdT, 16, 768, 768, "xlnT", "bqd"),
        ("krA", wkrT, 16, 0, 512, "xlnT", "bkr"),
        ("krB", wkrT, 16, 512, 512, "xlnT", "bkr"),
        ("qr", wqrT, 12, 0, 1024, "qcs", "bqr"),
        ("v", wvuT, 4, 0, 2048, "kvc", "bvu"),
        ("kcnt", wkuT, 4, 0, 2048, "kvc", "bku"),
        ("qcntA", wquT, 12, 0, 1024, "qcs", "bqu"),
        ("qcntB", wquT, 12, 1024, 1024, "qcs", "bqu"),
    ]

    def p0_precast():
        off = 0
        for k, (sname, wap, nd, fcol0, FW, _src, _b) in enumerate(SUBS):
            wcols = nd * FW
            st_["wscr_map"][sname] = off
            st_["wscr_stc"][sname] = 16 * (k + 1)
            wsl = k % 2
            ww = []
            if k >= 2:
                ww.append(s_wst.cur())
            ndh = max(1, nd // 2)
            op("gpsimd", "dma_start",
               out=wbuf[wsl][:, 0:ndh * FW].rearrange("p (g f) -> p g f", g=ndh),
               in_=wap.rearrange("(g p) f -> p g f", p=128)[:, 0:ndh, fcol0:fcol0 + FW],
               waits=ww, inc=s_w[wsl].bump(16))
            if nd > ndh:
                op("gpsimd", "dma_start",
                   out=wbuf[wsl][:, ndh * FW:nd * FW].rearrange("p (g f) -> p g f", g=nd - ndh),
                   in_=wap.rearrange("(g p) f -> p g f", p=128)[:, ndh:nd, fcol0:fcol0 + FW],
                   waits=ww, inc=s_w[wsl].bump(16))
            op("gpsimd", "dma_start", out=wscr[:, off:off + wcols],
               in_=wbuf[wsl][:, 0:wcols],
               waits=[s_w[wsl].cur(), s_wst.cur()], inc=s_wst.bump(16))
            off += wcols

    def p1_pass(bt):
        for tc in range(8):
            sl = tc % 3
            sl2 = tc % 2
            w = []
            if st_["x_snap"][sl] is not None:
                w.append((s_s.h, st_["x_snap"][sl][1]))
            op("sync", "dma_start", out=xf[sl],
               in_=x_sl[bt * TPB + tc * 128: bt * TPB + (tc + 1) * 128, :],
               waits=w, inc=s_x[sl].bump(16))
            if LN_CUT == "x":
                st_["x_snap"][sl] = None
                continue
            wt("vector", s_x[sl].cur())
            if st_["ln_s_snap"] is not None:
                wt("vector", (s_s.h, st_["ln_s_snap"]))
            for k in range(4):
                op("vector", "bn_stats", bnst[:, 6 * k:6 * k + 6], xf[sl][:, 512 * k:512 * (k + 1)],
                   nodrain=(k > 0))
            op("vector", "bn_aggr", mv[:, :], bnst[:, :], inc=s_v.bump())
            if LN_CUT == "bn":
                st_["x_snap"][sl] = s_v.v
                continue
            wt("scalar", (s_v.h, s_v.v))
            op("scalar", "activation", out=rstd[:, :], in_=mv[:, 1:2], func=AF.Ln, bias=sb_sm[:, 32:33], scale=1.0)
            op("scalar", "activation", out=rstd[:, :], in_=rstd[:, :], func=AF.Exp, bias=0.0,
               scale=-0.5)
            op("scalar", "activation", out=negr[:, :], in_=rstd[:, :], func=AF.Identity,
               bias=sb_sm[:, 34:35], scale=-1.0)
            op("scalar", "activation", out=nmr[:, :], in_=mv[:, 0:1], func=AF.Identity,
               bias=sb_sm[:, 34:35], scale=negr[:, 0:1])
            xw = []
            if st_["xln_free"][sl2] is not None:
                xw.append((s_t.h, st_["xln_free"][sl2]))
            op("scalar", "activation", out=xln[sl2][:, :], in_=xf[sl], func=AF.Identity,
               bias=nmr[:, 0:1], scale=rstd[:, 0:1], waits=xw, inc=s_s.bump())
            st_["x_snap"][sl] = ("s", s_s.v)
            st_["ln_s_snap"] = s_s.v
            xlnsnap = ("s", s_s.v)
            if LN_CUT == "xln":
                st_["xln_free"][sl2] = None
                continue
            for g4 in range(4):
                par = g4 % 2
                pb = PS[:, 2048 + par * 512: 2048 + par * 512 + 512].bitcast(BF16)
                tw = [(s_s.h, xlnsnap[1])]
                if st_["tp_evac"][par] is not None:
                    tw.append((s_s.h, st_["tp_evac"][par]))
                if st_["vt_evac"][par] is not None:
                    tw.append((s_s.h, st_["vt_evac"][par][1]))
                for j in range(4):
                    dc = g4 * 4 + j
                    op("tensor", "transpose", pb[:, j * 128:(j + 1) * 128],
                       xln[sl2][:, dc * 128:(dc + 1) * 128], sb_id[:, :],
                       waits=(tw if j == 0 else ()))
                tinc("tensor", s_t.bump())
                tsn = s_t.v
                if LN_CUT == "tpv":
                    op("vector", "tensor_copy", out=xlnT[:, g4 * 512:(g4 + 1) * 512],
                       in_=pb[:, 0:512], waits=[(s_t.h, tsn)], inc=s_v.bump())
                    st_["tp_evac"][par] = None
                    stv_snap = s_v.v
                    op("scalar", "_wait_only") if False else None
                    st_["vt_evac"][par] = stv_snap
                    continue
                if LN_CUT == "tps":
                    op("scalar", "copy", out=xlnT[:, g4 * 512:(g4 + 1) * 512],
                       in_=pb[:, 0:512], waits=[(s_t.h, tsn)], inc=s_s.bump())
                    st_["tp_evac"][par] = s_s.v
                    continue
                dview = xlnT.rearrange("p (c t) -> p c t", c=16)[:, g4 * 4:g4 * 4 + 4,
                                                                tc * 128:(tc + 1) * 128]
                op("scalar", "copy", out=dview,
                   in_=pb[:, 0:512].rearrange("p (a b) -> p a b", a=4),
                   waits=[(s_t.h, tsn)], inc=s_s.bump(), nodrain=True)
                st_["tp_evac"][par] = s_s.v
            st_["xln_free"][sl2] = s_t.v
            if tc == 3:
                st_["xlnT_half0"] = s_s.v
        st_["xlnT_done"] = s_s.v

        if p1_subs is not None and len(p1_subs) == 0:
            return
        subs = list(SUBS)
        if p1_subs is not None:
            subs = [s for s in subs if s[0] in p1_subs]
        for sname, wap, nd, fcol0, FW, src, bnm in subs:
            wsl = st_["wtog"] % 2
            st_["wtog"] += 1
            ww = []
            if st_["wfree"][wsl] is not None:
                ww.append((s_t.h, st_["wfree"][wsl]))
            wcols = nd * FW
            ndh = max(1, nd // 2)
            op("gpsimd", "dma_start",
               out=wbuf[wsl][:, 0:ndh * FW].rearrange("p (g f) -> p g f", g=ndh),
               in_=wap.rearrange("(g p) f -> p g f", p=128)[:, 0:ndh, fcol0:fcol0 + FW],
               waits=ww, inc=s_w[wsl].bump(16))
            if nd > ndh:
                op("gpsimd", "dma_start",
                   out=wbuf[wsl][:, ndh * FW:nd * FW].rearrange("p (g f) -> p g f", g=nd - ndh),
                   in_=wap.rearrange("(g p) f -> p g f", p=128)[:, ndh:nd, fcol0:fcol0 + FW],
                   waits=ww, inc=s_w[wsl].bump(16))
            wsem = s_w[wsl].cur()
            nf = FW // 128
            for fc in range(nf):
                fcg = (fcol0 // 128) + fc
                par = st_["pjpar"] % 2
                st_["pjpar"] += 1
                pdst = PS[:, par * 1024:(par + 1) * 1024]
                mw = [wsem]
                if st_["proj_evac"][par] is not None:
                    mw.append((s_s.h, st_["proj_evac"][par]))
                mwh = list(mw)
                if src == "xlnT":
                    mw.append((s_s.h, st_["xlnT_half0"]))
                    mwh.append((s_s.h, st_["xlnT_done"]))
                elif src == "kvc":
                    mw.append((s_s.h, st_["kvc_done"]))
                else:
                    mw.append((s_s.h, st_["qcs_done"]))
                for qh in range(2):
                    for dc in range(nd):
                        if src == "xlnT":
                            rhs_full = xlnT_dc(dc)
                        elif src == "kvc":
                            rhs_full = kvc[:, dc * 1024:(dc + 1) * 1024]
                        else:
                            rhs_full = qcs[:, dc * 1024:(dc + 1) * 1024]
                        lhs = wbuf[wsl][:, dc * FW + fc * 128: dc * FW + fc * 128 + 128]
                        w_ = mw if qh == 0 else mwh
                        op("tensor", "matmul", pdst[:, qh * 512:(qh + 1) * 512], lhs,
                           rhs_full[:, qh * 512:(qh + 1) * 512],
                           start=(dc == 0), stop=(dc == nd - 1),
                           waits=(w_ if dc == 0 else ()))
                tinc("tensor", s_t.bump())
                mmsnap = s_t.v
                bap = sb_bias[:, BOFF[bnm] + fcg: BOFF[bnm] + fcg + 1]
                ew = [(s_t.h, mmsnap)]
                ssl = None
                if sname == "kvc":
                    edst = kvc[:, fcg * 1024:(fcg + 1) * 1024]
                elif sname in ("qcA", "qcB"):
                    edst = qcs[:, fcg * 1024:(fcg + 1) * 1024]
                elif sname in ("krA", "krB", "qr"):
                    edst = krr[fcg % 2]
                    if st_["kr_free"][fcg % 2] is not None:
                        ew.append((s_v.h, st_["kr_free"][fcg % 2]))
                elif sname == "v":
                    edst = vfm[fcg % 4]
                    if st_["vfm_free"][fcg % 4] is not None:
                        ew.append((s_t.h, st_["vfm_free"][fcg % 4]))
                else:
                    ssl = st_["stgi"] % 3
                    st_["stgi"] += 1
                    edst = stg[ssl]
                    ew.append(s_sh[bt].cur())
                    ew.append(s_stg.cur())
                op("scalar", "activation", out=edst, in_=pdst, func=AF.Identity,
                   bias=bap, scale=1.0, waits=ew, inc=s_s.bump(), nodrain=True)
                st_["proj_evac"][par] = s_s.v
                evsnap = s_s.v
                if sname == "kvc" and fcg == 3:
                    st_["kvc_done"] = evsnap
                if sname == "qcB" and fcg == 11:
                    st_["qcs_done"] = evsnap
                if sname == "kcnt":
                    op("sync", "dma_start", out=shard[bt][fcg * 128:(fcg + 1) * 128, :],
                       in_=edst, waits=[(s_s.h, evsnap)], inc=s_sh[bt].bump(16))
                elif sname in ("qcntA", "qcntB"):
                    op("sync", "dma_start",
                       out=qscr[fcg * 128:(fcg + 1) * 128, bt * TPB:(bt + 1) * TPB],
                       in_=edst, waits=[(s_s.h, evsnap)], inc=s_stg.bump(16))
                elif sname == "v":
                    st_["vfm_pend"].append((fcg, evsnap))
                    if len(st_["vfm_pend"]) == 4:
                        emit_v_transposes(bt)
                elif sname in ("krA", "krB", "qr"):
                    emit_rope(bt, sname, fcg, evsnap)
            st_["wfree"][wsl] = s_t.v

    defer_cc1 = do_cc and ("p3" in phases) and p1_passes > 1
    cc1_w = [None]

    def emit_cc(bt, ccw):
        for (r0, r1) in ((0, 1280), (1280, 2560), (2560, 3840), (3840, 5120)):
            op("gpsimd", "collective", kind="AllGather",
               cins=[shard_t[bt].ap()[r0:r1, :]],
               couts=[gath_t[bt].ap()[8 * r0:8 * r1, :]],
               waits=ccw, inc=s_cc[bt].bump(1))

    for bt in range(p1_passes):
        p1_pass(bt)
        if do_cc:
            ccw = [s_sh[bt].cur()]
            if bt == 1 and defer_cc1:
                cc1_w[0] = ccw
                continue
            emit_cc(bt, ccw)
        if dbg:
            op("gpsimd", "dma_start", out=dbgsh_t[bt].ap(), in_=shard[bt],
               waits=[s_sh[bt].cur()], inc=s_dbg.bump(16))
    QS_TOT = s_stg.cur()
    if dbg:
        wt("gpsimd", s_dbg.cur())
    if "p3" not in phases:
        wt("sync", QS_TOT, s_sh[0].cur(), s_sh[1].cur())
        wt("vector", (s_v.h, s_v.v))
        wt("scalar", (s_s.h, s_s.v))
        wt("tensor", (s_t.h, s_t.v))
        if do_cc:
            wt("gpsimd", (s_cc[0].h, 1) if p1_passes > 0 else (s_cc[0].h, 0))
            if p1_passes > 1:
                wt("gpsimd", (s_cc[1].h, 1))

    if "p3" not in phases:
        return _finish(nc, es, ops, ENGS, NCORE, drains)
    AB.reset(); AFA.reset()
    AB.mark(); AFA.mark()
    qbuf = [AB.alloc(6 * 1024) for _ in range(2)]
    kvb = [AB.alloc(10240) for _ in range(4)]
    ptil = [AB.alloc(1024) for _ in range(6)]
    onstg = [AB.alloc(1024) for _ in range(4)]
    lsb16s = [AB.alloc(1024, parts=1) for _ in range(4)]
    osbb = AFA.alloc(4096)
    lsbb = AFA.alloc(4096, parts=1)

    OO, LL = 2048, 3072
    P1T, P1V, P1S = s_t.cur(), s_v.cur(), s_s.cur()
    wt("sync", P1T, P1V, P1S)
    wt("gpsimd", P1T, P1V, P1S)

    a_ = dict(scpar=0, ptc=0, pt_cons=[None] * 6, sc_evac=[None] * 4,
              ol_free=None, bc_evac=None, kv_free=[None] * 4, q_free=[None, None],
              kv_snap=[None] * 4, q_snap=[None, None],
              oni=0, sweep=0, kvctr=0, pend_onst=[], lp_snap=[None, None], lp_free=None, ptil_guard=None)
    SWEEPS = [(bt, g) for bt in range(2) for g in range(4)]
    KVPOS = [(bt, g, bi) for (bt, g) in SWEEPS for bi in range(8)]

    def kv_load(bt, g, bi, slot):
        h0 = g * 4
        w = [(s_cc[bt].h, 4)] if do_cc else []
        if a_["kv_free"][slot] is not None:
            w.append((s_t.h, a_["kv_free"][slot]))
        gbc = [gath[bt][8 * 1280 * c: 8 * 1280 * (c + 1), :].rearrange(
            "(r f) c -> r f c", f=1280) for c in range(4)]

        def grow(row, nrow):
            c = row // 1280
            return gbc[c][:, row - 1280 * c: row - 1280 * c + nrow, :]

        for hh in range(4):
            h = h0 + hh
            src = grow(h * 128, 128)[:, :, bi * 128:(bi + 1) * 128]
            dst = kvb[slot][:, hh * 1024:(hh + 1) * 1024].rearrange("p (r t) -> p r t", r=8)
            if hh < 2:
                op("sync", "dma_start", out=dst, in_=src.rearrange("r f c -> f r c"),
                   waits=(w if hh == 0 else ()), inc=s_kv[slot].bump(16))
            else:
                op("sync", "dma_start", out=dst, in_=src.rearrange("r f c -> f r c"),
                   waits=(w if hh == 2 else ()), inc=s_kvs[slot].bump(16))
        for pp in range(2):
            pj = g * 2 + pp
            src = grow(2048 + pj * 128, 128)[:, :, bi * 128:(bi + 1) * 128]
            dst = kvb[slot][:, 4096 + pp * 1024: 4096 + (pp + 1) * 1024].rearrange(
                "p (r t) -> p r t", r=8)
            op("gpsimd", "dma_start", out=dst, in_=src.rearrange("r f c -> f r c"),
               waits=(w if pp == 0 else ()), inc=s_kvg[slot].bump(16))
        dcol0 = h0 * 128
        half = dcol0 // 1024
        col0 = dcol0 % 1024
        src = grow(3072 + 256 * bi, 256).rearrange(
            "r (m two) c -> r m two c", two=2)[:, :, half, col0:col0 + 512]
        dst = kvb[slot][:, 6144:6144 + 4096].rearrange("p (r d) -> p r d", r=8)
        op("gpsimd", "dma_start", out=dst, in_=src.rearrange("r m d -> m r d"),
           inc=s_kvg[slot].bump(16))
        a_["kv_snap"][slot] = (s_kv[slot].cur(), s_kvg[slot].cur(), s_kvs[slot].cur())

    def q_load(bt, g, slot):
        w = [QS_TOT]
        if a_["q_free"][slot] is not None:
            w.append((s_t.h, a_["q_free"][slot]))
        for hh in range(4):
            h = g * 4 + hh
            op("sync", "dma_start", out=qbuf[slot][:, hh * 1024:(hh + 1) * 1024],
               in_=qscr[h * 128:(h + 1) * 128, bt * TPB:(bt + 1) * TPB],
               waits=(w if hh == 0 else ()), inc=s_q[slot].bump(16))
        for pp in range(2):
            pj = g * 2 + pp
            op("sync", "dma_start", out=qbuf[slot][:, 4096 + pp * 1024: 4096 + (pp + 1) * 1024],
               in_=qscr[2048 + pj * 128: 2048 + (pj + 1) * 128, bt * TPB:(bt + 1) * TPB],
               inc=s_q[slot].bump(16))
        a_["q_snap"][slot] = s_q[slot].cur()

    def flush_onst():
        for (h, bt2, osl, onsnap) in a_["pend_onst"]:
            op("sync", "dma_start",
               out=onorm[h * 128:(h + 1) * 128, bt2 * TPB:(bt2 + 1) * TPB],
               in_=onstg[osl][:, :], waits=[(s_v.h, onsnap)], inc=s_onst.bump(16))
            if bt2 == 0:
                a_["onst_b0"] = s_onst.v
        a_["pend_onst"] = []

    def attention_sweep(bt, g, nxt=None):
        si = a_["sweep"]
        a_["sweep"] += 1
        qsl = si % 2
        if si == 0:
            q_load(bt, g, qsl)
            kv_load(*KVPOS[0], 0)
            kv_load(*KVPOS[1], 1)
            kv_load(*KVPOS[2], 2)
        qsem = a_["q_snap"][qsl]
        for bi in range(8):
            if si == 1 and bi == 0 and cc1_w[0] is not None:
                emit_cc(1, cc1_w[0])
                cc1_w[0] = None
            pf = si * 8 + bi + 3
            if pf < len(KVPOS):
                kv_load(*KVPOS[pf], pf % 4)
            if bi == 5 and nxt is not None:
                q_load(nxt[0], nxt[1], (si + 1) % 2)
            ksl = (si * 8 + bi) % 4
            if bi == 1:
                flush_onst()
            kvsem, kvgsem, kvssem = a_["kv_snap"][ksl]
            Nq = 1024 - 128 * bi
            nqt = (Nq + 511) // 512
            qoff = 128 * bi
            for pair in range(2):
                for qh in range(nqt):
                    wq = min(512, Nq - qh * 512)
                    q0 = qoff + qh * 512
                    pend = []

                    def emit_attnv(j, psl, msnap, wq=wq, q0=q0, sem=None):
                        aw = [((sem or s_v).h, msnap)]
                        if j == 0 and a_["ol_free"] is not None:
                            aw.append((s_v.h, a_["ol_free"]))
                        for lh in range(2):
                            hh = pair * 2 + lh
                            vlhs = kvb[ksl][:, 6144 + j * 512 + hh * 128:
                                            6144 + j * 512 + hh * 128 + 128]
                            rhs = ptil[psl][:, lh * 512: lh * 512 + wq]
                            op("tensor", "matmul",
                               PS[0:1, 3072 + lh * 512: 3072 + lh * 512 + wq],
                               sb_ones[:, 0:1], rhs,
                               start=(j == 0), stop=(j == 7),
                               waits=(aw if lh == 0 else ()))
                            op("tensor", "matmul",
                               PS[:, 2048 + lh * 512: 2048 + lh * 512 + wq],
                               vlhs, rhs,
                               start=(j == 0), stop=(j == 7))
                        tinc("tensor", s_t.bump())
                        a_["pt_cons"][psl] = s_t.v

                    for j in range(8):
                        par = j % 2
                        w = [kvsem, kvgsem, kvssem, qsem]
                        if a_["sc_evac"][par] is not None:
                            w.append((s_s.h, a_["sc_evac"][par]))
                        if a_["bc_evac"] is not None:
                            w.append((s_v.h, a_["bc_evac"]))
                        for lh in range(2):
                            hh = pair * 2 + lh
                            lhs_n = kvb[ksl][:, hh * 1024 + j * 128: hh * 1024 + j * 128 + 128]
                            rq_n = qbuf[qsl][:, hh * 1024 + q0: hh * 1024 + q0 + wq]
                            op("tensor", "matmul",
                               PS[:, par * 1024 + lh * 512: par * 1024 + lh * 512 + wq],
                               lhs_n, rq_n, start=True, stop=False,
                               waits=(w if lh == 0 else ()))
                        for lh in range(2):
                            hh = pair * 2 + lh
                            half = hh % 2
                            lhs_r = kvb[ksl][64 * half:64 * half + 64,
                                             4096 + pair * 1024 + j * 128:
                                             4096 + pair * 1024 + j * 128 + 128]
                            rq_r = qbuf[qsl][64 * half:64 * half + 64,
                                             4096 + pair * 1024 + q0:
                                             4096 + pair * 1024 + q0 + wq]
                            op("tensor", "matmul",
                               PS[:, par * 1024 + lh * 512: par * 1024 + lh * 512 + wq],
                               lhs_r, rq_r, start=False, stop=True)
                        tinc("tensor", s_t.bump())
                        scsnap = s_t.v
                        psl = a_["ptc"] % 6
                        a_["ptc"] += 1
                        ew = [(s_t.h, scsnap)]
                        if a_["pt_cons"][psl] is not None:
                            ew.append((s_t.h, a_["pt_cons"][psl]))
                        op("scalar", "activation", out=ptil[psl][:, 0:512 + wq],
                           in_=PS[:, par * 1024: par * 1024 + 512 + wq],
                           func=AF.Exp, bias=sb_sm[:, 33:34], scale=SCALE,
                           waits=ew, inc=s_s.bump(), nodrain=True)
                        a_["sc_evac"][par] = s_s.v
                        expsnap = s_s.v
                        if qh == 0:
                            op("vector", "tensor_tensor", out=ptil[psl][:, 0:128],
                               in0=ptil[psl][:, 0:128],
                               in1=sb_mask[:, j * 128:(j + 1) * 128],
                               op=ALU.mult, waits=[(s_s.h, expsnap)], nodrain=True)
                            op("vector", "tensor_tensor", out=ptil[psl][:, 512:640],
                               in0=ptil[psl][:, 512:640],
                               in1=sb_mask[:, j * 128:(j + 1) * 128],
                               op=ALU.mult, inc=s_v.bump(), nodrain=True)
                            msnap = s_v.v
                        else:
                            msnap = None
                        pend.append((j, psl, msnap if msnap is not None else expsnap))
                        while len(pend) > 2:
                            jj, psl2, sn2 = pend.pop(0)
                            emit_attnv(jj, psl2, sn2,
                                       sem=(s_v if qh == 0 else s_s))
                    while pend:
                        jj, psl2, sn2 = pend.pop(0)
                        emit_attnv(jj, psl2, sn2,
                                   sem=(s_v if qh == 0 else s_s))
                    avsnap = s_t.v
                    o_out = osbb[:, pair * 2048: pair * 2048 + 2048].rearrange(
                        "p (g c) -> p g c", g=2)[:, :, q0:q0 + wq]
                    o_in = PS[:, 2048:3072].rearrange("p (g c) -> p g c", g=2)[:, :, 0:wq]
                    l_out = lsbb[0:1, pair * 2048: pair * 2048 + 2048].rearrange(
                        "p (g c) -> p g c", g=2)[:, :, q0:q0 + wq]
                    l_in = PS[0:1, 3072:4096].rearrange("p (g c) -> p g c", g=2)[:, :, 0:wq]
                    if bi == 0:
                        op("vector", "tensor_copy", out=o_out, in_=o_in,
                           waits=[(s_t.h, avsnap)], nodrain=True)
                        op("vector", "tensor_copy", out=l_out, in_=l_in,
                           inc=s_v.bump(), nodrain=True)
                    else:
                        op("vector", "tensor_tensor", out=o_out, in0=o_out, in1=o_in,
                           op=ALU.add, waits=[(s_t.h, avsnap)], nodrain=True)
                        op("vector", "tensor_tensor", out=l_out, in0=l_out, in1=l_in,
                           op=ALU.add, inc=s_v.bump(), nodrain=True)
                    a_["ol_free"] = s_v.v
            a_["kv_free"][ksl] = s_t.v
        a_["q_free"][qsl] = s_t.v
        rsnaps = []
        for hh in range(4):
            lview = lsbb[0:1, hh * 1024:(hh + 1) * 1024]
            op("vector", "reciprocal", lview, lview, nodrain=(hh > 0))
            op("vector", "tensor_copy", out=lsb16s[hh][0:1, :], in_=lview,
               inc=s_v.bump())
            rsnaps.append(s_v.v)
        mult_snap = [None, None]
        for hh in range(4):
            h = g * 4 + hh
            pb_off = (hh % 2) * 1024
            bw = [(s_v.h, rsnaps[hh])]
            if mult_snap[hh % 2] is not None:
                bw.append((s_v.h, mult_snap[hh % 2]))
            elif a_["bc_evac"] is not None:
                bw.append((s_v.h, a_["bc_evac"]))
            for k4 in range(4):
                if a_["sc_evac"][k4] is not None:
                    bw.append((s_s.h, a_["sc_evac"][k4]))
            for qt in range(2):
                op("tensor", "matmul", PS[:, pb_off + qt * 512: pb_off + qt * 512 + 512],
                   sb_ones[0:1, :], lsb16s[hh][0:1, qt * 512: qt * 512 + 512],
                   start=True, stop=True, waits=(bw if qt == 0 else ()))
            tinc("tensor", s_t.bump())
            bsnap = s_t.v
            osl = a_["oni"] % 4
            a_["oni"] += 1
            ow = [(s_t.h, bsnap), s_onst.cur()]
            op("vector", "tensor_tensor", out=onstg[osl][:, :],
               in0=osbb[:, hh * 1024:(hh + 1) * 1024],
               in1=PS[:, pb_off: pb_off + 1024], op=ALU.mult, waits=ow, inc=s_v.bump())
            mult_snap[hh % 2] = s_v.v
            a_["bc_evac"] = s_v.v
            onsnap = s_v.v
            a_["pend_onst"].append((h, bt, osl, onsnap))

    for sidx, (bt, g) in enumerate(SWEEPS):
        nxt = SWEEPS[sidx + 1] if sidx + 1 < len(SWEEPS) else None
        attention_sweep(bt, g, nxt=nxt)
    flush_onst()
    ON_TOT = s_onst.cur()
    ON_B0 = (s_onst.h, a_.get("onst_b0", s_onst.v))
    if "p4" not in phases:
        wt("sync", ON_TOT)

    if "p4" not in phases:
        return _finish(nc, es, ops, ENGS, NCORE, drains)
    AB.reset(); AFA.reset()
    AB.mark(); AFA.mark()
    woT = AB.alloc(16 * 2048)
    onb = [AB.alloc(2048) for _ in range(2)]
    x2l = [AB.alloc(2048) for _ in range(2)]
    x2T = AB.alloc(16 * 256)
    aTb = AB.alloc(16 * E * R)
    bwTb = AB.alloc(E * D, parts=64)
    wgTb = AB.alloc(16 * E)
    ghb = AB.alloc(E * 256, parts=64)
    gbc = AB.alloc(256, parts=64)
    gtb = AB.alloc(256, parts=8)
    scr8 = AB.alloc(8)
    xf4 = [AFA.alloc(2048) for _ in range(2)]
    xnw = [AFA.alloc(2048) for _ in range(2)]

    P3T, P3V, P3S = s_t.cur(), s_v.cur(), s_s.cur()
    wt("sync", P3T, P3V, P3S)
    wt("gpsimd", P3T, P3V, P3S)

    op("gpsimd", "dma_start", out=woT[:, 0:8 * 2048].rearrange("p (c d) -> p c d", c=8),
       in_=woTr.rearrange("(c p) d -> p c d", p=128)[:, 0:8, :], inc=s_w[0].bump(16))
    P4WA = s_w[0].cur()
    op("gpsimd", "dma_start", out=woT[:, 8 * 2048:].rearrange("p (c d) -> p c d", c=8),
       in_=woTr.rearrange("(c p) d -> p c d", p=128)[:, 8:16, :], inc=s_w[0].bump(16))
    P4WB = s_w[0].cur()
    op("gpsimd", "dma_start", out=aTb, in_=aTh, inc=s_w[0].bump(16))
    op("gpsimd", "dma_start", out=bwTb, in_=bwTh, inc=s_w[0].bump(16))
    op("gpsimd", "dma_start", out=wgTb, in_=wgTh, inc=s_w[0].bump(16))
    P4W = s_w[0].cur()

    f_ = dict(on_free=[None, None], x_free=[None, None], tp_evac=[None, None],
              x2l_free=[None, None], x2T_cons=None, logit_free=None,
              gt_free=None, gbc_free=None, h_free=[None, None],
              py_free=[None, None], xn_store=[None, None], op_free=None,
              y67_free=None, xnw_snap=[None, None])

    def p4_loads(bt, gg):
        for ii, tt in enumerate((gg * 2, gg * 2 + 1)):
            r0 = bt * TPB + tt * 128
            osl = tt % 2
            xsl = tt % 2
            w = [ON_B0 if bt == 0 else ON_TOT]
            if f_["on_free"][osl] is not None:
                w.append((s_t.h, f_["on_free"][osl]))
            src2 = onorm.rearrange("(c p) t -> p c t", p=128)[:, :, r0: r0 + 128]
            onv = onb[osl].rearrange("p (c t) -> p c t", c=16)
            op("sync", "dma_start", out=onv[:, 0:8, :], in_=src2[:, 0:8, :],
               waits=w, inc=s_on[osl].bump(16))
            op("sync", "dma_start", out=onv[:, 8:16, :], in_=src2[:, 8:16, :],
               inc=s_on[osl].bump(16))
            xw = []
            if f_["x_free"][xsl] is not None:
                xw.append((s_v.h, f_["x_free"][xsl]))
            op("sync", "dma_start", out=xf4[xsl], in_=x_sl[r0: r0 + 128, :], waits=xw,
               inc=s_x[xsl].bump(16))

    def p4_opmm(bt, gg, ii):
        tt = gg * 2 + ii
        osl = tt % 2
        mw = [s_on[osl].cur(), P4WA]
        if f_["op_free"] is not None:
            mw.append((s_v.h, f_["op_free"]))
        for hc in range(16):
            for dt in range(4):
                w_ = ()
                if hc == 0 and dt == 0:
                    w_ = mw
                elif hc == 8 and dt == 0:
                    w_ = [P4WB]
                op("tensor", "matmul", PS[:, dt * 512: dt * 512 + 512],
                   onb[osl][:, hc * 128:(hc + 1) * 128],
                   woT[:, hc * 2048 + dt * 512: hc * 2048 + dt * 512 + 512],
                   start=(hc == 0), stop=False,
                   waits=w_)
        for dt in range(4):
            op("tensor", "matmul", PS[:, dt * 512: dt * 512 + 512],
               sb_ones[0:1, 0:128], sb_bo16[0:1, dt * 512:(dt + 1) * 512],
               start=False, stop=True)
        tinc("tensor", s_t.bump())
        f_["on_free"][osl] = s_t.v
        f_["opmm_snap_%d" % ii] = s_t.v

    def p4_post_add(bt, gg, ii):
        tt = gg * 2 + ii
        xsl = tt % 2
        mmsnap = f_["opmm_snap_%d" % ii]
        if f_["xn_store"][ii] is not None:
            wt("vector", (s_out[ii].h, f_["xn_store"][ii]))
        vw = [(s_t.h, mmsnap), s_x[xsl].cur()]
        for dt in range(4):
            last = dt == 3
            op("vector", "tensor_tensor", out=xnw[ii][:, dt * 512:(dt + 1) * 512],
               in0=xf4[xsl][:, dt * 512:(dt + 1) * 512], in1=PS[:, dt * 512: dt * 512 + 512],
               op=ALU.add, waits=(vw if dt == 0 else ()),
               inc=(s_v.bump() if last else None), nodrain=(dt > 0))
        f_["x_free"][xsl] = s_v.v
        f_["xnw_snap"][ii] = s_v.v
        f_["op_free"] = s_v.v

    def p4_post_ln(bt, gg, ii):
        if f_.get("p4_ln_s") is not None:
            wt("vector", (s_s.h, f_["p4_ln_s"]))
        for k in range(4):
            op("vector", "bn_stats", bnst[:, 6 * k:6 * k + 6],
               xnw[ii][:, 512 * k:512 * (k + 1)], nodrain=(k > 0))
        op("vector", "bn_aggr", mv[:, :], bnst[:, :], inc=s_v.bump())
        wt("scalar", (s_v.h, s_v.v))
        op("scalar", "activation", out=rstd[:, :], in_=mv[:, 1:2], func=AF.Ln,
           bias=sb_sm[:, 32:33], scale=1.0)
        op("scalar", "activation", out=rstd[:, :], in_=rstd[:, :], func=AF.Exp, bias=0.0,
           scale=-0.5)
        op("scalar", "activation", out=negr[:, :], in_=rstd[:, :], func=AF.Identity,
           bias=sb_sm[:, 34:35], scale=-1.0)
        op("scalar", "activation", out=nmr[:, :], in_=mv[:, 0:1], func=AF.Identity,
           bias=sb_sm[:, 34:35], scale=negr[:, 0:1])
        x2w = []
        if f_["x2l_free"][ii] is not None:
            x2w.append((s_t.h, f_["x2l_free"][ii]))
        op("scalar", "activation", out=x2l[ii][:, :], in_=xnw[ii], func=AF.Identity,
           bias=nmr[:, 0:1], scale=rstd[:, 0:1], waits=x2w, inc=s_s.bump())
        x2snap = s_s.v
        f_["p4_ln_s"] = s_s.v
        for g4 in range(4):
            par = g4 % 2
            pb = PS[:, 2048 + par * 512: 2048 + par * 512 + 512].bitcast(BF16)
            tw = [(s_s.h, x2snap)]
            if f_["tp_evac"][par] is not None:
                tw.append((s_s.h, f_["tp_evac"][par]))
            if f_["h_free"][par] is not None:
                tw.append((s_v.h, f_["h_free"][par]))
            for j in range(4):
                dc = g4 * 4 + j
                op("tensor", "transpose", pb[:, j * 128:(j + 1) * 128],
                   x2l[ii][:, dc * 128:(dc + 1) * 128], sb_id[:, :],
                   waits=(tw if j == 0 else ()))
            tinc("tensor", s_t.bump())
            tsn = s_t.v
            dview = x2T.rearrange("p (c t) -> p c t", c=16)[:, g4 * 4: g4 * 4 + 4,
                                                            ii * 128:(ii + 1) * 128]
            ew = [(s_t.h, tsn)]
            if ii == 0 and g4 == 0 and f_["x2T_cons"] is not None:
                ew.append((s_t.h, f_["x2T_cons"]))
            op("scalar", "copy", out=dview,
               in_=pb[:, 0:512].rearrange("p (a b) -> p a b", a=4),
               waits=ew, inc=s_s.bump(), nodrain=True)
            f_["tp_evac"][par] = s_s.v
        f_["x2l_free"][ii] = s_t.v

    def p4_ffn(bt, gg, next_cb=None):
        tts = (gg * 2, gg * 2 + 1)
        x2T_snap = s_s.cur()
        for ii in range(2):
            lw = [x2T_snap, P4W]
            if f_["logit_free"] is not None:
                lw.append((s_v.h, f_["logit_free"]))
            if f_["y67_free"] is not None:
                lw.append((s_v.h, f_["y67_free"]))
            for dc in range(16):
                op("tensor", "matmul", PS[:, 3072:3080],
                   x2T.rearrange("p (c t) -> p c t", c=16)[:, dc, ii * 128:(ii + 1) * 128],
                   wgTb.rearrange("p (c e) -> p c e", c=16)[:, dc, :],
                   start=(dc == 0), stop=(dc == 15), waits=(lw if dc == 0 else ()))
            tinc("tensor", s_t.bump())
            lsnap = s_t.v
            op("vector", "tensor_reduce", sb_sm[:, 28:29], PS[:, 3072:3080],
               mybir.AxisListType.X, ALU.max, waits=[(s_t.h, lsnap)])
            op("vector", "tensor_scalar", out=sb_sm[:, 29:30], in0=sb_sm[:, 28:29],
               scalar1=-1.0, scalar2=None, op0=ALU.mult, inc=s_v.bump())
            wt("scalar", (s_v.h, s_v.v))
            op("scalar", "activation", out=scr8[:, :], in_=PS[:, 3072:3080], func=AF.Exp,
               bias=sb_sm[:, 29:30], scale=1.0, accum_out=sb_sm[:, 30:31], inc=s_s.bump())
            wt("vector", (s_s.h, s_s.v))
            op("vector", "reciprocal", sb_sm[:, 31:32], sb_sm[:, 30:31])
            op("vector", "tensor_scalar", out=scr8[:, :], in0=PS[:, 3072:3080],
               scalar1=sb_sm[:, 29:30], scalar2=0.0, op0=ALU.add, op1=ALU.is_equal)
            op("vector", "tensor_scalar", out=scr8[:, :], in0=scr8[:, :],
               scalar1=sb_sm[:, 31:32], scalar2=MOE_SCALING, op0=ALU.mult, op1=ALU.mult,
               inc=s_v.bump())
            f_["logit_free"] = s_v.v
            gsnap = s_v.v
            pgt = PS[:, 3584:3648].bitcast(BF16)
            gw = [(s_v.h, gsnap)]
            if f_["gt_free"] is not None:
                gw.append((s_v.h, f_["gt_free"]))
            if f_["y67_free"] is not None:
                gw.append((s_v.h, f_["y67_free"]))
            op("tensor", "transpose", pgt[0:8, 0:128], scr8[:, :], sb_id[:, :], waits=gw)
            tinc("tensor", s_t.bump())
            gtsn = s_t.v
            op("vector", "tensor_copy", out=gtb[0:8, ii * 128:(ii + 1) * 128],
               in_=pgt[0:8, 0:128], waits=[(s_t.h, gtsn)], inc=s_v.bump())
            f_["gt_free"] = s_v.v
        gT_snap = s_v.cur()

        if next_cb is not None:
            next_cb()

        for e in range(E):
            par = e % 2
            bcw = [gT_snap]
            if f_["gbc_free"] is not None:
                bcw.append((s_v.h, f_["gbc_free"]))
            if f_["y67_free"] is not None:
                bcw.append((s_v.h, f_["y67_free"]))
            op("tensor", "matmul", PS[0:64, 3584:3840], sb_oh[:, e * 64:(e + 1) * 64],
               gtb[0:8, 0:256], start=True, stop=True, waits=bcw)
            tinc("tensor", s_t.bump())
            bcsnap = s_t.v
            op("vector", "tensor_copy", out=gbc[0:64, :], in_=PS[0:64, 3584:3840],
               waits=[(s_t.h, bcsnap)], inc=s_v.bump())
            gbcsnap = s_v.v
            f_["gbc_free"] = gbcsnap
            hw = [(s_v.h, gbcsnap), P4W]
            if f_["h_free"][par] is not None:
                hw.append((s_v.h, f_["h_free"][par]))
            for dc in range(16):
                op("tensor", "matmul", PS[0:64, 2048 + par * 512: 2048 + par * 512 + 256],
                   aTb.rearrange("p (c e r) -> p c e r", c=16, e=E)[:, dc, e, :],
                   x2T.rearrange("p (c t) -> p c t", c=16)[:, dc, :],
                   start=(dc == 0), stop=(dc == 15), waits=(hw if dc == 0 else ()))
            tinc("tensor", s_t.bump())
            hsnap = s_t.v
            op("vector", "tensor_tensor", out=ghb[0:64, e * 256:(e + 1) * 256],
               in0=PS[0:64, 2048 + par * 512: 2048 + par * 512 + 256], in1=gbc[0:64, :],
               op=ALU.mult, waits=[(s_t.h, hsnap)], inc=s_v.bump())
            f_["h_free"][par] = s_v.v
        gh_snap = s_v.cur()
        f_["x2T_cons"] = s_t.v

        ycnt = 0
        for ii in range(2):
            tt = tts[ii]
            r0 = bt * TPB + tt * 128
            for dt in range(4):
                yb = 3072 + (ycnt % 2) * 512
                ycnt += 1
                yw = [gh_snap, P4W]
                if f_["py_free"][(ycnt - 1) % 2] is not None:
                    yw.append((s_v.h, f_["py_free"][(ycnt - 1) % 2]))
                for e in range(E):
                    op("tensor", "matmul", PS[:, yb: yb + 512],
                       ghb[0:64, e * 256 + ii * 128: e * 256 + ii * 128 + 128],
                       bwTb[0:64, e * 2048 + dt * 512: e * 2048 + dt * 512 + 512],
                       start=(e == 0), stop=(e == E - 1), waits=(yw if e == 0 else ()))
                tinc("tensor", s_t.bump())
                ysnap = s_t.v
                op("vector", "tensor_tensor", out=xnw[ii][:, dt * 512:(dt + 1) * 512],
                   in0=xnw[ii][:, dt * 512:(dt + 1) * 512],
                   in1=PS[:, yb: yb + 512],
                   op=ALU.add, waits=[(s_t.h, ysnap)], inc=s_v.bump(), nodrain=(dt > 0))
                f_["py_free"][(ycnt - 1) % 2] = s_v.v
                f_["y67_free"] = s_v.v
            donesnap = s_v.v
            op("sync", "dma_start", out=out_sl[r0: r0 + 128, :], in_=xnw[ii][:, :],
               waits=[(s_v.h, donesnap)], inc=s_out[ii].bump(16))
            f_["xn_store"][ii] = s_out[ii].v

    groups = [(bt, gg) for bt in range(2) for gg in range(4)]
    p4_loads(*groups[0])
    p4_opmm(*groups[0], 0)
    p4_post_add(*groups[0], 0)
    p4_opmm(*groups[0], 1)
    p4_post_ln(*groups[0], 0)
    p4_post_add(*groups[0], 1)
    p4_post_ln(*groups[0], 1)
    for gi, (bt, gg) in enumerate(groups):
        nxt = groups[gi + 1] if gi + 1 < len(groups) else None
        if nxt is not None:
            p4_loads(*nxt)

            def cb(nxt=nxt):
                p4_opmm(*nxt, 0)
            p4_ffn(bt, gg, next_cb=cb)
            p4_post_add(*nxt, 0)
            p4_opmm(*nxt, 1)
            p4_post_ln(*nxt, 0)
            p4_post_add(*nxt, 1)
            p4_post_ln(*nxt, 1)
        else:
            p4_ffn(bt, gg)

    wt("sync", s_out[0].cur(), s_out[1].cur())
    return _finish(nc, es, ops, ENGS, NCORE, drains)


def _finish(nc, es, ops, engs, ncore, drains=False):
    block = es.enter_context(nc.Block())

    def runner(lst, auto_drain=False):
        def go(e):
            waited = {}
            for meth, args, kw, waits, inc in lst:
                for (h, v) in waits:
                    k = id(h)
                    if waited.get(k, -1) >= v:
                        continue
                    e.wait_ge(h, v)
                    waited[k] = v
                if meth == "_wait":
                    continue
                if meth == "_drain":
                    e.drain()
                    continue
                if meth.startswith("!"):
                    meth = meth[1:]
                elif auto_drain and meth not in ("_nop", "dma_start"):
                    e.drain()
                if meth == "_nop":
                    ins = e.nop()
                elif meth == "collective":
                    ins = e.collective_compute(
                        kw["kind"], ALU.bypass,
                        replica_groups=[list(range(ncore))],
                        ins=[a.opt() for a in kw["cins"]],
                        outs=[a.opt() for a in kw["couts"]],
                    )
                    if inc is not None:
                        ins.then_inc(inc[0])
                    continue
                else:
                    ins = getattr(e, meth)(*args, **kw)
                if inc is not None:
                    ins.then_inc(inc[0], inc[2])
        return go

    @block.tensor
    def _(e):
        runner(ops["tensor"])(e)

    @block.vector
    def _(e):
        runner(ops["vector"], auto_drain=drains)(e)

    @block.scalar
    def _(e):
        runner(ops["scalar"], auto_drain=True)(e)

    @block.sync
    def _(e):
        runner(ops["sync"])(e)

    @block.gpsimd
    def _(e):
        runner(ops["gpsimd"])(e)

    es.close()
    return nc



def _prep_inputs(kw):
    f32 = np.float32
    bf = ml_dtypes.bfloat16
    x = np.asarray(kw["L0_attn_x"], f32)

    def T(a):
        return np.ascontiguousarray(np.asarray(a, f32).T)

    wkdT = T(kw["L0_attn_Wkd"])
    wqdT = T(kw["L0_attn_Wqd"])
    wkuT = T(kw["L0_attn_Wku"])
    wvuT = T(kw["L0_attn_Wvu"])
    wquT = T(kw["L0_attn_Wqu"])
    wkrT = np.ascontiguousarray(np.asarray(kw["L0_attn_Wkr"], f32)[ROPE_ORDER, :].T)
    wqrT = np.ascontiguousarray(np.asarray(kw["L0_attn_Wqr"], f32)[ROPE_ORDER, :].T)
    woTr = T(kw["L0_attn_Wo"])
    A = np.asarray(kw["L0_ffn_A"], f32)
    Bw = np.asarray(kw["L0_ffn_Bw"], f32)
    Wg = np.asarray(kw["L0_ffn_Wg"], f32)

    aTh = np.ascontiguousarray(
        A.transpose(2, 0, 1).reshape(16, 128, E, R).transpose(1, 0, 2, 3).reshape(128, 16 * E * R))
    bwTh = np.ascontiguousarray(Bw.transpose(2, 0, 1).reshape(64, E * D))
    wgTh = np.ascontiguousarray(
        Wg.T.reshape(16, 128, E).transpose(1, 0, 2).reshape(128, 16 * E))

    def b2(v, nf):
        return np.ascontiguousarray(np.asarray(v, f32).reshape(nf, 128).T)

    bkd2 = b2(kw["L0_attn_bkd"], 4)
    bqd2 = b2(kw["L0_attn_bqd"], 12)
    bku2 = b2(kw["L0_attn_bku"], 16)
    bvu2 = b2(kw["L0_attn_bvu"], 16)
    bqu2 = b2(kw["L0_attn_bqu"], 16)
    bkr2 = b2(np.asarray(kw["L0_attn_bkr"], f32)[ROPE_ORDER], 8)
    bqr2 = b2(np.asarray(kw["L0_attn_bqr"], f32)[ROPE_ORDER], 8)
    bo2 = np.ascontiguousarray(np.asarray(kw["L0_attn_bo"], f32).reshape(1, D))

    ident = np.eye(128, dtype=bf)
    ones = np.ones((128, 128), dtype=bf)
    onehot = np.zeros((8, E * 64), dtype=bf)
    for e in range(E):
        onehot[e, e * 64:(e + 1) * 64] = 1.0

    inv_freq = (1.0 / (10000.0 ** (np.arange(0, DR, 2, dtype=np.float64) / DR))).astype(f32)

    shared = dict(wkdT=wkdT, wqdT=wqdT, wkuT=wkuT, wvuT=wvuT, wquT=wquT,
                  wkrT=wkrT, wqrT=wqrT, woTr=woTr, aTh=aTh, bwTh=bwTh, wgTh=wgTh,
                  bkd2=bkd2, bqd2=bqd2, bku2=bku2, bvu2=bvu2, bqu2=bqu2,
                  bkr2=bkr2, bqr2=bqr2, bo2=bo2, identt=ident, onest=ones,
                  onehot=onehot)
    in_maps = []
    for c in range(NCORE):
        pos = (8 * np.arange(TPB, dtype=f32) + c).astype(f32)
        ang = pos[None, :] * inv_freq[:, None]
        cos64 = np.cos(ang).astype(f32)
        sin64 = np.sin(ang).astype(f32)
        cosp = np.empty((64, TPB), f32)
        sinp = np.empty((64, TPB), f32)
        for j, d in enumerate(PERM64):
            cosp[j] = cos64[d % 32]
            sinp[j] = sin64[d % 32] * (-1.0 if d < 32 else 1.0)
        costab = np.ascontiguousarray(np.concatenate([cosp, cosp], 0))
        sintb = np.ascontiguousarray(np.concatenate([sinp, sinp], 0))
        m = np.arange(128)
        jj = np.arange(8)
        mq = np.arange(128)
        mk = (8 * m[:, None, None] + jj[None, :, None] <= 8 * mq[None, None, :] + c)
        masktab = np.ascontiguousarray(mk.astype(bf).reshape(128, 8 * 128))
        x_c = np.ascontiguousarray(x[:, c::8, :].reshape(2 * TPB, D))
        im = dict(shared)
        im.update(x_sl=x_c, cost=costab, sint=sintb, maskt=masktab)
        in_maps.append(im)
    return in_maps


_NC_CACHE = {}


def kernel(**kw):
    in_maps = _prep_inputs(kw)
    if "nc" not in _NC_CACHE:
        _NC_CACHE["nc"] = build_nc(drains=True)
    nc = _NC_CACHE["nc"]
    res = run_bass_kernel_spmd(nc, in_maps, core_ids=list(range(NCORE)))
    out = np.empty((B, S, D), np.float32)
    for c in range(NCORE):
        o = np.asarray(res.results[c]["out_sl"], np.float32).reshape(B, TPB, D)
        out[:, c::8, :] = o
    return out
